# version 16
# baseline (speedup 1.0000x reference)
import numpy as np
import concourse.bass as bass
import concourse.mybir as mybir
from concourse.bass_utils import run_bass_kernel_spmd

F32 = mybir.dt.float32
BF16 = mybir.dt.bfloat16
AF = mybir.ActivationFunctionType
ALU = mybir.AluOpType
AX = mybir.AxisListType

NCORES = 8
T = 1024
TP = T + 3
TH = 512
KD = 16
IT = 32
NS = 16
MH = 11
HSC = 22
NE = 8
EPS = 1e-5
INV_D = 1.0 / 2048.0


class Prog:

    ENGINES = ("pe", "v", "s", "g", "q")
    SKIP_SELF = {"pe"}
    NQ = 8

    def __init__(self):
        self.ops = []
        self.last_w = {}
        self.readers = {}
        self.last_op = {e: None for e in self.ENGINES}
        self.qlane = {}
        self.lane_last = [None] * self.NQ
        self.lane_cnt = [0] * self.NQ
        self.glane = {}
        self.glane_last = [None] * 2
        self.glane_cnt = [0, 0]
        self.slane = {}
        self.slane_last = [None] * 2
        self.slane_cnt = [0, 0]
        self.is_fence = set()

    def op(self, eng, fn, reads=(), writes=(), fence_deps=None, fence=False,
           gdma=False, sdma=False):
        idx = len(self.ops)
        deps = set()

        def need(i2, force=False):
            e2 = self.ops[i2][0]
            if e2 == eng and eng in self.SKIP_SELF and not force:
                return
            deps.add(i2)

        if fence_deps:
            pass
        self.ops.append((eng, None, None))
        if fence_deps:
            for i2 in fence_deps:
                if i2 is not None:
                    need(i2, force=True)
        for r in reads:
            if r in self.last_w:
                need(self.last_w[r])
        for w in writes:
            if w in self.last_w:
                need(self.last_w[w])
            for i2 in self.readers.get(w, ()):
                need(i2)
        self.ops[idx] = (eng, fn, deps)
        for r in reads:
            self.readers.setdefault(r, []).append(idx)
        for w in writes:
            self.last_w[w] = idx
            self.readers[w] = []
        self.last_op[eng] = idx
        if eng == "q" and not fence:
            lane = sum(self.lane_cnt) % self.NQ
            self.lane_cnt[lane] += 1
            self.qlane[idx] = (lane, self.lane_cnt[lane])
            self.lane_last[lane] = idx
        if gdma:
            lane = sum(self.glane_cnt) % 2
            self.glane_cnt[lane] += 1
            self.glane[idx] = (lane, self.glane_cnt[lane])
            self.glane_last[lane] = idx
        if sdma:
            lane = sum(self.slane_cnt) % 2
            self.slane_cnt[lane] += 1
            self.slane[idx] = (lane, self.slane_cnt[lane])
            self.slane_last[lane] = idx
        if fence:
            self.is_fence.add(idx)
        return idx

    def fence(self):
        snap = [self.last_op[e] for e in self.ENGINES if e != "q"]
        snap += [i for i in self.lane_last if i is not None]
        snap += [i for i in self.glane_last if i is not None]
        snap += [i for i in self.slane_last if i is not None]
        for e in self.ENGINES:
            self.op(e, lambda E: None, fence_deps=snap, fence=True)
        self.last_w.clear()
        self.readers.clear()

    def prepare(self):
        needed = set()
        for _, _, deps in self.ops:
            needed.update(deps)
        self.eng_ops = {e: [] for e in self.ENGINES}
        for idx, (eng, _, _) in enumerate(self.ops):
            self.eng_ops[eng].append(idx)
        cnt = {e: 0 for e in self.ENGINES}
        self.inc_val = [0] * len(self.ops)
        self.has_inc = [False] * len(self.ops)
        for idx, (eng, _, _) in enumerate(self.ops):
            if idx in self.glane:
                self.has_inc[idx] = True
                self.inc_val[idx] = 16 * self.glane[idx][1]
            elif idx in self.slane:
                self.has_inc[idx] = True
                self.inc_val[idx] = 16 * self.slane[idx][1]
            elif eng == "q":
                if idx in self.qlane:
                    self.has_inc[idx] = True
                    self.inc_val[idx] = 16 * self.qlane[idx][1]
                    cnt["q"] += 16
            elif idx in needed:
                cnt[eng] += 1
                self.has_inc[idx] = True
                self.inc_val[idx] = cnt[eng]
            else:
                self.inc_val[idx] = cnt[eng]
        run = {e: 0 for e in self.ENGINES}
        for idx, (eng, _, _) in enumerate(self.ops):
            if eng != "q":
                if self.has_inc[idx]:
                    if idx not in self.glane and idx not in self.slane:
                        run[eng] = self.inc_val[idx]
                else:
                    self.inc_val[idx] = run[eng]
        self._know = {}
        return cnt

    def _sem_of(self, idx, sems, qsems):
        eng = self.ops[idx][0]
        if idx in self.glane:
            lane, n = self.glane[idx]
            return self.gqsems[lane], 16 * n, ("gq", lane)
        if idx in self.slane:
            lane, n = self.slane[idx]
            return self.sqsems[lane], 16 * n, ("sq", lane)
        if eng == "q":
            lane, n = self.qlane[idx]
            return qsems[lane], 16 * n, ("q", lane)
        return sems[eng], self.inc_val[idx], (eng,)

    def emit_engine(self, e, eobj, sems, qsems):
        know = self._know.setdefault(e, {})
        for idx in self.eng_ops[e]:
            _, fn, deps = self.ops[idx]
            for i2 in sorted(deps):
                if (self.ops[i2][0] == "q" and i2 not in self.qlane):
                    continue
                sem, v, key = self._sem_of(i2, sems, qsems)
                if v > know.get(key, 0):
                    eobj.wait_ge(sem, v)
                    know[key] = v
            if e == "q" and idx in self.qlane:
                lane, n = self.qlane[idx]
                if n > 1:
                    v = 16 * (n - 1)
                    key = ("q", lane)
                    if v > know.get(key, 0):
                        eobj.wait_ge(qsems[lane], v)
                        know[key] = v
            if idx in self.glane:
                lane, n = self.glane[idx]
                if n > 1:
                    v = 16 * (n - 1)
                    key = ("gq", lane)
                    if v > know.get(key, 0):
                        eobj.wait_ge(self.gqsems[lane], v)
                        know[key] = v
            if idx in self.slane:
                lane, n = self.slane[idx]
                if n > 1:
                    v = 16 * (n - 1)
                    key = ("sq", lane)
                    if v > know.get(key, 0):
                        eobj.wait_ge(self.sqsems[lane], v)
                        know[key] = v
            inst = fn(eobj)
            if idx in self.glane:
                lane, n = self.glane[idx]
                inst.then_inc(self.gqsems[lane], 16)
            elif idx in self.slane:
                lane, n = self.slane[idx]
                inst.then_inc(self.sqsems[lane], 16)
            elif self.has_inc[idx] and e != "q":
                if inst is None:
                    eobj.sem_inc(sems[e], 1)
                else:
                    inst.then_inc(sems[e], 1)
            elif e == "q" and idx in self.qlane:
                lane, n = self.qlane[idx]
                inst.then_inc(qsems[lane], 16)


def _tile_w(w):
    K, M = w.shape
    return np.ascontiguousarray(
        w.reshape(K // 128, 128, M // 128, 128).transpose(2, 1, 0, 3))


def _ct1(v):
    return np.ascontiguousarray(np.asarray(v, np.float32).reshape(-1, 128).T)


def _ct2(v):
    v = np.asarray(v, np.float32)
    C, W = v.shape
    return np.ascontiguousarray(v.reshape(C // 128, 128, W).transpose(1, 0, 2))


def build_program(taps=()):
    nc = bass.Bass("TRN2", target_bir_lowering=False)
    inv_d = 1.0 / (KD * 128)
    tap_set = set(taps)

    dI = lambda nm, shp: nc.dram_tensor(nm, shp, F32, kind="ExternalInput").ap()
    dO = lambda nm, shp: nc.dram_tensor(nm, shp, F32, kind="ExternalOutput").ap()
    dS = lambda nm, shp: nc.dram_tensor(nm, shp, F32).ap()

    x_in = dI("x_in", [128, KD, TP])
    w_inproj = dI("w_inproj", [2 * IT, 128, KD, 128])
    convw_i = dI("convw", [128, IT, 4])
    convb_i = dI("convb", [128, IT])
    xproj_i = dI("w_xproj", [128, IT, 160])
    dtproj_i = dI("w_dtproj", [128, IT, 128])
    dtb_i = dI("dtb", [128, IT])
    alog_i = dI("alog", [128, IT, NS])
    dvec_i = dI("dvec", [128, IT])
    w_outproj = dI("w_outproj", [KD, 128, IT, 128])
    gatew_i = dI("gatew", [128, KD, NE])
    w_mg = dI("w_mg", [NE, MH, 128, KD, 128])
    w_mu = dI("w_mu", [NE, MH, 128, KD, 128])
    w_md = dI("w_md", [NE, KD, 128, MH, 128])
    w_sg = dI("w_sg", [HSC, 128, KD, 128])
    w_su = dI("w_su", [HSC, 128, KD, 128])
    w_sd = dI("w_sd", [KD, 128, HSC, 128])
    csel_i = dI("csel", [128, NCORES])
    eye_i = dI("eye", [128, 128])

    out = dO("out", [128, KD, T])
    tp = {}
    for name, shp in (("t_ln1", [128, KD, TP]), ("t_u", [IT, 128, T]),
                      ("t_ssm", [160, T]), ("t_yloc", [IT, 128, T]),
                      ("t_hin", [128, IT, NS]), ("t_y", [IT, 128, T]),
                      ("t_mo", [128, KD, T]), ("t_x2", [128, KD, T]),
                      ("t_comb", [NE, T]), ("t_ffn", [128, KD, T]),
                      ("t_dt", [IT, 128, T]), ("t_S", [IT, 128, T]),
                      ("t_ln3", [128, KD, T])):
        if name in tap_set:
            tp[name] = dO(name, shp)

    hb_u = dS("hb_u", [IT, 128, T])
    hb_gate = dS("hb_gate", [IT, 128, T])
    hb_S = dS("hb_S", [IT, 128, T])
    hb_yloc = dS("hb_yloc", [IT, 128, T])
    hb_x2 = dS("hb_x2", [KD, 128, T])
    hb_yb = nc.dram_tensor("hb_yb", [IT, 128, 2 * TH], BF16).ap()
    cc_in = dS("cc_in", [IT * 128, 2 * NS])
    cc_out = dS("cc_out", [NCORES * IT * 128, 2 * NS])

    from contextlib import ExitStack
    ctx = ExitStack()
    sb = lambda nm, shp: ctx.enter_context(nc.sbuf_tensor(nm, shp, F32))
    pst = lambda nm, shp: ctx.enter_context(nc.psum_tensor(nm, shp, F32))

    AR1 = sb("AR1", [128, max(KD * TP, NS * TH + IT * 128, IT * TH,
                              KD * TH + 8 * TH)])
    WSL = max(KD, IT // 2, MH) * 128
    AR2 = sb("AR2", [128, 3, WSL])
    AR3 = sb("AR3", [128, max(IT * 160, NS * TH, (MH + 2) * TH)])
    AR4 = sb("AR4", [128, max(2 * TP + 2 * TH, NCORES * IT * NS, KD * TH, 16 * TH)])
    AR5 = sb("AR5", [128, max(12 * TH, 2 * IT * 2 * NS)])
    c_convw = sb("c_convw", [128, IT, 4])
    c_convb = sb("c_convb", [128, IT])
    c_dtb = sb("c_dtb", [128, IT])
    c_D = sb("c_D", [128, IT])
    c_A = sb("c_A", [128, IT, NS])
    c_gatew = sb("c_gatew", [128, KD, NE])
    c_csel = sb("c_csel", [128, NCORES])
    c_eye = sb("c_eye", [128, 128])
    c_ones1 = sb("c_ones1", [1, 128])
    c_onesk = sb("c_onesk", [128, 1])
    c_zeros = sb("c_zeros", [128, TH])
    c_eps = sb("c_eps", [1, 1])
    c_neghalf = sb("c_neghalf", [1, 1])
    c_negone = sb("c_negone", [128, 1])
    s_ts = sb("s_ts", [128, T])
    s_bc = sb("s_bc", [32, T])
    s_hcar = sb("s_hcar", [128, IT, NS])
    s_scar = sb("s_scar", [128, IT])
    s_dtsum = sb("s_dtsum", [128, IT, 2])
    s_hin = sb("s_hin", [128, IT, NS])
    r_m = sb("r_m", [1, TH])
    r_v = sb("r_v", [1, TH])
    s_rs = sb("s_rs", [1, TH])
    s_rb = sb("s_rb", [1, TH])
    s_stage = sb("s_stage", [32, TH])
    s_lgf = s_stage
    s_tok = sb("s_tok", [128, TH // 128, NE])
    s_tok2 = sb("s_tok2", [128, max(TH // 128, 2), NE])
    s_trow = sb("s_trow", [128, TH // 128])
    s_comb = sb("s_comb", [32, TH])

    _a1 = nc.lookup_mloc(AR1).addr
    _a2 = nc.lookup_mloc(AR2).addr
    _a3 = nc.lookup_mloc(AR3).addr
    bln3 = nc.alloc_sbuf_tensor_at(
        "bln3", [128, 2, KD, TH], BF16, offset=_a1).ap()
    s_stage2 = nc.alloc_sbuf_tensor_at(
        "s_stage2", [32, 8, TH], F32, offset=_a1 + 4 * KD * TH).ap()
    mo2 = nc.alloc_sbuf_tensor_at(
        "mo2", [128, KD, TH], F32, offset=_a3).ap()
    bw = nc.alloc_sbuf_tensor_at(
        "bw", [128, 2, 2, WSL], BF16, offset=_a2 + WSL * 4).ap()
    bg = nc.alloc_sbuf_tensor_at(
        "bg", [128, 2, MH, TH], BF16, offset=_a3).ap()
    _a4 = nc.lookup_mloc(AR4).addr
    _a5 = nc.lookup_mloc(AR5).addr
    sxlo = nc.alloc_sbuf_tensor_at(
        "sxlo", [128, KD, TP], BF16, offset=_a1).ap()
    SXHI = []
    _scr_stride = ((TP * 2 + 31) // 32) * 32
    _n4 = (AR4.shape[1] * 4 - ((2 * TP * 4 + 31) // 32) * 32) // _scr_stride
    for k in range(KD):
        if k < _n4:
            off = _a4 + ((2 * TP * 4 + 31) // 32) * 32 + k * _scr_stride
        else:
            off = _a3 + IT * 160 * 4 + (k - _n4) * _scr_stride
        SXHI.append(nc.alloc_sbuf_tensor_at(
            f"sxhi{k}", [128, TP], BF16, offset=off).ap())
    ylo = nc.alloc_sbuf_tensor_at(
        "ylo", [128, IT, TH], BF16, offset=_a1).ap()
    ybar = nc.alloc_sbuf_tensor_at(
        "ybar", [128, IT, 2 * TH], BF16, offset=_a1).ap()
    ybsp = nc.alloc_sbuf_tensor_at(
        "ybsp", [128, 2, TH], BF16, offset=_a5 + 5 * TH * 4).ap()
    YHI = []
    for i in range(IT):
        if i < IT - 1:
            off = _a3 + i * TH * 2
        else:
            off = _a5 + 4 * TH * 4
        YHI.append(nc.alloc_sbuf_tensor_at(
            f"yhi{i}", [128, TH], BF16, offset=off).ap())
    p0 = pst("p0", [128, TH])
    p1 = pst("p1", [128, TH])
    p2 = pst("p2", [128, TH])
    p3 = pst("p3", [128, TH])
    p4 = pst("p4", [128, TH])
    PS = [p0, p1, p2, p3]
    PK = ["p0", "p1", "p2", "p3"]

    sems = {}
    for e in Prog.ENGINES:
        sems[e] = ctx.enter_context(nc.semaphore("sem_" + e))
    qsems = [ctx.enter_context(nc.semaphore(f"sem_q{i}")) for i in range(Prog.NQ)]
    gqsems = [ctx.enter_context(nc.semaphore(f"sem_gq{i}")) for i in range(2)]
    sqsems = [ctx.enter_context(nc.semaphore(f"sem_sq{i}")) for i in range(2)]
    sem_boot = ctx.enter_context(nc.semaphore("sem_boot"))

    P = Prog()
    P.gqsems = gqsems
    P.sqsems = sqsems

    sx = AR1[:, 0:KD * TP].rearrange("p (k t) -> p k t", k=KD)
    wsl = lambda i: AR2[:, i % 2, 0:KD * 128 if True else 0].rearrange("p (k c) -> p k c", c=128)
    wk = lambda i: f"w{i % 2}"
    xprojT = AR3[:, 0:IT * 160].rearrange("p (i j) -> p i j", j=160)
    B_rep = AR3[:, 0:NS * TH].rearrange("p (n t) -> p n t", t=TH)
    C_rep5 = AR3[:, 0:NS * TH].rearrange("p (n t) -> p n t", t=TH)
    g_ar = AR3[:, 0:MH * TH].rearrange("p (m t) -> p m t", t=TH)
    ga_tmp = AR3[:, MH * TH:(MH + 1) * TH]
    s_crep = AR3[:, (MH + 1) * TH:(MH + 2) * TH]
    C_rep4 = AR1[:, 0:NS * TH].rearrange("p (n t) -> p n t", t=TH)
    dtprojT = AR1[:, NS * TH:NS * TH + IT * 128].rearrange("p (i c) -> p i c", c=128)
    y_half = AR1[:, 0:IT * TH].rearrange("p (i t) -> p i t", t=TH)
    ln3 = AR1[:, 0:KD * TH].rearrange("p (k t) -> p k t", t=TH)
    hin = lambda i: AR4[:, (i % 2) * TP:(i % 2) * TP + TP]
    hends = AR4[:, 0:NCORES * IT * NS].rearrange("p (c i n) -> p c i n", i=IT, n=NS)
    mo = AR4[:, 0:KD * TH].rearrange("p (k t) -> p k t", t=TH)
    racc = AR4[:, 0:KD * TH].rearrange("p (k t) -> p k t", t=TH)
    A5 = lambda i: AR5[:, i * TH:(i + 1) * TH]
    A5k = lambda i: f"A5s{i}"
    s_all = AR5[:, 0:2 * IT * 2 * NS].rearrange("p (b i w) -> p b i w", i=IT, w=2 * NS)
    s_pq = AR5[:, 5 * TH:5 * TH + IT * 2 * NS].rearrange("p (i w) -> p i w", w=2 * NS)
    s_sb = AR5[:, 8 * TH:9 * TH]
    s_bb = AR5[:, 9 * TH:10 * TH]

    def ld(dst_ap, src_ap, wkeys, rkeys=()):
        return P.op("q", lambda q: q.dma_start(out=dst_ap, in_=src_ap),
                    reads=list(rkeys), writes=list(wkeys))

    st = ld

    ld(c_convw[:, :, :], convw_i, ["c_convw"])
    ld(c_convb[:, :], convb_i, ["c_convb"])
    ld(c_dtb[:, :], dtb_i, ["c_dtb"])
    ld(c_D[:, :], dvec_i, ["c_D"])
    ld(c_A[:, :, :], alog_i, ["c_A"])
    ld(c_gatew[:, :, :], gatew_i, ["c_gatew"])
    ld(c_csel[:, :], csel_i, ["c_csel"])
    ld(c_eye[:, :], eye_i, ["c_eye"])
    ld(xprojT[:, :, :], xproj_i, ["xprojT"])
    ld(sx[:, :, :], x_in[:, :, :], ["sx"])
    P.op("g", lambda g: g.memset(c_ones1[:, :], 1.0), writes=["c_ones1"])
    P.op("g", lambda g: g.memset(c_onesk[:, :], 1.0), writes=["c_onesk"])
    P.op("g", lambda g: g.memset(c_zeros[:, :], 0.0), writes=["c_zeros"])
    P.op("g", lambda g: g.memset(c_eps[:, :], EPS), writes=["c_eps"])
    P.op("g", lambda g: g.memset(c_neghalf[:, :], -0.5), writes=["c_neghalf"])
    P.op("g", lambda g: g.memset(c_negone[:, :], -1.0), writes=["c_negone"])
    P.op("g", lambda g: g.memset(s_comb[:, :], 0.0), writes=["s_comb"])
    P.op("s", lambda s: s.activation(c_A[:, :, :], c_A[:, :, :], AF.Exp,
                                     bias=0.0, scale=1.0),
         reads=["c_A"], writes=["c_A"])
    P.op("v", lambda v: v.tensor_scalar_mul(c_A[:, :, :], c_A[:, :, :], -1.0),
         reads=["c_A"], writes=["c_A"])

    def wsplit(src_ap, nk, slot, cast_eng=None):
        stg = AR2[:, 0, 0:nk * 128]
        whi = bw[:, slot, 0, 0:nk * 128]
        wlo = bw[:, slot, 1, 0:nk * 128]
        if cast_eng is None:
            P.op("g", lambda g, whi=whi, src_ap=src_ap: g.dma_start(
                out=whi.rearrange("p (k c) -> p k c", c=128), in_=src_ap),
                reads=[], writes=[f"bw{slot}h"], gdma=True)
            ld(stg, src_ap, ["wstg"])
        else:
            ld(stg, src_ap, ["wstg"])
            if cast_eng == "s":
                P.op("s", lambda s, whi=whi, stg=stg: s.activation(
                    whi, stg, AF.Copy, bias=0.0, scale=1.0),
                    reads=["wstg"], writes=[f"bw{slot}h"])
            else:
                P.op("v", lambda v, whi=whi, stg=stg: v.tensor_copy(whi, stg),
                     reads=["wstg"], writes=[f"bw{slot}h"])
        P.op("v", lambda v, stg=stg, whi=whi, wlo=wlo: v.tensor_sub(
            wlo, stg, whi),
            reads=["wstg", f"bw{slot}h"], writes=[f"bw{slot}l"])
        return (bw[:, slot, 0, 0:nk * 128].rearrange("p (k c) -> p k c", c=128),
                bw[:, slot, 1, 0:nk * 128].rearrange("p (k c) -> p k c", c=128),
                [f"bw{slot}h"], [f"bw{slot}l"])

    def mm3(bank, bkey, whi, wlo, wkh, wkl, rh_hi, rh_lo, rkeys, nk,
            first, last, w=TH):
        for k in range(nk):
            P.op("pe", lambda pe, k=k, bank=bank, whi=whi, rh_hi=rh_hi, w=w:
                 pe.matmul(bank[:, 0:w], whi[:, k, :], rh_hi(k),
                           start=(first and k == 0), stop=False),
                 reads=wkh + rkeys, writes=[bkey])
            P.op("pe", lambda pe, k=k, bank=bank, whi=whi, rh_lo=rh_lo, w=w:
                 pe.matmul(bank[:, 0:w], whi[:, k, :], rh_lo(k),
                           start=False, stop=False),
                 reads=wkh + rkeys, writes=[bkey])
        for k in range(nk):
            P.op("pe", lambda pe, k=k, bank=bank, wlo=wlo, rh_hi=rh_hi, w=w:
                 pe.matmul(bank[:, 0:w], wlo[:, k, :], rh_hi(k),
                           start=False, stop=(last and k == nk - 1)),
                 reads=wkl + rkeys, writes=[bkey])

    def emit_ln(src3, nk, tlen, dst3, src_reg, dst_reg, scr, scrk,
                res_load=None, dst_sink=None):
        slices = [(a, min(a + TH, tlen)) for a in range(0, tlen, TH)]
        for (a, b) in slices:
            w = b - a
            for k in range(nk):
                P.op("pe", lambda pe, k=k, a=a, b=b, w=w: pe.matmul(
                    p2[0:1, 0:w], c_onesk[:, :], src3[:, k, a:b],
                    start=(k == 0), stop=(k == nk - 1)),
                    reads=[src_reg, "c_onesk"], writes=["p2"])
            for k in range(nk):
                sc, sck = scr(k), scrk(k)
                P.op("s", lambda s, k=k, a=a, w=w, sc=sc: s.activation(
                    sc[:, 0:w], src3[:, k, a:a + w], AF.Square, bias=0.0,
                    scale=1.0),
                    reads=[src_reg], writes=[sck])
                P.op("pe", lambda pe, k=k, w=w, sc=sc: pe.matmul(
                    p3[0:1, 0:w], c_onesk[:, :], sc[:, 0:w],
                    start=(k == 0), stop=(k == nk - 1)),
                    reads=[sck, "c_onesk"], writes=["p3"])
            P.op("v", lambda v, w=w: v.tensor_scalar_mul(
                r_m[0:1, 0:w], p2[0:1, 0:w], inv_d),
                reads=["p2"], writes=["r_m"])
            P.op("v", lambda v, w=w: v.tensor_scalar_mul(
                r_v[0:1, 0:w], p3[0:1, 0:w], inv_d),
                reads=["p3"], writes=["r_v"])
            P.op("v", lambda v, w=w: v.tensor_mul(
                s_rs[0:1, 0:w], r_m[0:1, 0:w], r_m[0:1, 0:w]),
                reads=["r_m"], writes=["s_rs"])
            P.op("v", lambda v, w=w: v.tensor_sub(
                r_v[0:1, 0:w], r_v[0:1, 0:w], s_rs[0:1, 0:w]),
                reads=["r_v", "s_rs"], writes=["r_v"])
            P.op("s", lambda s, w=w: s.activation(
                r_v[0:1, 0:w], r_v[0:1, 0:w], AF.Ln,
                bias=c_eps[0:1, 0:1], scale=1.0),
                reads=["r_v", "c_eps"], writes=["r_v"])
            P.op("s", lambda s, w=w: s.activation(
                s_rs[0:1, 0:w], r_v[0:1, 0:w], AF.Exp, bias=0.0,
                scale=c_neghalf[0:1, 0:1]),
                reads=["r_v", "c_neghalf"], writes=["s_rs"])
            P.op("v", lambda v, w=w: v.tensor_mul(
                s_rb[0:1, 0:w], r_m[0:1, 0:w], s_rs[0:1, 0:w]),
                reads=["r_m", "s_rs"], writes=["s_rb"])
            P.op("v", lambda v, a=a, w=w: v.tensor_scalar_mul(
                s_rb[0:1, 0:w], s_rb[0:1, 0:w], -1.0),
                reads=["s_rb"], writes=["s_rb"])
            P.op("pe", lambda pe, a=a, w=w: pe.matmul(
                p2[:, 0:w], c_ones1[:, :], s_rs[0:1, 0:w],
                start=True, stop=True),
                reads=["s_rs", "c_ones1"], writes=["p2"])
            P.op("v", lambda v, a=a, w=w: v.tensor_copy(
                s_sb[:, 0:w], p2[:, 0:w]),
                reads=["p2"], writes=["s_sb"])
            P.op("pe", lambda pe, a=a, w=w: pe.matmul(
                p3[:, 0:w], c_ones1[:, :], s_rb[0:1, 0:w],
                start=True, stop=True),
                reads=["s_rb", "c_ones1"], writes=["p3"])
            P.op("v", lambda v, a=a, w=w: v.tensor_copy(
                s_bb[:, 0:w], p3[:, 0:w]),
                reads=["p3"], writes=["s_bb"])
            for k in range(nk):
                sc, sck = scr(k), scrk(k)
                P.op("v", lambda v, k=k, a=a, w=w, sc=sc: v.tensor_mul(
                    sc[:, 0:w], src3[:, k, a:a + w], s_sb[:, 0:w]),
                    reads=[src_reg, "s_sb"], writes=[sck])
                if res_load is None and dst_sink is None:
                    P.op("v", lambda v, k=k, a=a, w=w, sc=sc: v.tensor_add(
                        dst3[:, k, a:a + w], sc[:, 0:w], s_bb[:, 0:w]),
                        reads=[sck, "s_bb"], writes=[dst_reg])
                else:
                    P.op("v", lambda v, k=k, w=w, sc=sc: v.tensor_add(
                        sc[:, 0:w], sc[:, 0:w], s_bb[:, 0:w]),
                        reads=[sck, "s_bb"], writes=[sck])
                    if res_load is not None:
                        rap, rkey = res_load(k)
                        if dst_sink is None:
                            P.op("v", lambda v, k=k, a=a, w=w, sc=sc, rap=rap:
                                 v.tensor_add(dst3[:, k, a:a + w], sc[:, 0:w],
                                              rap),
                                 reads=[sck, rkey], writes=[dst_reg])
                        else:
                            dst_sink(k, sc, sck, rap, rkey)
                    else:
                        dst_sink(k, sc, sck, None, None)

    emit_ln(sx, KD, TP, sx, "sx", "sx",
            scr=lambda k: hin(k), scrk=lambda k: f"hin{k % 2}")
    if "t_ln1" in tp:
        st(tp["t_ln1"][:, :, :], sx[:, :, :], [], rkeys=["sx"])

    for k in range(KD):
        P.op("s", lambda s, k=k: s.activation(
            SXHI[k][:, :], sx[:, k, :], AF.Copy, bias=0.0, scale=1.0),
            reads=["sx"], writes=[f"sxh{k}"])
    for k in range(KD):
        P.op("v", lambda v, k=k: v.tensor_sub(
            sxlo[:, k, :], sx[:, k, :], SXHI[k][:, :]),
            reads=["sx", f"sxh{k}"], writes=["sxlo", "sx"])

    NT1 = [(0, 3)] + [(3 + i * TH, 3 + (i + 1) * TH) for i in range(T // TH)]
    NT2 = NT1[1:]
    pre1 = wsplit(w_inproj[0], KD, 0)
    for m in range(2 * IT):
        whi, wlo, wkh, wkl = pre1
        is_h = m < IT
        ub = AR5[:, (m % 2) * 2 * TH:((m % 2) * 2 + 2) * TH]
        uk = f"ub{m % 2}"
        gb = AR5[:, (4 + (m % 2) * 2) * TH:(4 + (m % 2) * 2 + 2) * TH]
        gk = f"gb{m % 2}"
        for ti, (a, b) in enumerate(NT1 if is_h else NT2):
            w = b - a
            bank, bkey = PS[ti % 2], PK[ti % 2]
            mm3(bank, bkey, whi, wlo, wkh, wkl,
                lambda k, a=a, b=b: SXHI[k][:, a:b],
                lambda k, a=a, b=b: sxlo[:, k, a:b],
                [f"sxh{kk}" for kk in range(KD)] + ["sxlo"], KD,
                True, True, w=w)
            if is_h:
                P.op("v", lambda v, a=a, w=w, bank=bank, m=m: v.tensor_copy(
                    hin(m)[:, a:a + w], bank[:, 0:w]),
                    reads=[bkey], writes=[f"hin{m % 2}"])
            else:
                P.op("s", lambda s, a=a, w=w, bank=bank, gb=gb: s.activation(
                    gb[:, a - 3:a - 3 + w], bank[:, 0:w], AF.Silu,
                    bias=0.0, scale=1.0),
                    reads=[bkey], writes=[gk])
        if m + 1 < 2 * IT:
            pre1 = wsplit(w_inproj[m + 1], KD, (m + 1) % 2)
        if is_h:
            hv, hk = hin(m), f"hin{m % 2}"
            P.op("v", lambda v, m=m, hv=hv, ub=ub: v.tensor_scalar(
                ub[:, 0:T], hv[:, 0:T], c_convw[:, m, 0:1], None, ALU.mult),
                reads=[hk, "c_convw"], writes=[uk])
            for j in range(1, 4):
                P.op("v", lambda v, m=m, j=j, hv=hv, ub=ub: v.scalar_tensor_tensor(
                    ub[:, 0:T], hv[:, j:j + T], c_convw[:, m, j:j + 1], ub[:, 0:T],
                    ALU.mult, ALU.add),
                    reads=[hk, "c_convw", uk], writes=[uk])
            P.op("s", lambda s, m=m, ub=ub: s.activation(
                ub[:, 0:T], ub[:, 0:T], AF.Silu, bias=c_convb[:, m:m + 1],
                scale=1.0),
                reads=[uk, "c_convb"], writes=[uk])
            for a2 in (0, TH):
                for Mg, (ja, jb) in enumerate(((0, 128), (128, 160))):
                    bank, bkey = PS[2 + Mg], PK[2 + Mg]
                    P.op("pe", lambda pe, m=m, a2=a2, ja=ja, jb=jb, bank=bank,
                         ub=ub: pe.matmul(
                         bank[0:jb - ja, 0:TH], xprojT[:, m, ja:jb],
                         ub[:, a2:a2 + TH], start=True, stop=True),
                         reads=["xprojT", uk], writes=[bkey])
                    dst, dk = (s_ts, "s_ts") if Mg == 0 else (s_bc, "s_bc")
                    np_ = 128 if Mg == 0 else 32
                    if m == 0:
                        P.op("v", lambda v, a2=a2, bank=bank, dst=dst, np_=np_:
                             v.tensor_copy(dst[:, a2:a2 + TH],
                                           bank[0:np_, 0:TH]),
                             reads=[bkey], writes=[dk])
                    else:
                        P.op("v", lambda v, a2=a2, bank=bank, dst=dst, np_=np_:
                             v.tensor_add(dst[:, a2:a2 + TH],
                                          dst[:, a2:a2 + TH],
                                          bank[0:np_, 0:TH]),
                             reads=[bkey, dk], writes=[dk])
            st(hb_u[m], ub[:, 0:T], [f"hbu{m}"], rkeys=[uk])
        else:
            st(hb_gate[m - IT], gb[:, 0:T], [f"hbg{m - IT}"], rkeys=[gk])

    if "t_u" in tp:
        st(tp["t_u"][:, :, :], hb_u[:, :, :], [], rkeys=[f"hbu{i}" for i in range(IT)])
    if "t_ssm" in tp:
        st(tp["t_ssm"][0:128, :], s_ts[:, :], [], rkeys=["s_ts"])
        st(tp["t_ssm"][128:160, :], s_bc[:, :], [], rkeys=["s_bc"])

    P.fence()

    ld(dtprojT[:, :, :], dtproj_i, ["dtprojT"])
    for th in range(2):
        ta, tb = th * TH, (th + 1) * TH
        for n in range(NS):
            P.op("v", lambda v, n=n, ta=ta, tb=tb: v.stream_shuffle(
                s_stage[:, :], s_bc[:, ta:tb], [n] + [0] * 31),
                reads=["s_bc"], writes=["s_stage"])
            P.op("pe", lambda pe, n=n: pe.matmul(
                p2[:, 0:TH], c_ones1[:, :], s_stage[0:1, 0:TH],
                start=True, stop=True),
                reads=["s_stage", "c_ones1"], writes=["p2"])
            P.op("v", lambda v, n=n: v.tensor_copy(B_rep[:, n, :], p2[:, 0:TH]),
                reads=["p2"], writes=["B_rep"])
            P.op("v", lambda v, n=n, ta=ta, tb=tb: v.stream_shuffle(
                s_stage[:, :], s_bc[:, ta:tb], [NS + n] + [0] * 31),
                reads=["s_bc"], writes=["s_stage"])
            P.op("pe", lambda pe, n=n: pe.matmul(
                p3[:, 0:TH], c_ones1[:, :], s_stage[0:1, 0:TH],
                start=True, stop=True),
                reads=["s_stage", "c_ones1"], writes=["p3"])
            P.op("s", lambda s, n=n: s.activation(
                C_rep4[:, n, :], p3[:, 0:TH], AF.Copy, bias=0.0, scale=1.0),
                reads=["p3"], writes=["C_rep4"])
        t_dtu = A5(2)
        for it in range(IT):
            t_t1, k1 = A5(3), "A5s3"
            t_t2, k2 = A5(4), "A5s4"
            t_dt = t_t1
            t_S, sSk = A5(it % 2), A5k(it % 2)
            t_ya, yak = A5(8), A5k(8)
            t_yb, ybk = A5(9), A5k(9)
            t_u, uk2 = A5(10 + it % 2), A5k(10 + it % 2)
            ld(t_u[:, :], hb_u[it][:, ta:tb], [uk2], rkeys=[f"hbu{it}"])
            bank, bkey = PS[it % 2], PK[it % 2]
            P.op("pe", lambda pe, it=it, bank=bank, ta=ta, tb=tb: pe.matmul(
                bank[:, 0:TH], dtprojT[:, it, :], s_ts[:, ta:tb],
                start=True, stop=True),
                reads=["dtprojT", "s_ts"], writes=[bkey])
            P.op("s", lambda s, it=it, bank=bank, t_t1=t_t1: s.activation(
                t_t1[:, :], bank[:, 0:TH], AF.Abs, bias=c_dtb[:, it:it + 1],
                scale=1.0),
                reads=[bkey, "c_dtb"], writes=[k1])
            P.op("s", lambda s, t_t1=t_t1, t_t2=t_t2: s.activation(
                t_t2[:, :], t_t1[:, :], AF.Exp, bias=0.0,
                scale=c_negone[:, 0:1]),
                reads=[k1, "c_negone"], writes=[k2])
            P.op("s", lambda s, t_t1=t_t1, t_t2=t_t2: s.activation(
                t_t1[:, :], t_t2[:, :], AF.Ln, bias=1.0, scale=1.0),
                reads=[k2], writes=[k1])
            P.op("s", lambda s, it=it, bank=bank, t_t2=t_t2: s.activation(
                t_t2[:, :], bank[:, 0:TH], AF.Relu, bias=c_dtb[:, it:it + 1],
                scale=1.0),
                reads=[bkey, "c_dtb"], writes=[k2])
            P.op("v", lambda v, it=it, th=th, t_dt=t_dt, t_t1=t_t1, t_t2=t_t2:
                 v.scalar_tensor_tensor(
                 t_dt[:, :], t_t1[:, :], 1.0, t_t2[:, :], ALU.mult, ALU.add,
                 accum_out=s_dtsum[:, it, th:th + 1]),
                reads=[k1, k2], writes=[k1, "s_dtsum"])
            if "t_dt" in tp:
                st(tp["t_dt"][it][:, ta:tb], t_dt[:, :], [], rkeys=[k1])
            init_s = 0.0 if th == 0 else s_scar[:, it:it + 1]
            P.op("v", lambda v, init_s=init_s, t_S=t_S, t_dt=t_dt:
                 v.tensor_tensor_scan(
                 out=t_S[:, :], data0=t_dt[:, :], data1=c_zeros[:, :],
                 initial=init_s, op0=ALU.add, op1=ALU.add),
                reads=[k1, "c_zeros"] + ([f"sc{it}"] if th else []),
                writes=[sSk])
            if th == 0:
                P.op("v", lambda v, it=it, t_S=t_S: v.tensor_copy(
                    s_scar[:, it:it + 1], t_S[:, TH - 1:TH]),
                    reads=[sSk], writes=[f"sc{it}"])
            st(hb_S[it][:, ta:tb], t_S[:, :], [f"hbS{it}"], rkeys=[sSk])
            P.op("v", lambda v, t_u=t_u, uk2=uk2, t_dt=t_dt: v.tensor_mul(
                t_dtu[:, :], t_dt[:, :], t_u[:, :]),
                reads=[k1, uk2], writes=["A5s2"])
            for ng in range(2):
                nsl = slice(8 * ng, 8 * ng + 8)
                dA8f = AR4[:, 0:8 * TH]
                dA8 = dA8f.rearrange("p (n t) -> p n t", t=TH)
                h8f = AR4[:, 8 * TH:16 * TH]
                h8 = h8f.rearrange("p (n t) -> p n t", t=TH)
                for j in range(8):
                    P.op("s", lambda s, it=it, ng=ng, j=j, dA8=dA8, t_dt=t_dt:
                         s.activation(
                         dA8[:, j, :], t_dt[:, :], AF.Exp, bias=0.0,
                         scale=c_A[:, it, 8 * ng + j:8 * ng + j + 1]),
                        reads=[k1, "c_A"], writes=["dA8"])
                P.op("v", lambda v, nsl=nsl, h8=h8: v.tensor_mul(
                    h8[:, :, :],
                    t_dtu[:, :].unsqueeze(1).broadcast_to([128, 8, TH]),
                    B_rep[:, nsl, :]),
                    reads=["A5s2", "B_rep"], writes=["h8"])
                P.op("v", lambda v, dA8=dA8: v.tensor_copy(
                    s_tok2[:, 0, :], dA8[:, :, 0]),
                    reads=["dA8"], writes=["s_tok2"])
                P.op("v", lambda v, dA8=dA8: v.memset(dA8[:, :, 0], 0.0),
                    reads=[], writes=["dA8"])
                if th == 1:
                    P.op("v", lambda v, it=it, nsl=nsl: v.tensor_mul(
                        s_tok2[:, 1, :], s_tok2[:, 0, :], s_hcar[:, it, nsl]),
                        reads=["s_tok2", f"hc{it}"], writes=["s_tok2"])
                    P.op("v", lambda v, h8=h8: v.tensor_add(
                        h8[:, :, 0], h8[:, :, 0], s_tok2[:, 1, :]),
                        reads=["s_tok2", "h8"], writes=["h8"])
                P.op("v", lambda v, dA8f=dA8f, h8f=h8f: v.tensor_tensor_scan(
                    out=h8f, data0=dA8f, data1=h8f,
                    initial=0.0, op0=ALU.mult, op1=ALU.add),
                    reads=["dA8", "h8"], writes=["h8"])
                P.op("v", lambda v, it=it, nsl=nsl, h8=h8: v.tensor_copy(
                    s_hcar[:, it, nsl], h8[:, :, TH - 1]),
                    reads=["h8"], writes=[f"hc{it}"])
                P.op("v", lambda v, nsl=nsl, h8=h8: v.tensor_mul(
                    h8[:, :, :], h8[:, :, :], C_rep4[:, nsl, :]),
                    reads=["h8", "C_rep4"], writes=["h8"])
                P.op("v", lambda v, h8=h8: v.tensor_add(
                    h8[:, 0:4, :], h8[:, 0:4, :], h8[:, 4:8, :]),
                    reads=["h8"], writes=["h8"])
                P.op("v", lambda v, h8=h8: v.tensor_add(
                    h8[:, 0:2, :], h8[:, 0:2, :], h8[:, 2:4, :]),
                    reads=["h8"], writes=["h8"])
                if ng == 0:
                    P.op("v", lambda v, t_ya=t_ya, h8=h8: v.tensor_add(
                        t_ya[:, :], h8[:, 0, :], h8[:, 1, :]),
                        reads=["h8"], writes=[yak])
                else:
                    P.op("v", lambda v, t_yb=t_yb, h8=h8: v.tensor_add(
                        t_yb[:, :], h8[:, 0, :], h8[:, 1, :]),
                        reads=["h8"], writes=[ybk])
            P.op("v", lambda v, t_ya=t_ya, t_yb=t_yb: v.tensor_add(
                t_ya[:, :], t_ya[:, :], t_yb[:, :]),
                reads=[yak, ybk], writes=[yak])
            st(hb_yloc[it][:, ta:tb], t_ya[:, :], [f"hbY{it}"], rkeys=[yak])

    if "t_yloc" in tp:
        st(tp["t_yloc"][:, :, :], hb_yloc[:, :, :], [], rkeys=[f"hbY{i}" for i in range(IT)])
    if "t_S" in tp:
        st(tp["t_S"][:, :, :], hb_S[:, :, :], [], rkeys=[f"hbS{i}" for i in range(IT)])

    P.op("v", lambda v: v.tensor_add(
        s_dtsum[:, :, 0:1], s_dtsum[:, :, 0:1], s_dtsum[:, :, 1:2]),
        reads=["s_dtsum"], writes=["s_dtsum"])
    for it in range(IT):
        P.op("s", lambda s, it=it: s.activation(
            s_pq[:, it, 0:NS], c_A[:, it, :], AF.Exp, bias=0.0,
            scale=s_dtsum[:, it, 0:1]),
            reads=["c_A", "s_dtsum"], writes=["s_pq"])
    P.op("v", lambda v: v.tensor_copy(s_pq[:, :, NS:2 * NS], s_hcar[:, :, :]),
         reads=[f"hc{i}" for i in range(IT)], writes=["s_pq"])
    st(cc_in.rearrange("(i p) w -> p i w", p=128), s_pq[:, :, :], ["cc_in"],
       rkeys=["s_pq"])
    P.op("g", lambda g: g.collective_compute(
        "AllGather", ALU.bypass, replica_groups=[list(range(NCORES))],
        ins=[cc_in.opt()], outs=[cc_out.opt()]),
        reads=["cc_in"], writes=["cc_out"])
    P.fence()
    for c in range(NCORES):
        ld(s_all[:, c % 2, :, :],
           cc_out.rearrange("(b i p) w -> b p i w", i=IT, p=128)[c],
           [f"all{c % 2}"], rkeys=["cc_out"])
        if c == 0:
            P.op("v", lambda v: v.tensor_copy(
                hends[:, 0, :, :], s_all[:, 0, :, NS:2 * NS]),
                reads=["all0"], writes=["hends"])
        else:
            P.op("v", lambda v, c=c: v.tensor_mul(
                hends[:, c, :, :], s_all[:, c % 2, :, 0:NS],
                hends[:, c - 1, :, :]),
                reads=[f"all{c % 2}", "hends"], writes=["hends"])
            P.op("v", lambda v, c=c: v.tensor_add(
                hends[:, c, :, :], hends[:, c, :, :],
                s_all[:, c % 2, :, NS:2 * NS]),
                reads=[f"all{c % 2}", "hends"], writes=["hends"])
    P.op("g", lambda g: g.memset(s_hin[:, :, :], 0.0), writes=["s_hin"])
    for c in range(NCORES):
        P.op("v", lambda v, c=c: v.scalar_tensor_tensor(
            s_hin[:, :, :], hends[:, c, :, :], c_csel[:, c:c + 1],
            s_hin[:, :, :], ALU.mult, ALU.add),
            reads=["hends", "c_csel", "s_hin"], writes=["s_hin"])
    if "t_hin" in tp:
        st(tp["t_hin"][:, :, :], s_hin[:, :, :], [], rkeys=["s_hin"])
    P.fence()
    def emit_crep5_th0():
        for n in range(NS):
            P.op("v", lambda v, n=n: v.stream_shuffle(
                s_stage[:, :], s_bc[:, 0:TH], [NS + n] + [0] * 31),
                reads=["s_bc"], writes=["s_stage"])
            P.op("pe", lambda pe, n=n: pe.matmul(
                p2[:, 0:TH], c_ones1[:, :], s_stage[0:1, 0:TH],
                start=True, stop=True),
                reads=["s_stage", "c_ones1"], writes=["p2"])
            P.op("v", lambda v, n=n: v.tensor_copy(C_rep5[:, n, :], p2[:, 0:TH]),
                reads=["p2"], writes=["C_rep5"])

    def emit_fix_th0():
        ta, tb = 0, TH
        f_e, f_t = A5(6), A5(7)
        for it in range(IT):
            f_S, kS = A5(it % 2), A5k(it % 2)
            f_u, ku = A5(2 + it % 2), A5k(2 + it % 2)
            f_g, kg = A5(4 + it % 2), A5k(4 + it % 2)
            ld(y_half[:, it, :], hb_yloc[it][:, ta:tb], [f"y{it}"],
               rkeys=[f"hbY{it}"])
            ld(f_S[:, :], hb_S[it][:, ta:tb], [kS], rkeys=[f"hbS{it}"])
            ld(f_u[:, :], hb_u[it][:, ta:tb], [ku], rkeys=[f"hbu{it}"])
            ld(f_g[:, :], hb_gate[it][:, ta:tb], [kg], rkeys=[f"hbg{it}"])
            f_a2, k_a2 = A5(10 + it % 2), A5k(10 + it % 2)
            for n in range(NS):
                e_sl, e_k = (A5(6), "A5s6") if n % 2 == 0 else (A5(8), "A5s8")
                t_sl, t_k = (A5(7), "A5s7") if n % 2 == 0 else (A5(9), "A5s9")
                P.op("s", lambda s, it=it, n=n, f_S=f_S, e_sl=e_sl: s.activation(
                    e_sl[:, :], f_S[:, :], AF.Exp, bias=0.0,
                    scale=c_A[:, it, n:n + 1]),
                    reads=[kS, "c_A"], writes=[e_k])
                P.op("v", lambda v, n=n, e_sl=e_sl, t_sl=t_sl: v.tensor_mul(
                    t_sl[:, :], e_sl[:, :], C_rep5[:, n, :]),
                    reads=[e_k, "C_rep5"], writes=[t_k])
                if n % 2 == 0:
                    P.op("v", lambda v, it=it, n=n, t_sl=t_sl:
                         v.scalar_tensor_tensor(
                         y_half[:, it, :], t_sl[:, :], s_hin[:, it, n:n + 1],
                         y_half[:, it, :], ALU.mult, ALU.add),
                         reads=[t_k, "s_hin", f"y{it}"], writes=[f"y{it}"])
                else:
                    src2 = c_zeros if n == 1 else f_a2
                    src2k = "c_zeros" if n == 1 else k_a2
                    P.op("v", lambda v, it=it, n=n, t_sl=t_sl, f_a2=f_a2,
                         src2=src2: v.scalar_tensor_tensor(
                         f_a2[:, :], t_sl[:, :], s_hin[:, it, n:n + 1],
                         src2[:, 0:TH], ALU.mult, ALU.add),
                         reads=[t_k, "s_hin", src2k], writes=[k_a2])
            P.op("v", lambda v, it=it, f_a2=f_a2: v.tensor_add(
                y_half[:, it, :], y_half[:, it, :], f_a2[:, :]),
                reads=[f"y{it}", k_a2], writes=[f"y{it}"])
            P.op("v", lambda v, it=it, f_u=f_u: v.scalar_tensor_tensor(
                y_half[:, it, :], f_u[:, :], c_D[:, it:it + 1],
                y_half[:, it, :], ALU.mult, ALU.add),
                reads=[ku, "c_D", f"y{it}"], writes=[f"y{it}"])
            P.op("v", lambda v, it=it, f_g=f_g: v.tensor_mul(
                y_half[:, it, :], y_half[:, it, :], f_g[:, :]),
                reads=[kg, f"y{it}"], writes=[f"y{it}"])
            if "t_y" in tp:
                st(tp["t_y"][it][:, ta:tb], y_half[:, it, :], [],
                   rkeys=[f"y{it}"])

    def emit_ysplit_th0():
        for it in range(IT):
            P.op("s", lambda s, it=it: s.activation(
                YHI[it][:, :], y_half[:, it, :], AF.Copy, bias=0.0, scale=1.0),
                reads=[f"y{it}"], writes=[f"yh{it}", "C_rep5"])
        for it in range(IT):
            P.op("v", lambda v, it=it: v.tensor_sub(
                ylo[:, it, :], y_half[:, it, :], YHI[it][:, :]),
                reads=[f"y{it}", f"yh{it}"],
                writes=["yloK", f"y{it // 2}", f"y{(it + 1) // 2}"])

    def emit_outproj(rh_hi, rh_lo, rkeys, mo_ap, mo_key):
        for k in range(KD):
            bank, bkey = PS[k % 2], PK[k % 2]
            for half in range(2):
                whi, wlo, wkh, wkl = wsplit(
                    w_outproj[k][:, half * (IT // 2):(half + 1) * (IT // 2), :],
                    IT // 2, half)
                mm3(bank, bkey, whi, wlo, wkh, wkl,
                    lambda kk, half=half: rh_hi(half * (IT // 2) + kk),
                    lambda kk, half=half: rh_lo(half * (IT // 2) + kk),
                    rkeys, IT // 2, half == 0, half == 1)
            P.op("v", lambda v, k=k, bank=bank, mo_ap=mo_ap: v.tensor_copy(
                mo_ap[:, k, :], bank[:, 0:TH]),
                reads=[bkey], writes=[mo_key])

    def emit_ln23(th, mo_ap, mo_key):
        ta, tb = th * TH, (th + 1) * TH

        def xres_load(k, ta=ta, tb=tb):
            sl, key = A5(2 + k % 2), A5k(2 + k % 2)
            ld(sl[:, :], x_in[:, k, 3 + ta:3 + tb], [key])
            return sl[:, :], key

        emit_ln(mo_ap, KD, TH, mo_ap, mo_key, mo_key,
                scr=lambda k: A5(k % 2), scrk=lambda k: A5k(k % 2),
                res_load=xres_load)
        for k in range(KD):
            st(hb_x2[k][:, ta:tb], mo_ap[:, k, :], [f"hbx2{k}"],
               rkeys=[mo_key])
            if "t_x2" in tp:
                st(tp["t_x2"][:, k, ta:tb], mo_ap[:, k, :], [],
                   rkeys=[mo_key])
        P.fence()

        def ln3_sink(k, sc, sck, rap, rkey):
            P.op("pe", lambda pe, k=k, sc=sc: pe.matmul(
                p2[0:NE, 0:TH], c_gatew[:, k, :], sc[:, 0:TH],
                start=(k == 0), stop=(k == KD - 1)),
                reads=["c_gatew", sck], writes=["p2"])
            P.op("s", lambda s, k=k, sc=sc: s.activation(
                bln3[:, 0, k, :], sc[:, 0:TH], AF.Copy, bias=0.0, scale=1.0),
                reads=[sck], writes=["bln3"])
            P.op("v", lambda v, k=k, sc=sc: v.tensor_sub(
                bln3[:, 1, k, :], sc[:, 0:TH], bln3[:, 0, k, :]),
                reads=[sck, "bln3"], writes=["bln3"])

        emit_ln(mo_ap, KD, TH, None, mo_key, "ln3",
                scr=lambda k: A5(k % 2), scrk=lambda k: A5k(k % 2),
                dst_sink=ln3_sink)
        P.fence()

    def emit_gating(th):
        ta, tb = th * TH, (th + 1) * TH
        P.op("v", lambda v: v.tensor_copy(s_lgf[0:NE, :], p2[0:NE, 0:TH]),
             reads=["p2"], writes=["s_stage"])
        for tc in range(TH // 128):
            P.op("pe", lambda pe, tc=tc: pe.transpose(
                p3[:, 0:NE], s_lgf[0:NE, tc * 128:(tc + 1) * 128],
                c_eye[0:NE, 0:NE]),
                reads=["s_stage", "c_eye"], writes=["p3"])
            P.op("v", lambda v, tc=tc: v.tensor_copy(
                s_tok[:, tc, :], p3[:, 0:NE]),
                reads=["p3"], writes=["s_tok"])
        for tc in range(TH // 128):
            P.op("v", lambda v, tc=tc: v.reduce_max(
                s_trow[:, tc:tc + 1], s_tok[:, tc, :], axis=AX.X, negate=True),
                reads=["s_tok"], writes=["s_trow"])
            P.op("s", lambda s, tc=tc: s.activation(
                s_tok2[:, tc, :], s_tok[:, tc, :], AF.Exp,
                bias=s_trow[:, tc:tc + 1], scale=1.0),
                reads=["s_tok", "s_trow"], writes=["s_tok2"])
            P.op("v", lambda v, tc=tc: v.reduce_sum(
                s_trow[:, tc:tc + 1], s_tok2[:, tc, :], axis=AX.X),
                reads=["s_tok2"], writes=["s_trow"])
            P.op("v", lambda v, tc=tc: v.reciprocal(
                s_trow[:, tc:tc + 1], s_trow[:, tc:tc + 1]),
                reads=["s_trow"], writes=["s_trow"])
            P.op("v", lambda v, tc=tc: v.tensor_scalar(
                s_tok[:, tc, :], s_tok2[:, tc, :], s_trow[:, tc:tc + 1], None,
                ALU.mult),
                reads=["s_tok2", "s_trow"], writes=["s_tok"])
            P.op("v", lambda v, tc=tc: v.reduce_max(
                s_trow[:, tc:tc + 1], s_tok[:, tc, :], axis=AX.X, negate=True),
                reads=["s_tok"], writes=["s_trow"])
            P.op("v", lambda v, tc=tc: v.tensor_scalar(
                s_tok2[:, tc, :], s_tok[:, tc, :], s_trow[:, tc:tc + 1], None,
                ALU.add),
                reads=["s_tok", "s_trow"], writes=["s_tok2"])
            P.op("v", lambda v, tc=tc: v.tensor_single_scalar(
                s_tok2[:, tc, :], s_tok2[:, tc, :], 0.0, ALU.is_ge),
                reads=["s_tok2"], writes=["s_tok2"])
            P.op("v", lambda v, tc=tc: v.scalar_tensor_tensor(
                s_tok2[:, tc, :], s_tok2[:, tc, :], -1e30, s_tok[:, tc, :],
                ALU.mult, ALU.add),
                reads=["s_tok2", "s_tok"], writes=["s_tok2"])
            P.op("v", lambda v, tc=tc: v.reduce_max(
                s_trow[:, tc:tc + 1], s_tok2[:, tc, :], axis=AX.X, negate=True),
                reads=["s_tok2"], writes=["s_trow"])
            P.op("v", lambda v, tc=tc: v.tensor_scalar(
                s_tok2[:, tc, :], s_tok[:, tc, :], s_trow[:, tc:tc + 1], None,
                ALU.add),
                reads=["s_tok", "s_trow"], writes=["s_tok2"])
            P.op("v", lambda v, tc=tc: v.tensor_single_scalar(
                s_tok2[:, tc, :], s_tok2[:, tc, :], 0.0, ALU.is_ge),
                reads=["s_tok2"], writes=["s_tok2"])
            P.op("v", lambda v, tc=tc: v.tensor_mul(
                s_tok[:, tc, :], s_tok[:, tc, :], s_tok2[:, tc, :]),
                reads=["s_tok", "s_tok2"], writes=["s_tok"])
            P.op("pe", lambda pe, tc=tc: pe.transpose(
                p3[0:NE, 0:128], s_tok[:, tc, :], c_eye[:, :]),
                reads=["s_tok", "c_eye"], writes=["p3"])
            P.op("v", lambda v, tc=tc: v.tensor_copy(
                s_comb[0:NE, tc * 128:(tc + 1) * 128], p3[0:NE, 0:128]),
                reads=["p3"], writes=["s_comb"])
        if "t_comb" in tp:
            st(tp["t_comb"][:, ta:tb], s_comb[0:NE, :], [], rkeys=["s_comb"])

    def gate_up(wg_pre, wu_ap, wg_next, mm, use_comb, ce=None):
        gf = A5(4)
        whi, wlo, wkh, wkl = wg_pre
        mm3(p0, "p0", whi, wlo, wkh, wkl,
            lambda k: bln3[:, 0, k, :], lambda k: bln3[:, 1, k, :],
            ["bln3"], KD, True, True)
        whi2, wlo2, wkh2, wkl2 = wsplit(wu_ap, KD, 1, cast_eng=ce)
        mm3(p1, "p1", whi2, wlo2, wkh2, wkl2,
            lambda k: bln3[:, 0, k, :], lambda k: bln3[:, 1, k, :],
            ["bln3"], KD, True, True)
        nxt = (wsplit(wg_next, KD, 0, cast_eng=ce)
               if wg_next is not None else None)
        P.op("s", lambda s: s.activation(
            ga_tmp[:, :], p0[:, 0:TH], AF.Silu, bias=0.0, scale=1.0),
            reads=["p0"], writes=["ga"])
        P.op("v", lambda v, gf=gf: v.tensor_mul(
            gf[:, :], ga_tmp[:, :], p1[:, 0:TH]),
            reads=["ga", "p1"], writes=["A5s4"])
        if use_comb:
            P.op("v", lambda v, gf=gf: v.tensor_mul(
                gf[:, :], gf[:, :], s_crep[:, :]),
                reads=["A5s4", "s_crep"], writes=["A5s4"])
        P.op("s", lambda s, mm=mm, gf=gf: s.activation(
            bg[:, 0, mm, :], gf[:, :], AF.Copy, bias=0.0, scale=1.0),
            reads=["A5s4"], writes=[f"bg{mm}h"])
        P.op("v", lambda v, mm=mm, gf=gf: v.tensor_sub(
            bg[:, 1, mm, :], gf[:, :], bg[:, 0, mm, :]),
            reads=["A5s4", f"bg{mm}h"], writes=[f"bg{mm}l"])
        return nxt

    def down(wd_fn, first, ce=None, nxt_gw=None):
        pre = wsplit(wd_fn(0), MH, 0, cast_eng=ce)
        nxt = None
        for k in range(KD):
            bank, bkey = PS[2 + k % 2], PK[2 + k % 2]
            whi, wlo, wkh, wkl = pre
            gkeys = [f"bg{kk}{hl}" for kk in range(MH) for hl in "hl"]
            mm3(bank, bkey, whi, wlo, wkh, wkl,
                lambda kk: bg[:, 0, kk, :], lambda kk: bg[:, 1, kk, :],
                gkeys, MH, True, True)
            if k + 1 < KD:
                pre = wsplit(wd_fn(k + 1), MH, (k + 1) % 2, cast_eng=ce)
            if k == KD - 1 and nxt_gw is not None:
                nxt = wsplit(nxt_gw, KD, 0, cast_eng=ce)
            if first:
                P.op("v", lambda v, k=k, bank=bank: v.tensor_copy(
                    racc[:, k, :], bank[:, 0:TH]),
                    reads=[bkey], writes=[f"racc{k}"])
            else:
                P.op("v", lambda v, k=k, bank=bank: v.tensor_add(
                    racc[:, k, :], racc[:, k, :], bank[:, 0:TH]),
                    reads=[bkey, f"racc{k}"], writes=[f"racc{k}"])
        return nxt

    def emit_moe(th, tick, ce_gu=None, ce_dn=None):
        ta, tb = th * TH, (th + 1) * TH
        pre = wsplit(w_sg[0], KD, 0, cast_eng=ce_gu)
        for hh in range(2):
            for mm in range(MH):
                m = hh * MH + mm
                nxt = w_sg[m + 1] if mm + 1 < MH else None
                pre = gate_up(pre, w_su[m], nxt, mm, use_comb=False, ce=ce_gu)
                tick(1)
            nseg = w_sg[MH] if hh == 0 else w_mg[0][0]
            pre = down(lambda k, hh=hh: w_sd[k][:, hh * MH:(hh + 1) * MH, :],
                       first=(hh == 0), ce=ce_dn, nxt_gw=nseg)
            tick(4)
        def eshuf(e):
            P.op("v", lambda v, e=e: v.stream_shuffle(
                s_stage2[:, 4 + e % 4, :], s_comb[:, :], [e] + [0] * 31),
                reads=["s_comb"], writes=[f"st2e{e % 4}"])

        for e in range(4):
            eshuf(e)
        for e in range(NE):
            P.op("pe", lambda pe, e=e: pe.matmul(
                p4[:, 0:TH], c_ones1[:, :], s_stage2[0:1, 4 + e % 4, :],
                start=True, stop=True),
                reads=[f"st2e{e % 4}", "c_ones1"], writes=["p4"])
            if e + 4 < NE:
                eshuf(e + 4)
            P.op("v", lambda v: v.tensor_copy(s_crep[:, :], p4[:, 0:TH]),
                reads=["p4"], writes=["s_crep"])
            for mm in range(MH):
                nxt = w_mg[e][mm + 1] if mm + 1 < MH else None
                pre = gate_up(pre, w_mu[e][mm], nxt, mm, use_comb=True,
                              ce=ce_gu)
                tick(1)
            pre = down(lambda k, e=e: w_md[e][k], first=False, ce=ce_dn,
                       nxt_gw=w_mg[e + 1][0] if e + 1 < NE else None)
            tick(4)
        if "t_ffn" in tp:
            for k in range(KD):
                st(tp["t_ffn"][:, k, ta:tb], racc[:, k, :], [],
                   rkeys=[f"racc{k}"])

    def emit_ln4(th):
        ta, tb = th * TH, (th + 1) * TH

        def x2_load(k, ta=ta, tb=tb):
            sl, key = A5(2 + k % 2), A5k(2 + k % 2)
            ld(sl[:, :], hb_x2[k][:, ta:tb], [key], rkeys=[f"hbx2{k}"])
            return sl[:, :], key

        def out_sink(k, sc, sck, rap, rkey, ta=ta, tb=tb):
            ostg, okey = A5(6 + k % 2), A5k(6 + k % 2)
            P.op("v", lambda v, sc=sc, rap=rap, ostg=ostg: v.tensor_add(
                ostg[:, :], sc[:, 0:TH], rap),
                reads=[sck, rkey], writes=[okey])
            st(out[:, k, ta:tb], ostg[:, :], ["out_hbm"], rkeys=[okey])

        emit_ln(racc, KD, TH, None, "racc", "unused",
                scr=lambda k: A5(k % 2), scrk=lambda k: A5k(k % 2),
                res_load=x2_load, dst_sink=out_sink)

    FXC = [A5(8), A5(9), A5(10), A5(11)]
    PRT = [A5(6), A5(7)]

    def s_pref(ng, it):
        sl, key = A5(it % 2), f"fxs{it % 2}"
        P.op("s", lambda s, it=it, sl=sl: s.dma_start(
            out=sl[:, :], in_=hb_S[it][:, TH:2 * TH]),
            reads=[f"hbS{it}"], writes=[key], sdma=True)

    def emit_shuf(ng):
        for j in range(4):
            n = 4 * ng + j
            P.op("v", lambda v, n=n, j=j: v.stream_shuffle(
                s_stage2[:, j, :], s_bc[:, TH:2 * TH], [NS + n] + [0] * 31),
                reads=["s_bc"], writes=[f"st2{j}"])

    def mk_crep1(ng):
        def blk():
            if ng > 0:
                cce_add(IT - 1)
            for j in range(4):
                P.op("pe", lambda pe, j=j: pe.matmul(
                    p4[:, 0:TH], c_ones1[:, :], s_stage2[0:1, j, :],
                    start=True, stop=True),
                    reads=[f"st2{j}", "c_ones1"], writes=["p4"])
                P.op("v", lambda v, j=j: v.tensor_copy(
                    FXC[j][:, :], p4[:, 0:TH]),
                    reads=["p4"], writes=[f"fxc{j}"])
            s_pref(ng, 0)
        return blk

    def cce_add(it):
        P.op("g", lambda g, it=it: g.dma_start(
            out=hb_yloc[it][:, TH:2 * TH], in_=PRT[it % 2][:, :],
            accum_op=ALU.add),
            reads=[f"fxp{it % 2}"], writes=[f"hbY{it}"], gdma=True)

    def mk_fix1(ng, it):
        def blk():
            if it > 0:
                cce_add(it - 1)
            if it + 1 < IT:
                s_pref(ng, it + 1)
            if it == IT // 2 and ng + 1 < 4:
                emit_shuf(ng + 1)
            kss = f"fxs{it % 2}"
            sS = A5(it % 2)
            kp = f"fxp{it % 2}"
            for j in range(4):
                n = 4 * ng + j
                et, ket = (A5(5), "fxe0") if j % 2 == 0 else (A5(2), "fxe1")
                P.op("s", lambda s, n=n, sS=sS, et=et: s.activation(
                    et[:, :], sS[:, :], AF.Exp, bias=0.0,
                    scale=c_A[:, it, n:n + 1]),
                    reads=[kss, "c_A"], writes=[ket])
                P.op("v", lambda v, j=j, et=et: v.tensor_mul(
                    et[:, :], et[:, :], FXC[j][:, :]),
                    reads=[ket, f"fxc{j}"], writes=[ket])
                src2 = c_zeros if j == 0 else PRT[it % 2]
                src2k = "c_zeros" if j == 0 else kp
                P.op("v", lambda v, n=n, et=et, src2=src2, it=it:
                     v.scalar_tensor_tensor(
                     PRT[it % 2][:, :], et[:, :], s_hin[:, it, n:n + 1],
                     src2[:, 0:TH], ALU.mult, ALU.add),
                     reads=[ket, "s_hin", src2k], writes=[kp])
        return blk

    def mk_flush():
        def blk():
            cce_add(IT - 1)
        return blk

    def mk_yfin1(it):
        def blk():
            yv, kyv = A5(it % 2), f"fxy{it % 2}"
            uv, kuv = A5(2 + it % 2), f"fxu{it % 2}"
            gv, kgv = A5(6 + it % 2), f"fxg{it % 2}"
            P.op("s", lambda s, it=it, yv=yv: s.dma_start(
                out=yv[:, :], in_=hb_yloc[it][:, TH:2 * TH]),
                reads=[f"hbY{it}"], writes=[kyv, f"fxs{it % 2}"], sdma=True)
            P.op("s", lambda s, it=it, uv=uv: s.dma_start(
                out=uv[:, :], in_=hb_u[it][:, TH:2 * TH]),
                reads=[f"hbu{it}"],
                writes=[kuv, "fxe1" if it % 2 == 0 else "A5s3f"], sdma=True)
            P.op("s", lambda s, it=it, gv=gv: s.dma_start(
                out=gv[:, :], in_=hb_gate[it][:, TH:2 * TH]),
                reads=[f"hbg{it}"], writes=[kgv, f"fxp{it % 2}"], sdma=True)
            P.op("v", lambda v, it=it, uv=uv, yv=yv: v.scalar_tensor_tensor(
                yv[:, :], uv[:, :], c_D[:, it:it + 1], yv[:, :],
                ALU.mult, ALU.add),
                reads=[kuv, "c_D", kyv], writes=[kyv])
            P.op("v", lambda v, yv=yv, gv=gv: v.tensor_mul(
                yv[:, :], yv[:, :], gv[:, :]),
                reads=[kgv, kyv], writes=[kyv])
            if "t_y" in tp:
                P.op("s", lambda s, it=it, yv=yv: s.dma_start(
                    out=tp["t_y"][it][:, TH:2 * TH], in_=yv[:, :]),
                    reads=[kyv], writes=[], sdma=True)
            P.op("s", lambda s, yv=yv: s.activation(
                ybsp[:, 0, :], yv[:, :], AF.Copy, bias=0.0, scale=1.0),
                reads=[kyv], writes=["ybsph", "fxe0"])
            P.op("v", lambda v, yv=yv: v.tensor_sub(
                ybsp[:, 1, :], yv[:, :], ybsp[:, 0, :]),
                reads=[kyv, "ybsph"], writes=["ybspl"])
            P.op("s", lambda s, it=it: s.dma_start(
                out=hb_yb[it], in_=ybsp[:, :, :].rearrange("p a w -> p (a w)")),
                reads=["ybsph", "ybspl"], writes=[f"hbyb{it}"], sdma=True)
        return blk

    fx_blocks = []
    for ng in range(4):
        if ng > 0:
            fx_blocks.append(mk_crep1(ng))
        for it in range(IT):
            fx_blocks.append(mk_fix1(ng, it))
    fx_blocks.append(mk_flush())
    for it in range(IT):
        fx_blocks.append(mk_yfin1(it))

    def tick(k):
        for _ in range(k):
            if fx_blocks:
                fx_blocks.pop(0)()

    emit_crep5_th0()
    emit_fix_th0()
    emit_ysplit_th0()
    ykeys0 = [f"yh{i}" for i in range(IT)] + ["yloK"]
    emit_outproj(lambda it: YHI[it][:, :], lambda it: ylo[:, it, :], ykeys0,
                 mo, "mo")
    if "t_mo" in tp:
        for k in range(KD):
            st(tp["t_mo"][:, k, 0:TH], mo[:, k, :], [], rkeys=["mo"])
    P.fence()
    emit_ln23(0, mo, "mo")
    emit_gating(0)
    emit_shuf(0)
    mk_crep1(0)()
    emit_moe(0, tick, ce_gu="s", ce_dn="v")
    while fx_blocks:
        fx_blocks.pop(0)()
    P.fence()
    ystep = max(1, IT // 4)
    for qq in range(IT // ystep):
        ld(ybar[:, ystep * qq:ystep * (qq + 1), :],
           hb_yb[ystep * qq:ystep * (qq + 1)].rearrange("i p w -> p i w"),
           ["ybar"], rkeys=[f"hbyb{i}" for i in range(IT)])
    emit_outproj(lambda it: ybar[:, it, 0:TH], lambda it: ybar[:, it, TH:2 * TH],
                 ["ybar"], mo2, "mo2")
    if "t_mo" in tp:
        for k in range(KD):
            st(tp["t_mo"][:, k, TH:2 * TH], mo2[:, k, :], [], rkeys=["mo2"])
    emit_ln4(0)
    P.fence()

    emit_ln23(1, mo2, "mo2")
    emit_gating(1)
    emit_moe(1, lambda k: None)
    P.fence()
    emit_ln4(1)
    P.fence()

    cnt = P.prepare()
    with nc.Block() as block:
        @block.gpsimd
        def _(g):
            for e in Prog.ENGINES:
                g.sem_clear(sems[e])
            for qs in qsems:
                g.sem_clear(qs)
            for qs in gqsems:
                g.sem_clear(qs)
            for qs in sqsems:
                g.sem_clear(qs)
            g.sem_clear(sem_boot)
            g.sem_inc(sem_boot, 1)
            P.emit_engine("g", g, sems, qsems)

        @block.tensor
        def _(pe):
            pe.wait_ge(sem_boot, 1)
            P.emit_engine("pe", pe, sems, qsems)

        @block.vector
        def _(v):
            v.wait_ge(sem_boot, 1)
            P.emit_engine("v", v, sems, qsems)

        @block.scalar
        def _(s):
            s.wait_ge(sem_boot, 1)
            P.emit_engine("s", s, sems, qsems)

        @block.sync
        def _(q):
            q.wait_ge(sem_boot, 1)
            P.emit_engine("q", q, sems, qsems)

    ctx.close()
    return nc


def _prep_inputs(inputs):
    f = lambda k: np.ascontiguousarray(np.asarray(inputs[k], np.float32))
    x = f("L0_attn_x")[0]
    common = {
        "w_inproj": _tile_w(np.ascontiguousarray(f("L0_attn_in_proj_w").T)),
        "convw": _ct2(f("L0_attn_conv_w")[:, 0, :]),
        "convb": _ct1(f("L0_attn_conv_b")),
        "w_xproj": _ct2(np.ascontiguousarray(f("L0_attn_x_proj_w").T)),
        "w_dtproj": np.ascontiguousarray(
            f("L0_attn_dt_proj_w").reshape(IT, 128, 128).transpose(2, 0, 1)),
        "dtb": _ct1(f("L0_attn_dt_proj_b")),
        "alog": _ct2(f("L0_attn_A_log")),
        "dvec": _ct1(f("L0_attn_D")),
        "w_outproj": _tile_w(np.ascontiguousarray(f("L0_attn_out_proj_w").T)),
        "gatew": _ct2(f("L0_ffn_gate_w")),
        "w_mg": np.ascontiguousarray(np.stack(
            [_tile_w(f("L0_ffn_w_gate")[e]) for e in range(NE)])),
        "w_mu": np.ascontiguousarray(np.stack(
            [_tile_w(f("L0_ffn_w_up")[e]) for e in range(NE)])),
        "w_md": np.ascontiguousarray(np.stack(
            [_tile_w(f("L0_ffn_w_down")[e]) for e in range(NE)])),
        "w_sg": _tile_w(f("L0_ffn_sw_gate")),
        "w_su": _tile_w(f("L0_ffn_sw_up")),
        "w_sd": _tile_w(f("L0_ffn_sw_down")),
        "eye": np.eye(128, dtype=np.float32),
    }
    dm = KD * 128
    xpad = np.concatenate([np.zeros((3, dm), np.float32), x], axis=0)
    in_maps = []
    for c in range(NCORES):
        xs = xpad[T * c:T * c + TP]
        xs = np.ascontiguousarray(xs.reshape(TP, KD, 128).transpose(2, 1, 0))
        csel = np.zeros((128, NCORES), np.float32)
        if c > 0:
            csel[:, c - 1] = 1.0
        m = dict(common)
        m["x_in"] = xs
        m["csel"] = csel
        in_maps.append(m)
    return in_maps


def kernel(**inputs):
    in_maps = _prep_inputs(inputs)
    nc = build_program()
    res = run_bass_kernel_spmd(nc, in_maps, core_ids=list(range(NCORES)))
    full = np.empty((1, NCORES * T, KD * 128), np.float32)
    for c in range(NCORES):
        o = np.asarray(res.results[c]["out"])
        full[0, T * c:T * (c + 1), :] = o.transpose(2, 1, 0).reshape(T, KD * 128)
    return full


# version 17
# speedup vs baseline: 1.0144x; 1.0144x over previous
import numpy as np
import concourse.bass as bass
import concourse.mybir as mybir
from concourse.bass_utils import run_bass_kernel_spmd

F32 = mybir.dt.float32
BF16 = mybir.dt.bfloat16
AF = mybir.ActivationFunctionType
ALU = mybir.AluOpType
AX = mybir.AxisListType

NCORES = 8
T = 1024
TP = T + 3
TH = 512
KD = 16
IT = 32
NS = 16
MH = 11
HSC = 22
NE = 8
EPS = 1e-5
INV_D = 1.0 / 2048.0


class Prog:

    ENGINES = ("pe", "v", "s", "g", "q")
    SKIP_SELF = {"pe"}
    NQ = 8

    def __init__(self):
        self.ops = []
        self.last_w = {}
        self.readers = {}
        self.last_op = {e: None for e in self.ENGINES}
        self.qlane = {}
        self.lane_last = [None] * self.NQ
        self.lane_cnt = [0] * self.NQ
        self.glane = {}
        self.glane_last = [None] * 2
        self.glane_cnt = [0, 0]
        self.slane = {}
        self.slane_last = [None] * 2
        self.slane_cnt = [0, 0]
        self.is_fence = set()

    def op(self, eng, fn, reads=(), writes=(), fence_deps=None, fence=False,
           gdma=False, sdma=False):
        idx = len(self.ops)
        deps = set()

        def need(i2, force=False):
            e2 = self.ops[i2][0]
            if e2 == eng and eng in self.SKIP_SELF and not force:
                return
            deps.add(i2)

        if fence_deps:
            pass
        self.ops.append((eng, None, None))
        if fence_deps:
            for i2 in fence_deps:
                if i2 is not None:
                    need(i2, force=True)
        for r in reads:
            if r in self.last_w:
                need(self.last_w[r])
        for w in writes:
            if w in self.last_w:
                need(self.last_w[w])
            for i2 in self.readers.get(w, ()):
                need(i2)
        self.ops[idx] = (eng, fn, deps)
        for r in reads:
            self.readers.setdefault(r, []).append(idx)
        for w in writes:
            self.last_w[w] = idx
            self.readers[w] = []
        self.last_op[eng] = idx
        if eng == "q" and not fence:
            lane = sum(self.lane_cnt) % self.NQ
            self.lane_cnt[lane] += 1
            self.qlane[idx] = (lane, self.lane_cnt[lane])
            self.lane_last[lane] = idx
        if gdma:
            lane = sum(self.glane_cnt) % 2
            self.glane_cnt[lane] += 1
            self.glane[idx] = (lane, self.glane_cnt[lane])
            self.glane_last[lane] = idx
        if sdma:
            lane = sum(self.slane_cnt) % 2
            self.slane_cnt[lane] += 1
            self.slane[idx] = (lane, self.slane_cnt[lane])
            self.slane_last[lane] = idx
        if fence:
            self.is_fence.add(idx)
        return idx

    def fence(self):
        snap = [self.last_op[e] for e in self.ENGINES if e != "q"]
        snap += [i for i in self.lane_last if i is not None]
        snap += [i for i in self.glane_last if i is not None]
        snap += [i for i in self.slane_last if i is not None]
        for e in self.ENGINES:
            self.op(e, lambda E: None, fence_deps=snap, fence=True)
        self.last_w.clear()
        self.readers.clear()

    def prepare(self):
        needed = set()
        for _, _, deps in self.ops:
            needed.update(deps)
        self.eng_ops = {e: [] for e in self.ENGINES}
        for idx, (eng, _, _) in enumerate(self.ops):
            self.eng_ops[eng].append(idx)
        cnt = {e: 0 for e in self.ENGINES}
        self.inc_val = [0] * len(self.ops)
        self.has_inc = [False] * len(self.ops)
        for idx, (eng, _, _) in enumerate(self.ops):
            if idx in self.glane:
                self.has_inc[idx] = True
                self.inc_val[idx] = 16 * self.glane[idx][1]
            elif idx in self.slane:
                self.has_inc[idx] = True
                self.inc_val[idx] = 16 * self.slane[idx][1]
            elif eng == "q":
                if idx in self.qlane:
                    self.has_inc[idx] = True
                    self.inc_val[idx] = 16 * self.qlane[idx][1]
                    cnt["q"] += 16
            elif idx in needed:
                cnt[eng] += 1
                self.has_inc[idx] = True
                self.inc_val[idx] = cnt[eng]
            else:
                self.inc_val[idx] = cnt[eng]
        run = {e: 0 for e in self.ENGINES}
        for idx, (eng, _, _) in enumerate(self.ops):
            if eng != "q":
                if self.has_inc[idx]:
                    if idx not in self.glane and idx not in self.slane:
                        run[eng] = self.inc_val[idx]
                else:
                    self.inc_val[idx] = run[eng]
        self._know = {}
        return cnt

    def _sem_of(self, idx, sems, qsems):
        eng = self.ops[idx][0]
        if idx in self.glane:
            lane, n = self.glane[idx]
            return self.gqsems[lane], 16 * n, ("gq", lane)
        if idx in self.slane:
            lane, n = self.slane[idx]
            return self.sqsems[lane], 16 * n, ("sq", lane)
        if eng == "q":
            lane, n = self.qlane[idx]
            return qsems[lane], 16 * n, ("q", lane)
        return sems[eng], self.inc_val[idx], (eng,)

    def emit_engine(self, e, eobj, sems, qsems):
        know = self._know.setdefault(e, {})
        for idx in self.eng_ops[e]:
            _, fn, deps = self.ops[idx]
            for i2 in sorted(deps):
                if (self.ops[i2][0] == "q" and i2 not in self.qlane):
                    continue
                sem, v, key = self._sem_of(i2, sems, qsems)
                if v > know.get(key, 0):
                    eobj.wait_ge(sem, v)
                    know[key] = v
            if e == "q" and idx in self.qlane:
                lane, n = self.qlane[idx]
                if n > 1:
                    v = 16 * (n - 1)
                    key = ("q", lane)
                    if v > know.get(key, 0):
                        eobj.wait_ge(qsems[lane], v)
                        know[key] = v
            if idx in self.glane:
                lane, n = self.glane[idx]
                if n > 1:
                    v = 16 * (n - 1)
                    key = ("gq", lane)
                    if v > know.get(key, 0):
                        eobj.wait_ge(self.gqsems[lane], v)
                        know[key] = v
            if idx in self.slane:
                lane, n = self.slane[idx]
                if n > 1:
                    v = 16 * (n - 1)
                    key = ("sq", lane)
                    if v > know.get(key, 0):
                        eobj.wait_ge(self.sqsems[lane], v)
                        know[key] = v
            inst = fn(eobj)
            if idx in self.glane:
                lane, n = self.glane[idx]
                inst.then_inc(self.gqsems[lane], 16)
            elif idx in self.slane:
                lane, n = self.slane[idx]
                inst.then_inc(self.sqsems[lane], 16)
            elif self.has_inc[idx] and e != "q":
                if inst is None:
                    eobj.sem_inc(sems[e], 1)
                else:
                    inst.then_inc(sems[e], 1)
            elif e == "q" and idx in self.qlane:
                lane, n = self.qlane[idx]
                inst.then_inc(qsems[lane], 16)


def _tile_w(w):
    K, M = w.shape
    return np.ascontiguousarray(
        w.reshape(K // 128, 128, M // 128, 128).transpose(2, 1, 0, 3))


def _ct1(v):
    return np.ascontiguousarray(np.asarray(v, np.float32).reshape(-1, 128).T)


def _ct2(v):
    v = np.asarray(v, np.float32)
    C, W = v.shape
    return np.ascontiguousarray(v.reshape(C // 128, 128, W).transpose(1, 0, 2))


def build_program(taps=()):
    nc = bass.Bass("TRN2", target_bir_lowering=False)
    inv_d = 1.0 / (KD * 128)
    tap_set = set(taps)

    dI = lambda nm, shp: nc.dram_tensor(nm, shp, F32, kind="ExternalInput").ap()
    dO = lambda nm, shp: nc.dram_tensor(nm, shp, F32, kind="ExternalOutput").ap()
    dS = lambda nm, shp: nc.dram_tensor(nm, shp, F32).ap()

    x_in = dI("x_in", [128, KD, TP])
    w_inproj = dI("w_inproj", [2 * IT, 128, KD, 128])
    convw_i = dI("convw", [128, IT, 4])
    convb_i = dI("convb", [128, IT])
    xproj_i = dI("w_xproj", [128, IT, 160])
    dtproj_i = dI("w_dtproj", [128, IT, 128])
    dtb_i = dI("dtb", [128, IT])
    alog_i = dI("alog", [128, IT, NS])
    dvec_i = dI("dvec", [128, IT])
    w_outproj = dI("w_outproj", [KD, 128, IT, 128])
    gatew_i = dI("gatew", [128, KD, NE])
    w_mg = dI("w_mg", [NE, MH, 128, KD, 128])
    w_mu = dI("w_mu", [NE, MH, 128, KD, 128])
    w_md = dI("w_md", [NE, KD, 128, MH, 128])
    w_sg = dI("w_sg", [HSC, 128, KD, 128])
    w_su = dI("w_su", [HSC, 128, KD, 128])
    w_sd = dI("w_sd", [KD, 128, HSC, 128])
    csel_i = dI("csel", [128, NCORES])
    eye_i = dI("eye", [128, 128])

    out = dO("out", [128, KD, T])
    tp = {}
    for name, shp in (("t_ln1", [128, KD, TP]), ("t_u", [IT, 128, T]),
                      ("t_ssm", [160, T]), ("t_yloc", [IT, 128, T]),
                      ("t_hin", [128, IT, NS]), ("t_y", [IT, 128, T]),
                      ("t_mo", [128, KD, T]), ("t_x2", [128, KD, T]),
                      ("t_comb", [NE, T]), ("t_ffn", [128, KD, T]),
                      ("t_dt", [IT, 128, T]), ("t_S", [IT, 128, T]),
                      ("t_ln3", [128, KD, T])):
        if name in tap_set:
            tp[name] = dO(name, shp)

    hb_u = dS("hb_u", [IT, 128, T])
    hb_gate = dS("hb_gate", [IT, 128, T])
    hb_S = dS("hb_S", [IT, 128, T])
    hb_yloc = dS("hb_yloc", [IT, 128, T])
    hb_x2 = dS("hb_x2", [KD, 128, T])
    hb_yb = nc.dram_tensor("hb_yb", [IT, 128, 2 * TH], BF16).ap()
    cc_in = dS("cc_in", [IT * 128, 2 * NS])
    cc_out = dS("cc_out", [NCORES * IT * 128, 2 * NS])

    from contextlib import ExitStack
    ctx = ExitStack()
    sb = lambda nm, shp: ctx.enter_context(nc.sbuf_tensor(nm, shp, F32))
    pst = lambda nm, shp: ctx.enter_context(nc.psum_tensor(nm, shp, F32))

    AR1 = sb("AR1", [128, max(KD * TP, NS * TH + IT * 128, IT * TH,
                              KD * TH + 8 * TH)])
    WSL = max(KD, IT // 2, MH) * 128
    AR2 = sb("AR2", [128, 3, WSL])
    AR3 = sb("AR3", [128, max(IT * 160, NS * TH, (MH + 2) * TH)])
    AR4 = sb("AR4", [128, max(2 * TP + 2 * TH, NCORES * IT * NS, KD * TH, 16 * TH)])
    AR5 = sb("AR5", [128, max(12 * TH, 2 * IT * 2 * NS)])
    c_convw = sb("c_convw", [128, IT, 4])
    c_convb = sb("c_convb", [128, IT])
    c_dtb = sb("c_dtb", [128, IT])
    c_D = sb("c_D", [128, IT])
    c_A = sb("c_A", [128, IT, NS])
    c_gatew = sb("c_gatew", [128, KD, NE])
    c_csel = sb("c_csel", [128, NCORES])
    c_eye = sb("c_eye", [128, 128])
    c_ones1 = sb("c_ones1", [1, 128])
    c_onesk = sb("c_onesk", [128, 1])
    c_zeros = sb("c_zeros", [128, TH])
    c_eps = sb("c_eps", [1, 1])
    c_neghalf = sb("c_neghalf", [1, 1])
    c_negone = sb("c_negone", [128, 1])
    s_ts = sb("s_ts", [128, T])
    s_bc = sb("s_bc", [32, T])
    s_hcar = sb("s_hcar", [128, IT, NS])
    s_scar = sb("s_scar", [128, IT])
    s_dtsum = sb("s_dtsum", [128, IT, 2])
    s_hin = sb("s_hin", [128, IT, NS])
    r_m = sb("r_m", [1, TH])
    r_v = sb("r_v", [1, TH])
    s_rs = sb("s_rs", [1, TH])
    s_rb = sb("s_rb", [1, TH])
    s_stage = sb("s_stage", [32, TH])
    s_lgf = s_stage
    s_tok = sb("s_tok", [128, TH // 128, NE])
    s_tok2 = sb("s_tok2", [128, max(TH // 128, 2), NE])
    s_trow = sb("s_trow", [128, TH // 128])
    s_comb = sb("s_comb", [32, TH])

    _a1 = nc.lookup_mloc(AR1).addr
    _a2 = nc.lookup_mloc(AR2).addr
    _a3 = nc.lookup_mloc(AR3).addr
    bln3 = nc.alloc_sbuf_tensor_at(
        "bln3", [128, 2, KD, TH], BF16, offset=_a1).ap()
    s_stage2 = nc.alloc_sbuf_tensor_at(
        "s_stage2", [32, 8, TH], F32, offset=_a1 + 4 * KD * TH).ap()
    mo2 = nc.alloc_sbuf_tensor_at(
        "mo2", [128, KD, TH], F32, offset=_a3).ap()
    bw = nc.alloc_sbuf_tensor_at(
        "bw", [128, 2, 2, WSL], BF16, offset=_a2 + WSL * 4).ap()
    bg = nc.alloc_sbuf_tensor_at(
        "bg", [128, 2, MH, TH], BF16, offset=_a3).ap()
    _a4 = nc.lookup_mloc(AR4).addr
    _a5 = nc.lookup_mloc(AR5).addr
    sxlo = nc.alloc_sbuf_tensor_at(
        "sxlo", [128, KD, TP], BF16, offset=_a1).ap()
    SXHI = []
    _scr_stride = ((TP * 2 + 31) // 32) * 32
    _n4 = (AR4.shape[1] * 4 - ((2 * TP * 4 + 31) // 32) * 32) // _scr_stride
    for k in range(KD):
        if k < _n4:
            off = _a4 + ((2 * TP * 4 + 31) // 32) * 32 + k * _scr_stride
        else:
            off = _a3 + IT * 160 * 4 + (k - _n4) * _scr_stride
        SXHI.append(nc.alloc_sbuf_tensor_at(
            f"sxhi{k}", [128, TP], BF16, offset=off).ap())
    ylo = nc.alloc_sbuf_tensor_at(
        "ylo", [128, IT, TH], BF16, offset=_a1).ap()
    ybar = nc.alloc_sbuf_tensor_at(
        "ybar", [128, IT, 2 * TH], BF16, offset=_a1).ap()
    ybsp = nc.alloc_sbuf_tensor_at(
        "ybsp", [128, 2, TH], BF16, offset=_a5 + 5 * TH * 4).ap()
    YHI = []
    for i in range(IT):
        if i < IT - 1:
            off = _a3 + i * TH * 2
        else:
            off = _a5 + 4 * TH * 4
        YHI.append(nc.alloc_sbuf_tensor_at(
            f"yhi{i}", [128, TH], BF16, offset=off).ap())
    p0 = pst("p0", [128, TH])
    p1 = pst("p1", [128, TH])
    p2 = pst("p2", [128, TH])
    p3 = pst("p3", [128, TH])
    p4 = pst("p4", [128, TH])
    PS = [p0, p1, p2, p3]
    PK = ["p0", "p1", "p2", "p3"]

    sems = {}
    for e in Prog.ENGINES:
        sems[e] = ctx.enter_context(nc.semaphore("sem_" + e))
    qsems = [ctx.enter_context(nc.semaphore(f"sem_q{i}")) for i in range(Prog.NQ)]
    gqsems = [ctx.enter_context(nc.semaphore(f"sem_gq{i}")) for i in range(2)]
    sqsems = [ctx.enter_context(nc.semaphore(f"sem_sq{i}")) for i in range(2)]
    sem_boot = ctx.enter_context(nc.semaphore("sem_boot"))

    P = Prog()
    P.gqsems = gqsems
    P.sqsems = sqsems

    sx = AR1[:, 0:KD * TP].rearrange("p (k t) -> p k t", k=KD)
    wsl = lambda i: AR2[:, i % 2, 0:KD * 128 if True else 0].rearrange("p (k c) -> p k c", c=128)
    wk = lambda i: f"w{i % 2}"
    xprojT = AR3[:, 0:IT * 160].rearrange("p (i j) -> p i j", j=160)
    B_rep = AR3[:, 0:NS * TH].rearrange("p (n t) -> p n t", t=TH)
    C_rep5 = AR3[:, 0:NS * TH].rearrange("p (n t) -> p n t", t=TH)
    g_ar = AR3[:, 0:MH * TH].rearrange("p (m t) -> p m t", t=TH)
    ga_tmp = AR3[:, MH * TH:(MH + 1) * TH]
    s_crep = AR3[:, (MH + 1) * TH:(MH + 2) * TH]
    C_rep4 = AR1[:, 0:NS * TH].rearrange("p (n t) -> p n t", t=TH)
    dtprojT = AR1[:, NS * TH:NS * TH + IT * 128].rearrange("p (i c) -> p i c", c=128)
    y_half = AR1[:, 0:IT * TH].rearrange("p (i t) -> p i t", t=TH)
    ln3 = AR1[:, 0:KD * TH].rearrange("p (k t) -> p k t", t=TH)
    hin = lambda i: AR4[:, (i % 2) * TP:(i % 2) * TP + TP]
    hends = AR4[:, 0:NCORES * IT * NS].rearrange("p (c i n) -> p c i n", i=IT, n=NS)
    mo = AR4[:, 0:KD * TH].rearrange("p (k t) -> p k t", t=TH)
    racc = AR4[:, 0:KD * TH].rearrange("p (k t) -> p k t", t=TH)
    A5 = lambda i: AR5[:, i * TH:(i + 1) * TH]
    A5k = lambda i: f"A5s{i}"
    s_all = AR5[:, 0:2 * IT * 2 * NS].rearrange("p (b i w) -> p b i w", i=IT, w=2 * NS)
    s_pq = AR5[:, 5 * TH:5 * TH + IT * 2 * NS].rearrange("p (i w) -> p i w", w=2 * NS)
    s_sb = AR5[:, 8 * TH:9 * TH]
    s_bb = AR5[:, 9 * TH:10 * TH]

    def ld(dst_ap, src_ap, wkeys, rkeys=()):
        return P.op("q", lambda q: q.dma_start(out=dst_ap, in_=src_ap),
                    reads=list(rkeys), writes=list(wkeys))

    st = ld

    ld(c_convw[:, :, :], convw_i, ["c_convw"])
    ld(c_convb[:, :], convb_i, ["c_convb"])
    ld(c_dtb[:, :], dtb_i, ["c_dtb"])
    ld(c_D[:, :], dvec_i, ["c_D"])
    ld(c_A[:, :, :], alog_i, ["c_A"])
    ld(c_gatew[:, :, :], gatew_i, ["c_gatew"])
    ld(c_csel[:, :], csel_i, ["c_csel"])
    ld(c_eye[:, :], eye_i, ["c_eye"])
    ld(xprojT[:, :, :], xproj_i, ["xprojT"])
    ld(sx[:, :, :], x_in[:, :, :], ["sx"])
    P.op("g", lambda g: g.memset(c_ones1[:, :], 1.0), writes=["c_ones1"])
    P.op("g", lambda g: g.memset(c_onesk[:, :], 1.0), writes=["c_onesk"])
    P.op("g", lambda g: g.memset(c_zeros[:, :], 0.0), writes=["c_zeros"])
    P.op("g", lambda g: g.memset(c_eps[:, :], EPS), writes=["c_eps"])
    P.op("g", lambda g: g.memset(c_neghalf[:, :], -0.5), writes=["c_neghalf"])
    P.op("g", lambda g: g.memset(c_negone[:, :], -1.0), writes=["c_negone"])
    P.op("g", lambda g: g.memset(s_comb[:, :], 0.0), writes=["s_comb"])
    P.op("s", lambda s: s.activation(c_A[:, :, :], c_A[:, :, :], AF.Exp,
                                     bias=0.0, scale=1.0),
         reads=["c_A"], writes=["c_A"])
    P.op("v", lambda v: v.tensor_scalar_mul(c_A[:, :, :], c_A[:, :, :], -1.0),
         reads=["c_A"], writes=["c_A"])

    def wsplit(src_ap, nk, slot, cast_eng=None):
        stg = AR2[:, 0, 0:nk * 128]
        whi = bw[:, slot, 0, 0:nk * 128]
        wlo = bw[:, slot, 1, 0:nk * 128]
        if cast_eng is None:
            P.op("g", lambda g, whi=whi, src_ap=src_ap: g.dma_start(
                out=whi.rearrange("p (k c) -> p k c", c=128), in_=src_ap),
                reads=[], writes=[f"bw{slot}h"], gdma=True)
            ld(stg, src_ap, ["wstg"])
        else:
            ld(stg, src_ap, ["wstg"])
            if cast_eng == "s":
                P.op("s", lambda s, whi=whi, stg=stg: s.activation(
                    whi, stg, AF.Copy, bias=0.0, scale=1.0),
                    reads=["wstg"], writes=[f"bw{slot}h"])
            else:
                P.op("v", lambda v, whi=whi, stg=stg: v.tensor_copy(whi, stg),
                     reads=["wstg"], writes=[f"bw{slot}h"])
        P.op("v", lambda v, stg=stg, whi=whi, wlo=wlo: v.tensor_sub(
            wlo, stg, whi),
            reads=["wstg", f"bw{slot}h"], writes=[f"bw{slot}l"])
        return (bw[:, slot, 0, 0:nk * 128].rearrange("p (k c) -> p k c", c=128),
                bw[:, slot, 1, 0:nk * 128].rearrange("p (k c) -> p k c", c=128),
                [f"bw{slot}h"], [f"bw{slot}l"])

    def mm3(bank, bkey, whi, wlo, wkh, wkl, rh_hi, rh_lo, rkeys, nk,
            first, last, w=TH):
        for k in range(nk):
            P.op("pe", lambda pe, k=k, bank=bank, whi=whi, rh_hi=rh_hi, w=w:
                 pe.matmul(bank[:, 0:w], whi[:, k, :], rh_hi(k),
                           start=(first and k == 0), stop=False),
                 reads=wkh + rkeys, writes=[bkey])
            P.op("pe", lambda pe, k=k, bank=bank, whi=whi, rh_lo=rh_lo, w=w:
                 pe.matmul(bank[:, 0:w], whi[:, k, :], rh_lo(k),
                           start=False, stop=False),
                 reads=wkh + rkeys, writes=[bkey])
        for k in range(nk):
            P.op("pe", lambda pe, k=k, bank=bank, wlo=wlo, rh_hi=rh_hi, w=w:
                 pe.matmul(bank[:, 0:w], wlo[:, k, :], rh_hi(k),
                           start=False, stop=(last and k == nk - 1)),
                 reads=wkl + rkeys, writes=[bkey])

    def emit_ln(src3, nk, tlen, dst3, src_reg, dst_reg, scr, scrk,
                res_load=None, dst_sink=None):
        slices = [(a, min(a + TH, tlen)) for a in range(0, tlen, TH)]
        for (a, b) in slices:
            w = b - a
            for k in range(nk):
                P.op("pe", lambda pe, k=k, a=a, b=b, w=w: pe.matmul(
                    p2[0:1, 0:w], c_onesk[:, :], src3[:, k, a:b],
                    start=(k == 0), stop=(k == nk - 1)),
                    reads=[src_reg, "c_onesk"], writes=["p2"])
            for k in range(nk):
                sc, sck = scr(k), scrk(k)
                P.op("s", lambda s, k=k, a=a, w=w, sc=sc: s.activation(
                    sc[:, 0:w], src3[:, k, a:a + w], AF.Square, bias=0.0,
                    scale=1.0),
                    reads=[src_reg], writes=[sck])
                P.op("pe", lambda pe, k=k, w=w, sc=sc: pe.matmul(
                    p3[0:1, 0:w], c_onesk[:, :], sc[:, 0:w],
                    start=(k == 0), stop=(k == nk - 1)),
                    reads=[sck, "c_onesk"], writes=["p3"])
            P.op("v", lambda v, w=w: v.tensor_scalar_mul(
                r_m[0:1, 0:w], p2[0:1, 0:w], inv_d),
                reads=["p2"], writes=["r_m"])
            P.op("v", lambda v, w=w: v.tensor_scalar_mul(
                r_v[0:1, 0:w], p3[0:1, 0:w], inv_d),
                reads=["p3"], writes=["r_v"])
            P.op("v", lambda v, w=w: v.tensor_mul(
                s_rs[0:1, 0:w], r_m[0:1, 0:w], r_m[0:1, 0:w]),
                reads=["r_m"], writes=["s_rs"])
            P.op("v", lambda v, w=w: v.tensor_sub(
                r_v[0:1, 0:w], r_v[0:1, 0:w], s_rs[0:1, 0:w]),
                reads=["r_v", "s_rs"], writes=["r_v"])
            P.op("s", lambda s, w=w: s.activation(
                r_v[0:1, 0:w], r_v[0:1, 0:w], AF.Ln,
                bias=c_eps[0:1, 0:1], scale=1.0),
                reads=["r_v", "c_eps"], writes=["r_v"])
            P.op("s", lambda s, w=w: s.activation(
                s_rs[0:1, 0:w], r_v[0:1, 0:w], AF.Exp, bias=0.0,
                scale=c_neghalf[0:1, 0:1]),
                reads=["r_v", "c_neghalf"], writes=["s_rs"])
            P.op("v", lambda v, w=w: v.tensor_mul(
                s_rb[0:1, 0:w], r_m[0:1, 0:w], s_rs[0:1, 0:w]),
                reads=["r_m", "s_rs"], writes=["s_rb"])
            P.op("v", lambda v, a=a, w=w: v.tensor_scalar_mul(
                s_rb[0:1, 0:w], s_rb[0:1, 0:w], -1.0),
                reads=["s_rb"], writes=["s_rb"])
            P.op("pe", lambda pe, a=a, w=w: pe.matmul(
                p2[:, 0:w], c_ones1[:, :], s_rs[0:1, 0:w],
                start=True, stop=True),
                reads=["s_rs", "c_ones1"], writes=["p2"])
            P.op("v", lambda v, a=a, w=w: v.tensor_copy(
                s_sb[:, 0:w], p2[:, 0:w]),
                reads=["p2"], writes=["s_sb"])
            P.op("pe", lambda pe, a=a, w=w: pe.matmul(
                p3[:, 0:w], c_ones1[:, :], s_rb[0:1, 0:w],
                start=True, stop=True),
                reads=["s_rb", "c_ones1"], writes=["p3"])
            P.op("v", lambda v, a=a, w=w: v.tensor_copy(
                s_bb[:, 0:w], p3[:, 0:w]),
                reads=["p3"], writes=["s_bb"])
            for k in range(nk):
                sc, sck = scr(k), scrk(k)
                P.op("v", lambda v, k=k, a=a, w=w, sc=sc: v.tensor_mul(
                    sc[:, 0:w], src3[:, k, a:a + w], s_sb[:, 0:w]),
                    reads=[src_reg, "s_sb"], writes=[sck])
                if res_load is None and dst_sink is None:
                    P.op("v", lambda v, k=k, a=a, w=w, sc=sc: v.tensor_add(
                        dst3[:, k, a:a + w], sc[:, 0:w], s_bb[:, 0:w]),
                        reads=[sck, "s_bb"], writes=[dst_reg])
                else:
                    P.op("v", lambda v, k=k, w=w, sc=sc: v.tensor_add(
                        sc[:, 0:w], sc[:, 0:w], s_bb[:, 0:w]),
                        reads=[sck, "s_bb"], writes=[sck])
                    if res_load is not None:
                        rap, rkey = res_load(k)
                        if dst_sink is None:
                            P.op("v", lambda v, k=k, a=a, w=w, sc=sc, rap=rap:
                                 v.tensor_add(dst3[:, k, a:a + w], sc[:, 0:w],
                                              rap),
                                 reads=[sck, rkey], writes=[dst_reg])
                        else:
                            dst_sink(k, sc, sck, rap, rkey)
                    else:
                        dst_sink(k, sc, sck, None, None)

    emit_ln(sx, KD, TP, sx, "sx", "sx",
            scr=lambda k: hin(k), scrk=lambda k: f"hin{k % 2}")
    if "t_ln1" in tp:
        st(tp["t_ln1"][:, :, :], sx[:, :, :], [], rkeys=["sx"])

    for k in range(KD):
        P.op("s", lambda s, k=k: s.activation(
            SXHI[k][:, :], sx[:, k, :], AF.Copy, bias=0.0, scale=1.0),
            reads=["sx"], writes=[f"sxh{k}"])
    for k in range(KD):
        P.op("v", lambda v, k=k: v.tensor_sub(
            sxlo[:, k, :], sx[:, k, :], SXHI[k][:, :]),
            reads=["sx", f"sxh{k}"], writes=["sxlo", "sx"])

    NT1 = [(0, 3)] + [(3 + i * TH, 3 + (i + 1) * TH) for i in range(T // TH)]
    NT2 = NT1[1:]
    pre1 = wsplit(w_inproj[0], KD, 0)
    xp_defer = [None]
    for m in range(2 * IT):
        whi, wlo, wkh, wkl = pre1
        is_h = m < IT
        ub = AR5[:, (m % 2) * 2 * TH:((m % 2) * 2 + 2) * TH]
        uk = f"ub{m % 2}"
        gb = AR5[:, (4 + (m % 2) * 2) * TH:(4 + (m % 2) * 2 + 2) * TH]
        gk = f"gb{m % 2}"
        for ti, (a, b) in enumerate(NT1 if is_h else NT2):
            w = b - a
            bank, bkey = PS[ti % 2], PK[ti % 2]
            mm3(bank, bkey, whi, wlo, wkh, wkl,
                lambda k, a=a, b=b: SXHI[k][:, a:b],
                lambda k, a=a, b=b: sxlo[:, k, a:b],
                [f"sxh{kk}" for kk in range(KD)] + ["sxlo"], KD,
                True, True, w=w)
            if is_h:
                P.op("v", lambda v, a=a, w=w, bank=bank, m=m: v.tensor_copy(
                    hin(m)[:, a:a + w], bank[:, 0:w]),
                    reads=[bkey], writes=[f"hin{m % 2}"])
            else:
                P.op("s", lambda s, a=a, w=w, bank=bank, gb=gb: s.activation(
                    gb[:, a - 3:a - 3 + w], bank[:, 0:w], AF.Silu,
                    bias=0.0, scale=1.0),
                    reads=[bkey], writes=[gk])
        if m + 1 < 2 * IT:
            pre1 = wsplit(w_inproj[m + 1], KD, (m + 1) % 2)
        if xp_defer[0] is not None:
            xp_defer[0]()
            xp_defer[0] = None
        if is_h:
            hv, hk = hin(m), f"hin{m % 2}"
            P.op("v", lambda v, m=m, hv=hv, ub=ub: v.tensor_scalar(
                ub[:, 0:T], hv[:, 0:T], c_convw[:, m, 0:1], None, ALU.mult),
                reads=[hk, "c_convw"], writes=[uk])
            for j in range(1, 4):
                P.op("v", lambda v, m=m, j=j, hv=hv, ub=ub: v.scalar_tensor_tensor(
                    ub[:, 0:T], hv[:, j:j + T], c_convw[:, m, j:j + 1], ub[:, 0:T],
                    ALU.mult, ALU.add),
                    reads=[hk, "c_convw", uk], writes=[uk])
            P.op("s", lambda s, m=m, ub=ub: s.activation(
                ub[:, 0:T], ub[:, 0:T], AF.Silu, bias=c_convb[:, m:m + 1],
                scale=1.0),
                reads=[uk, "c_convb"], writes=[uk])

            def xproj(m=m, ub=ub, uk=uk):
                for a2 in (0, TH):
                    for Mg, (ja, jb) in enumerate(((0, 128), (128, 160))):
                        bank, bkey = PS[2 + Mg], PK[2 + Mg]
                        P.op("pe", lambda pe, m=m, a2=a2, ja=ja, jb=jb,
                             bank=bank, ub=ub: pe.matmul(
                             bank[0:jb - ja, 0:TH], xprojT[:, m, ja:jb],
                             ub[:, a2:a2 + TH], start=True, stop=True),
                             reads=["xprojT", uk], writes=[bkey])
                        dst, dk = (s_ts, "s_ts") if Mg == 0 else (s_bc, "s_bc")
                        np_ = 128 if Mg == 0 else 32
                        if m == 0:
                            P.op("v", lambda v, a2=a2, bank=bank, dst=dst,
                                 np_=np_: v.tensor_copy(dst[:, a2:a2 + TH],
                                                        bank[0:np_, 0:TH]),
                                 reads=[bkey], writes=[dk])
                        else:
                            P.op("v", lambda v, a2=a2, bank=bank, dst=dst,
                                 np_=np_: v.tensor_add(dst[:, a2:a2 + TH],
                                                       dst[:, a2:a2 + TH],
                                                       bank[0:np_, 0:TH]),
                                 reads=[bkey, dk], writes=[dk])
                st(hb_u[m], ub[:, 0:T], [f"hbu{m}"], rkeys=[uk])

            if m == IT - 1:
                xproj()
            else:
                xp_defer[0] = xproj
        else:
            st(hb_gate[m - IT], gb[:, 0:T], [f"hbg{m - IT}"], rkeys=[gk])

    if "t_u" in tp:
        st(tp["t_u"][:, :, :], hb_u[:, :, :], [], rkeys=[f"hbu{i}" for i in range(IT)])
    if "t_ssm" in tp:
        st(tp["t_ssm"][0:128, :], s_ts[:, :], [], rkeys=["s_ts"])
        st(tp["t_ssm"][128:160, :], s_bc[:, :], [], rkeys=["s_bc"])

    P.fence()

    ld(dtprojT[:, :, :], dtproj_i, ["dtprojT"])
    for th in range(2):
        ta, tb = th * TH, (th + 1) * TH
        for n in range(NS):
            P.op("v", lambda v, n=n, ta=ta, tb=tb: v.stream_shuffle(
                s_stage[:, :], s_bc[:, ta:tb], [n] + [0] * 31),
                reads=["s_bc"], writes=["s_stage"])
            P.op("pe", lambda pe, n=n: pe.matmul(
                p2[:, 0:TH], c_ones1[:, :], s_stage[0:1, 0:TH],
                start=True, stop=True),
                reads=["s_stage", "c_ones1"], writes=["p2"])
            P.op("v", lambda v, n=n: v.tensor_copy(B_rep[:, n, :], p2[:, 0:TH]),
                reads=["p2"], writes=["B_rep"])
            P.op("v", lambda v, n=n, ta=ta, tb=tb: v.stream_shuffle(
                s_stage[:, :], s_bc[:, ta:tb], [NS + n] + [0] * 31),
                reads=["s_bc"], writes=["s_stage"])
            P.op("pe", lambda pe, n=n: pe.matmul(
                p3[:, 0:TH], c_ones1[:, :], s_stage[0:1, 0:TH],
                start=True, stop=True),
                reads=["s_stage", "c_ones1"], writes=["p3"])
            P.op("s", lambda s, n=n: s.activation(
                C_rep4[:, n, :], p3[:, 0:TH], AF.Copy, bias=0.0, scale=1.0),
                reads=["p3"], writes=["C_rep4"])
        t_dtu = A5(2)
        for it in range(IT):
            t_t1, k1 = A5(3), "A5s3"
            t_t2, k2 = A5(4), "A5s4"
            t_dt = t_t1
            t_S, sSk = A5(it % 2), A5k(it % 2)
            t_ya, yak = A5(8), A5k(8)
            t_yb, ybk = A5(9), A5k(9)
            t_u, uk2 = A5(10 + it % 2), A5k(10 + it % 2)
            ld(t_u[:, :], hb_u[it][:, ta:tb], [uk2], rkeys=[f"hbu{it}"])
            bank, bkey = PS[it % 2], PK[it % 2]
            P.op("pe", lambda pe, it=it, bank=bank, ta=ta, tb=tb: pe.matmul(
                bank[:, 0:TH], dtprojT[:, it, :], s_ts[:, ta:tb],
                start=True, stop=True),
                reads=["dtprojT", "s_ts"], writes=[bkey])
            P.op("s", lambda s, it=it, bank=bank, t_t1=t_t1: s.activation(
                t_t1[:, :], bank[:, 0:TH], AF.Abs, bias=c_dtb[:, it:it + 1],
                scale=1.0),
                reads=[bkey, "c_dtb"], writes=[k1])
            P.op("s", lambda s, t_t1=t_t1, t_t2=t_t2: s.activation(
                t_t2[:, :], t_t1[:, :], AF.Exp, bias=0.0,
                scale=c_negone[:, 0:1]),
                reads=[k1, "c_negone"], writes=[k2])
            P.op("s", lambda s, t_t1=t_t1, t_t2=t_t2: s.activation(
                t_t1[:, :], t_t2[:, :], AF.Ln, bias=1.0, scale=1.0),
                reads=[k2], writes=[k1])
            P.op("s", lambda s, it=it, bank=bank, t_t2=t_t2: s.activation(
                t_t2[:, :], bank[:, 0:TH], AF.Relu, bias=c_dtb[:, it:it + 1],
                scale=1.0),
                reads=[bkey, "c_dtb"], writes=[k2])
            P.op("v", lambda v, it=it, th=th, t_dt=t_dt, t_t1=t_t1, t_t2=t_t2:
                 v.scalar_tensor_tensor(
                 t_dt[:, :], t_t1[:, :], 1.0, t_t2[:, :], ALU.mult, ALU.add,
                 accum_out=s_dtsum[:, it, th:th + 1]),
                reads=[k1, k2], writes=[k1, "s_dtsum"])
            if "t_dt" in tp:
                st(tp["t_dt"][it][:, ta:tb], t_dt[:, :], [], rkeys=[k1])
            init_s = 0.0 if th == 0 else s_scar[:, it:it + 1]
            P.op("v", lambda v, init_s=init_s, t_S=t_S, t_dt=t_dt:
                 v.tensor_tensor_scan(
                 out=t_S[:, :], data0=t_dt[:, :], data1=c_zeros[:, :],
                 initial=init_s, op0=ALU.add, op1=ALU.add),
                reads=[k1, "c_zeros"] + ([f"sc{it}"] if th else []),
                writes=[sSk])
            if th == 0:
                P.op("v", lambda v, it=it, t_S=t_S: v.tensor_copy(
                    s_scar[:, it:it + 1], t_S[:, TH - 1:TH]),
                    reads=[sSk], writes=[f"sc{it}"])
            st(hb_S[it][:, ta:tb], t_S[:, :], [f"hbS{it}"], rkeys=[sSk])
            P.op("v", lambda v, t_u=t_u, uk2=uk2, t_dt=t_dt: v.tensor_mul(
                t_dtu[:, :], t_dt[:, :], t_u[:, :]),
                reads=[k1, uk2], writes=["A5s2"])
            for ng in range(2):
                nsl = slice(8 * ng, 8 * ng + 8)
                dA8f = AR4[:, 0:8 * TH]
                dA8 = dA8f.rearrange("p (n t) -> p n t", t=TH)
                h8f = AR4[:, 8 * TH:16 * TH]
                h8 = h8f.rearrange("p (n t) -> p n t", t=TH)
                for j in range(8):
                    P.op("s", lambda s, it=it, ng=ng, j=j, dA8=dA8, t_dt=t_dt:
                         s.activation(
                         dA8[:, j, :], t_dt[:, :], AF.Exp, bias=0.0,
                         scale=c_A[:, it, 8 * ng + j:8 * ng + j + 1]),
                        reads=[k1, "c_A"], writes=["dA8"])
                P.op("v", lambda v, nsl=nsl, h8=h8: v.tensor_mul(
                    h8[:, :, :],
                    t_dtu[:, :].unsqueeze(1).broadcast_to([128, 8, TH]),
                    B_rep[:, nsl, :]),
                    reads=["A5s2", "B_rep"], writes=["h8"])
                P.op("v", lambda v, dA8=dA8: v.tensor_copy(
                    s_tok2[:, 0, :], dA8[:, :, 0]),
                    reads=["dA8"], writes=["s_tok2"])
                P.op("v", lambda v, dA8=dA8: v.memset(dA8[:, :, 0], 0.0),
                    reads=[], writes=["dA8"])
                if th == 1:
                    P.op("v", lambda v, it=it, nsl=nsl: v.tensor_mul(
                        s_tok2[:, 1, :], s_tok2[:, 0, :], s_hcar[:, it, nsl]),
                        reads=["s_tok2", f"hc{it}"], writes=["s_tok2"])
                    P.op("v", lambda v, h8=h8: v.tensor_add(
                        h8[:, :, 0], h8[:, :, 0], s_tok2[:, 1, :]),
                        reads=["s_tok2", "h8"], writes=["h8"])
                P.op("v", lambda v, dA8f=dA8f, h8f=h8f: v.tensor_tensor_scan(
                    out=h8f, data0=dA8f, data1=h8f,
                    initial=0.0, op0=ALU.mult, op1=ALU.add),
                    reads=["dA8", "h8"], writes=["h8"])
                P.op("v", lambda v, it=it, nsl=nsl, h8=h8: v.tensor_copy(
                    s_hcar[:, it, nsl], h8[:, :, TH - 1]),
                    reads=["h8"], writes=[f"hc{it}"])
                P.op("v", lambda v, nsl=nsl, h8=h8: v.tensor_mul(
                    h8[:, :, :], h8[:, :, :], C_rep4[:, nsl, :]),
                    reads=["h8", "C_rep4"], writes=["h8"])
                P.op("v", lambda v, h8=h8: v.tensor_add(
                    h8[:, 0:4, :], h8[:, 0:4, :], h8[:, 4:8, :]),
                    reads=["h8"], writes=["h8"])
                P.op("v", lambda v, h8=h8: v.tensor_add(
                    h8[:, 0:2, :], h8[:, 0:2, :], h8[:, 2:4, :]),
                    reads=["h8"], writes=["h8"])
                if ng == 0:
                    P.op("v", lambda v, t_ya=t_ya, h8=h8: v.tensor_add(
                        t_ya[:, :], h8[:, 0, :], h8[:, 1, :]),
                        reads=["h8"], writes=[yak])
                else:
                    P.op("v", lambda v, t_yb=t_yb, h8=h8: v.tensor_add(
                        t_yb[:, :], h8[:, 0, :], h8[:, 1, :]),
                        reads=["h8"], writes=[ybk])
            P.op("v", lambda v, t_ya=t_ya, t_yb=t_yb: v.tensor_add(
                t_ya[:, :], t_ya[:, :], t_yb[:, :]),
                reads=[yak, ybk], writes=[yak])
            st(hb_yloc[it][:, ta:tb], t_ya[:, :], [f"hbY{it}"], rkeys=[yak])

    if "t_yloc" in tp:
        st(tp["t_yloc"][:, :, :], hb_yloc[:, :, :], [], rkeys=[f"hbY{i}" for i in range(IT)])
    if "t_S" in tp:
        st(tp["t_S"][:, :, :], hb_S[:, :, :], [], rkeys=[f"hbS{i}" for i in range(IT)])

    P.op("v", lambda v: v.tensor_add(
        s_dtsum[:, :, 0:1], s_dtsum[:, :, 0:1], s_dtsum[:, :, 1:2]),
        reads=["s_dtsum"], writes=["s_dtsum"])
    for it in range(IT):
        P.op("s", lambda s, it=it: s.activation(
            s_pq[:, it, 0:NS], c_A[:, it, :], AF.Exp, bias=0.0,
            scale=s_dtsum[:, it, 0:1]),
            reads=["c_A", "s_dtsum"], writes=["s_pq"])
    P.op("v", lambda v: v.tensor_copy(s_pq[:, :, NS:2 * NS], s_hcar[:, :, :]),
         reads=[f"hc{i}" for i in range(IT)], writes=["s_pq"])
    st(cc_in.rearrange("(i p) w -> p i w", p=128), s_pq[:, :, :], ["cc_in"],
       rkeys=["s_pq"])
    P.op("g", lambda g: g.collective_compute(
        "AllGather", ALU.bypass, replica_groups=[list(range(NCORES))],
        ins=[cc_in.opt()], outs=[cc_out.opt()]),
        reads=["cc_in"], writes=["cc_out"])
    P.fence()
    for c in range(NCORES):
        ld(s_all[:, c % 2, :, :],
           cc_out.rearrange("(b i p) w -> b p i w", i=IT, p=128)[c],
           [f"all{c % 2}"], rkeys=["cc_out"])
        if c == 0:
            P.op("v", lambda v: v.tensor_copy(
                hends[:, 0, :, :], s_all[:, 0, :, NS:2 * NS]),
                reads=["all0"], writes=["hends"])
        else:
            P.op("v", lambda v, c=c: v.tensor_mul(
                hends[:, c, :, :], s_all[:, c % 2, :, 0:NS],
                hends[:, c - 1, :, :]),
                reads=[f"all{c % 2}", "hends"], writes=["hends"])
            P.op("v", lambda v, c=c: v.tensor_add(
                hends[:, c, :, :], hends[:, c, :, :],
                s_all[:, c % 2, :, NS:2 * NS]),
                reads=[f"all{c % 2}", "hends"], writes=["hends"])
    P.op("g", lambda g: g.memset(s_hin[:, :, :], 0.0), writes=["s_hin"])
    for c in range(NCORES):
        P.op("v", lambda v, c=c: v.scalar_tensor_tensor(
            s_hin[:, :, :], hends[:, c, :, :], c_csel[:, c:c + 1],
            s_hin[:, :, :], ALU.mult, ALU.add),
            reads=["hends", "c_csel", "s_hin"], writes=["s_hin"])
    if "t_hin" in tp:
        st(tp["t_hin"][:, :, :], s_hin[:, :, :], [], rkeys=["s_hin"])
    P.fence()
    def emit_crep5_th0():
        for n in range(NS):
            P.op("v", lambda v, n=n: v.stream_shuffle(
                s_stage[:, :], s_bc[:, 0:TH], [NS + n] + [0] * 31),
                reads=["s_bc"], writes=["s_stage"])
            P.op("pe", lambda pe, n=n: pe.matmul(
                p2[:, 0:TH], c_ones1[:, :], s_stage[0:1, 0:TH],
                start=True, stop=True),
                reads=["s_stage", "c_ones1"], writes=["p2"])
            P.op("v", lambda v, n=n: v.tensor_copy(C_rep5[:, n, :], p2[:, 0:TH]),
                reads=["p2"], writes=["C_rep5"])

    def emit_fix_th0():
        ta, tb = 0, TH
        f_e, f_t = A5(6), A5(7)
        for it in range(IT):
            f_S, kS = A5(it % 2), A5k(it % 2)
            f_u, ku = A5(2 + it % 2), A5k(2 + it % 2)
            f_g, kg = A5(4 + it % 2), A5k(4 + it % 2)
            ld(y_half[:, it, :], hb_yloc[it][:, ta:tb], [f"y{it}"],
               rkeys=[f"hbY{it}"])
            ld(f_S[:, :], hb_S[it][:, ta:tb], [kS], rkeys=[f"hbS{it}"])
            ld(f_u[:, :], hb_u[it][:, ta:tb], [ku], rkeys=[f"hbu{it}"])
            ld(f_g[:, :], hb_gate[it][:, ta:tb], [kg], rkeys=[f"hbg{it}"])
            f_a2, k_a2 = A5(10 + it % 2), A5k(10 + it % 2)
            for n in range(NS):
                e_sl, e_k = (A5(6), "A5s6") if n % 2 == 0 else (A5(8), "A5s8")
                t_sl, t_k = (A5(7), "A5s7") if n % 2 == 0 else (A5(9), "A5s9")
                P.op("s", lambda s, it=it, n=n, f_S=f_S, e_sl=e_sl: s.activation(
                    e_sl[:, :], f_S[:, :], AF.Exp, bias=0.0,
                    scale=c_A[:, it, n:n + 1]),
                    reads=[kS, "c_A"], writes=[e_k])
                P.op("v", lambda v, n=n, e_sl=e_sl, t_sl=t_sl: v.tensor_mul(
                    t_sl[:, :], e_sl[:, :], C_rep5[:, n, :]),
                    reads=[e_k, "C_rep5"], writes=[t_k])
                if n % 2 == 0:
                    P.op("v", lambda v, it=it, n=n, t_sl=t_sl:
                         v.scalar_tensor_tensor(
                         y_half[:, it, :], t_sl[:, :], s_hin[:, it, n:n + 1],
                         y_half[:, it, :], ALU.mult, ALU.add),
                         reads=[t_k, "s_hin", f"y{it}"], writes=[f"y{it}"])
                else:
                    src2 = c_zeros if n == 1 else f_a2
                    src2k = "c_zeros" if n == 1 else k_a2
                    P.op("v", lambda v, it=it, n=n, t_sl=t_sl, f_a2=f_a2,
                         src2=src2: v.scalar_tensor_tensor(
                         f_a2[:, :], t_sl[:, :], s_hin[:, it, n:n + 1],
                         src2[:, 0:TH], ALU.mult, ALU.add),
                         reads=[t_k, "s_hin", src2k], writes=[k_a2])
            P.op("v", lambda v, it=it, f_a2=f_a2: v.tensor_add(
                y_half[:, it, :], y_half[:, it, :], f_a2[:, :]),
                reads=[f"y{it}", k_a2], writes=[f"y{it}"])
            P.op("v", lambda v, it=it, f_u=f_u: v.scalar_tensor_tensor(
                y_half[:, it, :], f_u[:, :], c_D[:, it:it + 1],
                y_half[:, it, :], ALU.mult, ALU.add),
                reads=[ku, "c_D", f"y{it}"], writes=[f"y{it}"])
            P.op("v", lambda v, it=it, f_g=f_g: v.tensor_mul(
                y_half[:, it, :], y_half[:, it, :], f_g[:, :]),
                reads=[kg, f"y{it}"], writes=[f"y{it}"])
            if "t_y" in tp:
                st(tp["t_y"][it][:, ta:tb], y_half[:, it, :], [],
                   rkeys=[f"y{it}"])

    def emit_ysplit_th0():
        for it in range(IT):
            P.op("s", lambda s, it=it: s.activation(
                YHI[it][:, :], y_half[:, it, :], AF.Copy, bias=0.0, scale=1.0),
                reads=[f"y{it}"], writes=[f"yh{it}", "C_rep5"])
        for it in range(IT):
            P.op("v", lambda v, it=it: v.tensor_sub(
                ylo[:, it, :], y_half[:, it, :], YHI[it][:, :]),
                reads=[f"y{it}", f"yh{it}"],
                writes=["yloK", f"y{it // 2}", f"y{(it + 1) // 2}"])

    def emit_outproj(rh_hi, rh_lo, rkeys, mo_ap, mo_key):
        for k in range(KD):
            bank, bkey = PS[k % 2], PK[k % 2]
            for half in range(2):
                whi, wlo, wkh, wkl = wsplit(
                    w_outproj[k][:, half * (IT // 2):(half + 1) * (IT // 2), :],
                    IT // 2, half)
                mm3(bank, bkey, whi, wlo, wkh, wkl,
                    lambda kk, half=half: rh_hi(half * (IT // 2) + kk),
                    lambda kk, half=half: rh_lo(half * (IT // 2) + kk),
                    rkeys, IT // 2, half == 0, half == 1)
            P.op("v", lambda v, k=k, bank=bank, mo_ap=mo_ap: v.tensor_copy(
                mo_ap[:, k, :], bank[:, 0:TH]),
                reads=[bkey], writes=[mo_key])

    def emit_ln23(th, mo_ap, mo_key):
        ta, tb = th * TH, (th + 1) * TH

        def xres_load(k, ta=ta, tb=tb):
            sl, key = A5(2 + k % 2), A5k(2 + k % 2)
            ld(sl[:, :], x_in[:, k, 3 + ta:3 + tb], [key])
            return sl[:, :], key

        emit_ln(mo_ap, KD, TH, mo_ap, mo_key, mo_key,
                scr=lambda k: A5(k % 2), scrk=lambda k: A5k(k % 2),
                res_load=xres_load)
        for k in range(KD):
            st(hb_x2[k][:, ta:tb], mo_ap[:, k, :], [f"hbx2{k}"],
               rkeys=[mo_key])
            if "t_x2" in tp:
                st(tp["t_x2"][:, k, ta:tb], mo_ap[:, k, :], [],
                   rkeys=[mo_key])
        P.fence()

        def ln3_sink(k, sc, sck, rap, rkey):
            P.op("pe", lambda pe, k=k, sc=sc: pe.matmul(
                p2[0:NE, 0:TH], c_gatew[:, k, :], sc[:, 0:TH],
                start=(k == 0), stop=(k == KD - 1)),
                reads=["c_gatew", sck], writes=["p2"])
            P.op("s", lambda s, k=k, sc=sc: s.activation(
                bln3[:, 0, k, :], sc[:, 0:TH], AF.Copy, bias=0.0, scale=1.0),
                reads=[sck], writes=["bln3"])
            P.op("v", lambda v, k=k, sc=sc: v.tensor_sub(
                bln3[:, 1, k, :], sc[:, 0:TH], bln3[:, 0, k, :]),
                reads=[sck, "bln3"], writes=["bln3"])

        emit_ln(mo_ap, KD, TH, None, mo_key, "ln3",
                scr=lambda k: A5(k % 2), scrk=lambda k: A5k(k % 2),
                dst_sink=ln3_sink)
        P.fence()

    def emit_gating(th):
        ta, tb = th * TH, (th + 1) * TH
        P.op("v", lambda v: v.tensor_copy(s_lgf[0:NE, :], p2[0:NE, 0:TH]),
             reads=["p2"], writes=["s_stage"])
        for tc in range(TH // 128):
            P.op("pe", lambda pe, tc=tc: pe.transpose(
                p3[:, 0:NE], s_lgf[0:NE, tc * 128:(tc + 1) * 128],
                c_eye[0:NE, 0:NE]),
                reads=["s_stage", "c_eye"], writes=["p3"])
            P.op("v", lambda v, tc=tc: v.tensor_copy(
                s_tok[:, tc, :], p3[:, 0:NE]),
                reads=["p3"], writes=["s_tok"])
        for tc in range(TH // 128):
            P.op("v", lambda v, tc=tc: v.reduce_max(
                s_trow[:, tc:tc + 1], s_tok[:, tc, :], axis=AX.X, negate=True),
                reads=["s_tok"], writes=["s_trow"])
            P.op("s", lambda s, tc=tc: s.activation(
                s_tok2[:, tc, :], s_tok[:, tc, :], AF.Exp,
                bias=s_trow[:, tc:tc + 1], scale=1.0),
                reads=["s_tok", "s_trow"], writes=["s_tok2"])
            P.op("v", lambda v, tc=tc: v.reduce_sum(
                s_trow[:, tc:tc + 1], s_tok2[:, tc, :], axis=AX.X),
                reads=["s_tok2"], writes=["s_trow"])
            P.op("v", lambda v, tc=tc: v.reciprocal(
                s_trow[:, tc:tc + 1], s_trow[:, tc:tc + 1]),
                reads=["s_trow"], writes=["s_trow"])
            P.op("v", lambda v, tc=tc: v.tensor_scalar(
                s_tok[:, tc, :], s_tok2[:, tc, :], s_trow[:, tc:tc + 1], None,
                ALU.mult),
                reads=["s_tok2", "s_trow"], writes=["s_tok"])
            P.op("v", lambda v, tc=tc: v.reduce_max(
                s_trow[:, tc:tc + 1], s_tok[:, tc, :], axis=AX.X, negate=True),
                reads=["s_tok"], writes=["s_trow"])
            P.op("v", lambda v, tc=tc: v.tensor_scalar(
                s_tok2[:, tc, :], s_tok[:, tc, :], s_trow[:, tc:tc + 1], None,
                ALU.add),
                reads=["s_tok", "s_trow"], writes=["s_tok2"])
            P.op("v", lambda v, tc=tc: v.tensor_single_scalar(
                s_tok2[:, tc, :], s_tok2[:, tc, :], 0.0, ALU.is_ge),
                reads=["s_tok2"], writes=["s_tok2"])
            P.op("v", lambda v, tc=tc: v.scalar_tensor_tensor(
                s_tok2[:, tc, :], s_tok2[:, tc, :], -1e30, s_tok[:, tc, :],
                ALU.mult, ALU.add),
                reads=["s_tok2", "s_tok"], writes=["s_tok2"])
            P.op("v", lambda v, tc=tc: v.reduce_max(
                s_trow[:, tc:tc + 1], s_tok2[:, tc, :], axis=AX.X, negate=True),
                reads=["s_tok2"], writes=["s_trow"])
            P.op("v", lambda v, tc=tc: v.tensor_scalar(
                s_tok2[:, tc, :], s_tok[:, tc, :], s_trow[:, tc:tc + 1], None,
                ALU.add),
                reads=["s_tok", "s_trow"], writes=["s_tok2"])
            P.op("v", lambda v, tc=tc: v.tensor_single_scalar(
                s_tok2[:, tc, :], s_tok2[:, tc, :], 0.0, ALU.is_ge),
                reads=["s_tok2"], writes=["s_tok2"])
            P.op("v", lambda v, tc=tc: v.tensor_mul(
                s_tok[:, tc, :], s_tok[:, tc, :], s_tok2[:, tc, :]),
                reads=["s_tok", "s_tok2"], writes=["s_tok"])
            P.op("pe", lambda pe, tc=tc: pe.transpose(
                p3[0:NE, 0:128], s_tok[:, tc, :], c_eye[:, :]),
                reads=["s_tok", "c_eye"], writes=["p3"])
            P.op("v", lambda v, tc=tc: v.tensor_copy(
                s_comb[0:NE, tc * 128:(tc + 1) * 128], p3[0:NE, 0:128]),
                reads=["p3"], writes=["s_comb"])
        if "t_comb" in tp:
            st(tp["t_comb"][:, ta:tb], s_comb[0:NE, :], [], rkeys=["s_comb"])

    def gate_up(wg_pre, wu_ap, wg_next, mm, use_comb, ce=None):
        gf = A5(4)
        whi, wlo, wkh, wkl = wg_pre
        mm3(p0, "p0", whi, wlo, wkh, wkl,
            lambda k: bln3[:, 0, k, :], lambda k: bln3[:, 1, k, :],
            ["bln3"], KD, True, True)
        whi2, wlo2, wkh2, wkl2 = wsplit(wu_ap, KD, 1, cast_eng=ce)
        mm3(p1, "p1", whi2, wlo2, wkh2, wkl2,
            lambda k: bln3[:, 0, k, :], lambda k: bln3[:, 1, k, :],
            ["bln3"], KD, True, True)
        nxt = (wsplit(wg_next, KD, 0, cast_eng=ce)
               if wg_next is not None else None)
        P.op("s", lambda s: s.activation(
            ga_tmp[:, :], p0[:, 0:TH], AF.Silu, bias=0.0, scale=1.0),
            reads=["p0"], writes=["ga"])
        P.op("v", lambda v, gf=gf: v.tensor_mul(
            gf[:, :], ga_tmp[:, :], p1[:, 0:TH]),
            reads=["ga", "p1"], writes=["A5s4"])
        if use_comb:
            P.op("v", lambda v, gf=gf: v.tensor_mul(
                gf[:, :], gf[:, :], s_crep[:, :]),
                reads=["A5s4", "s_crep"], writes=["A5s4"])
        P.op("s", lambda s, mm=mm, gf=gf: s.activation(
            bg[:, 0, mm, :], gf[:, :], AF.Copy, bias=0.0, scale=1.0),
            reads=["A5s4"], writes=[f"bg{mm}h"])
        P.op("v", lambda v, mm=mm, gf=gf: v.tensor_sub(
            bg[:, 1, mm, :], gf[:, :], bg[:, 0, mm, :]),
            reads=["A5s4", f"bg{mm}h"], writes=[f"bg{mm}l"])
        return nxt

    def down(wd_fn, first, ce=None, nxt_gw=None):
        pre = wsplit(wd_fn(0), MH, 0, cast_eng=ce)
        nxt = None
        for k in range(KD):
            bank, bkey = PS[2 + k % 2], PK[2 + k % 2]
            whi, wlo, wkh, wkl = pre
            gkeys = [f"bg{kk}{hl}" for kk in range(MH) for hl in "hl"]
            mm3(bank, bkey, whi, wlo, wkh, wkl,
                lambda kk: bg[:, 0, kk, :], lambda kk: bg[:, 1, kk, :],
                gkeys, MH, True, True)
            if k + 1 < KD:
                pre = wsplit(wd_fn(k + 1), MH, (k + 1) % 2, cast_eng=ce)
            if k == KD - 1 and nxt_gw is not None:
                nxt = wsplit(nxt_gw, KD, 0, cast_eng=ce)
            if first:
                P.op("v", lambda v, k=k, bank=bank: v.tensor_copy(
                    racc[:, k, :], bank[:, 0:TH]),
                    reads=[bkey], writes=[f"racc{k}"])
            else:
                P.op("v", lambda v, k=k, bank=bank: v.tensor_add(
                    racc[:, k, :], racc[:, k, :], bank[:, 0:TH]),
                    reads=[bkey, f"racc{k}"], writes=[f"racc{k}"])
        return nxt

    def emit_moe(th, tick, ce_gu=None, ce_dn=None):
        ta, tb = th * TH, (th + 1) * TH
        pre = wsplit(w_sg[0], KD, 0, cast_eng=ce_gu)
        for hh in range(2):
            for mm in range(MH):
                m = hh * MH + mm
                nxt = w_sg[m + 1] if mm + 1 < MH else None
                pre = gate_up(pre, w_su[m], nxt, mm, use_comb=False, ce=ce_gu)
                tick(1)
            nseg = w_sg[MH] if hh == 0 else w_mg[0][0]
            pre = down(lambda k, hh=hh: w_sd[k][:, hh * MH:(hh + 1) * MH, :],
                       first=(hh == 0), ce=ce_dn, nxt_gw=nseg)
            tick(4)
        def eshuf(e):
            P.op("v", lambda v, e=e: v.stream_shuffle(
                s_stage2[:, 4 + e % 4, :], s_comb[:, :], [e] + [0] * 31),
                reads=["s_comb"], writes=[f"st2e{e % 4}"])

        for e in range(4):
            eshuf(e)
        for e in range(NE):
            P.op("pe", lambda pe, e=e: pe.matmul(
                p4[:, 0:TH], c_ones1[:, :], s_stage2[0:1, 4 + e % 4, :],
                start=True, stop=True),
                reads=[f"st2e{e % 4}", "c_ones1"], writes=["p4"])
            if e + 4 < NE:
                eshuf(e + 4)
            P.op("v", lambda v: v.tensor_copy(s_crep[:, :], p4[:, 0:TH]),
                reads=["p4"], writes=["s_crep"])
            for mm in range(MH):
                nxt = w_mg[e][mm + 1] if mm + 1 < MH else None
                pre = gate_up(pre, w_mu[e][mm], nxt, mm, use_comb=True,
                              ce=ce_gu)
                tick(1)
            pre = down(lambda k, e=e: w_md[e][k], first=False, ce=ce_dn,
                       nxt_gw=w_mg[e + 1][0] if e + 1 < NE else None)
            tick(4)
        if "t_ffn" in tp:
            for k in range(KD):
                st(tp["t_ffn"][:, k, ta:tb], racc[:, k, :], [],
                   rkeys=[f"racc{k}"])

    def emit_ln4(th):
        ta, tb = th * TH, (th + 1) * TH

        def x2_load(k, ta=ta, tb=tb):
            sl, key = A5(2 + k % 2), A5k(2 + k % 2)
            ld(sl[:, :], hb_x2[k][:, ta:tb], [key], rkeys=[f"hbx2{k}"])
            return sl[:, :], key

        def out_sink(k, sc, sck, rap, rkey, ta=ta, tb=tb):
            ostg, okey = A5(6 + k % 2), A5k(6 + k % 2)
            P.op("v", lambda v, sc=sc, rap=rap, ostg=ostg: v.tensor_add(
                ostg[:, :], sc[:, 0:TH], rap),
                reads=[sck, rkey], writes=[okey])
            st(out[:, k, ta:tb], ostg[:, :], ["out_hbm"], rkeys=[okey])

        emit_ln(racc, KD, TH, None, "racc", "unused",
                scr=lambda k: A5(k % 2), scrk=lambda k: A5k(k % 2),
                res_load=x2_load, dst_sink=out_sink)

    FXC = [A5(8), A5(9), A5(10), A5(11)]
    PRT = [A5(6), A5(7)]

    def s_pref(ng, it):
        sl, key = A5(it % 2), f"fxs{it % 2}"
        P.op("s", lambda s, it=it, sl=sl: s.dma_start(
            out=sl[:, :], in_=hb_S[it][:, TH:2 * TH]),
            reads=[f"hbS{it}"], writes=[key], sdma=True)

    def emit_shuf(ng):
        for j in range(4):
            n = 4 * ng + j
            P.op("v", lambda v, n=n, j=j: v.stream_shuffle(
                s_stage2[:, j, :], s_bc[:, TH:2 * TH], [NS + n] + [0] * 31),
                reads=["s_bc"], writes=[f"st2{j}"])

    def mk_crep1(ng):
        def blk():
            if ng > 0:
                cce_add(IT - 1)
            for j in range(4):
                P.op("pe", lambda pe, j=j: pe.matmul(
                    p4[:, 0:TH], c_ones1[:, :], s_stage2[0:1, j, :],
                    start=True, stop=True),
                    reads=[f"st2{j}", "c_ones1"], writes=["p4"])
                P.op("v", lambda v, j=j: v.tensor_copy(
                    FXC[j][:, :], p4[:, 0:TH]),
                    reads=["p4"], writes=[f"fxc{j}"])
            s_pref(ng, 0)
        return blk

    def cce_add(it):
        P.op("g", lambda g, it=it: g.dma_start(
            out=hb_yloc[it][:, TH:2 * TH], in_=PRT[it % 2][:, :],
            accum_op=ALU.add),
            reads=[f"fxp{it % 2}"], writes=[f"hbY{it}"], gdma=True)

    def mk_fix1(ng, it):
        def blk():
            if it > 0:
                cce_add(it - 1)
            if it + 1 < IT:
                s_pref(ng, it + 1)
            if it == IT // 2 and ng + 1 < 4:
                emit_shuf(ng + 1)
            kss = f"fxs{it % 2}"
            sS = A5(it % 2)
            kp = f"fxp{it % 2}"
            for j in range(4):
                n = 4 * ng + j
                et, ket = (A5(5), "fxe0") if j % 2 == 0 else (A5(2), "fxe1")
                P.op("s", lambda s, n=n, sS=sS, et=et: s.activation(
                    et[:, :], sS[:, :], AF.Exp, bias=0.0,
                    scale=c_A[:, it, n:n + 1]),
                    reads=[kss, "c_A"], writes=[ket])
                P.op("v", lambda v, j=j, et=et: v.tensor_mul(
                    et[:, :], et[:, :], FXC[j][:, :]),
                    reads=[ket, f"fxc{j}"], writes=[ket])
                src2 = c_zeros if j == 0 else PRT[it % 2]
                src2k = "c_zeros" if j == 0 else kp
                P.op("v", lambda v, n=n, et=et, src2=src2, it=it:
                     v.scalar_tensor_tensor(
                     PRT[it % 2][:, :], et[:, :], s_hin[:, it, n:n + 1],
                     src2[:, 0:TH], ALU.mult, ALU.add),
                     reads=[ket, "s_hin", src2k], writes=[kp])
        return blk

    def mk_flush():
        def blk():
            cce_add(IT - 1)
        return blk

    def mk_yfin1(it):
        def blk():
            yv, kyv = A5(it % 2), f"fxy{it % 2}"
            uv, kuv = A5(2 + it % 2), f"fxu{it % 2}"
            gv, kgv = A5(6 + it % 2), f"fxg{it % 2}"
            P.op("s", lambda s, it=it, yv=yv: s.dma_start(
                out=yv[:, :], in_=hb_yloc[it][:, TH:2 * TH]),
                reads=[f"hbY{it}"], writes=[kyv, f"fxs{it % 2}"], sdma=True)
            P.op("s", lambda s, it=it, uv=uv: s.dma_start(
                out=uv[:, :], in_=hb_u[it][:, TH:2 * TH]),
                reads=[f"hbu{it}"],
                writes=[kuv, "fxe1" if it % 2 == 0 else "A5s3f"], sdma=True)
            P.op("s", lambda s, it=it, gv=gv: s.dma_start(
                out=gv[:, :], in_=hb_gate[it][:, TH:2 * TH]),
                reads=[f"hbg{it}"], writes=[kgv, f"fxp{it % 2}"], sdma=True)
            P.op("v", lambda v, it=it, uv=uv, yv=yv: v.scalar_tensor_tensor(
                yv[:, :], uv[:, :], c_D[:, it:it + 1], yv[:, :],
                ALU.mult, ALU.add),
                reads=[kuv, "c_D", kyv], writes=[kyv])
            P.op("v", lambda v, yv=yv, gv=gv: v.tensor_mul(
                yv[:, :], yv[:, :], gv[:, :]),
                reads=[kgv, kyv], writes=[kyv])
            if "t_y" in tp:
                P.op("s", lambda s, it=it, yv=yv: s.dma_start(
                    out=tp["t_y"][it][:, TH:2 * TH], in_=yv[:, :]),
                    reads=[kyv], writes=[], sdma=True)
            P.op("s", lambda s, yv=yv: s.activation(
                ybsp[:, 0, :], yv[:, :], AF.Copy, bias=0.0, scale=1.0),
                reads=[kyv], writes=["ybsph", "fxe0"])
            P.op("v", lambda v, yv=yv: v.tensor_sub(
                ybsp[:, 1, :], yv[:, :], ybsp[:, 0, :]),
                reads=[kyv, "ybsph"], writes=["ybspl"])
            P.op("s", lambda s, it=it: s.dma_start(
                out=hb_yb[it], in_=ybsp[:, :, :].rearrange("p a w -> p (a w)")),
                reads=["ybsph", "ybspl"], writes=[f"hbyb{it}"], sdma=True)
        return blk

    fx_blocks = []
    for ng in range(4):
        if ng > 0:
            fx_blocks.append(mk_crep1(ng))
        for it in range(IT):
            fx_blocks.append(mk_fix1(ng, it))
    fx_blocks.append(mk_flush())
    for it in range(IT):
        fx_blocks.append(mk_yfin1(it))

    def tick(k):
        for _ in range(k):
            if fx_blocks:
                fx_blocks.pop(0)()

    emit_crep5_th0()
    emit_fix_th0()
    emit_ysplit_th0()
    ykeys0 = [f"yh{i}" for i in range(IT)] + ["yloK"]
    emit_outproj(lambda it: YHI[it][:, :], lambda it: ylo[:, it, :], ykeys0,
                 mo, "mo")
    if "t_mo" in tp:
        for k in range(KD):
            st(tp["t_mo"][:, k, 0:TH], mo[:, k, :], [], rkeys=["mo"])
    P.fence()
    emit_ln23(0, mo, "mo")
    emit_gating(0)
    emit_shuf(0)
    mk_crep1(0)()
    emit_moe(0, tick, ce_gu="s", ce_dn="v")
    while fx_blocks:
        fx_blocks.pop(0)()
    P.fence()
    ystep = max(1, IT // 4)
    for qq in range(IT // ystep):
        ld(ybar[:, ystep * qq:ystep * (qq + 1), :],
           hb_yb[ystep * qq:ystep * (qq + 1)].rearrange("i p w -> p i w"),
           ["ybar"], rkeys=[f"hbyb{i}" for i in range(IT)])
    emit_outproj(lambda it: ybar[:, it, 0:TH], lambda it: ybar[:, it, TH:2 * TH],
                 ["ybar"], mo2, "mo2")
    if "t_mo" in tp:
        for k in range(KD):
            st(tp["t_mo"][:, k, TH:2 * TH], mo2[:, k, :], [], rkeys=["mo2"])
    emit_ln4(0)
    P.fence()

    emit_ln23(1, mo2, "mo2")
    emit_gating(1)
    emit_moe(1, lambda k: None)
    P.fence()
    emit_ln4(1)
    P.fence()

    cnt = P.prepare()
    with nc.Block() as block:
        @block.gpsimd
        def _(g):
            for e in Prog.ENGINES:
                g.sem_clear(sems[e])
            for qs in qsems:
                g.sem_clear(qs)
            for qs in gqsems:
                g.sem_clear(qs)
            for qs in sqsems:
                g.sem_clear(qs)
            g.sem_clear(sem_boot)
            g.sem_inc(sem_boot, 1)
            P.emit_engine("g", g, sems, qsems)

        @block.tensor
        def _(pe):
            pe.wait_ge(sem_boot, 1)
            P.emit_engine("pe", pe, sems, qsems)

        @block.vector
        def _(v):
            v.wait_ge(sem_boot, 1)
            P.emit_engine("v", v, sems, qsems)

        @block.scalar
        def _(s):
            s.wait_ge(sem_boot, 1)
            P.emit_engine("s", s, sems, qsems)

        @block.sync
        def _(q):
            q.wait_ge(sem_boot, 1)
            P.emit_engine("q", q, sems, qsems)

    ctx.close()
    return nc


def _prep_inputs(inputs):
    f = lambda k: np.ascontiguousarray(np.asarray(inputs[k], np.float32))
    x = f("L0_attn_x")[0]
    common = {
        "w_inproj": _tile_w(np.ascontiguousarray(f("L0_attn_in_proj_w").T)),
        "convw": _ct2(f("L0_attn_conv_w")[:, 0, :]),
        "convb": _ct1(f("L0_attn_conv_b")),
        "w_xproj": _ct2(np.ascontiguousarray(f("L0_attn_x_proj_w").T)),
        "w_dtproj": np.ascontiguousarray(
            f("L0_attn_dt_proj_w").reshape(IT, 128, 128).transpose(2, 0, 1)),
        "dtb": _ct1(f("L0_attn_dt_proj_b")),
        "alog": _ct2(f("L0_attn_A_log")),
        "dvec": _ct1(f("L0_attn_D")),
        "w_outproj": _tile_w(np.ascontiguousarray(f("L0_attn_out_proj_w").T)),
        "gatew": _ct2(f("L0_ffn_gate_w")),
        "w_mg": np.ascontiguousarray(np.stack(
            [_tile_w(f("L0_ffn_w_gate")[e]) for e in range(NE)])),
        "w_mu": np.ascontiguousarray(np.stack(
            [_tile_w(f("L0_ffn_w_up")[e]) for e in range(NE)])),
        "w_md": np.ascontiguousarray(np.stack(
            [_tile_w(f("L0_ffn_w_down")[e]) for e in range(NE)])),
        "w_sg": _tile_w(f("L0_ffn_sw_gate")),
        "w_su": _tile_w(f("L0_ffn_sw_up")),
        "w_sd": _tile_w(f("L0_ffn_sw_down")),
        "eye": np.eye(128, dtype=np.float32),
    }
    dm = KD * 128
    xpad = np.concatenate([np.zeros((3, dm), np.float32), x], axis=0)
    in_maps = []
    for c in range(NCORES):
        xs = xpad[T * c:T * c + TP]
        xs = np.ascontiguousarray(xs.reshape(TP, KD, 128).transpose(2, 1, 0))
        csel = np.zeros((128, NCORES), np.float32)
        if c > 0:
            csel[:, c - 1] = 1.0
        m = dict(common)
        m["x_in"] = xs
        m["csel"] = csel
        in_maps.append(m)
    return in_maps


def kernel(**inputs):
    in_maps = _prep_inputs(inputs)
    nc = build_program()
    res = run_bass_kernel_spmd(nc, in_maps, core_ids=list(range(NCORES)))
    full = np.empty((1, NCORES * T, KD * 128), np.float32)
    for c in range(NCORES):
        o = np.asarray(res.results[c]["out"])
        full[0, T * c:T * (c + 1), :] = o.transpose(2, 1, 0).reshape(T, KD * 128)
    return full


# version 18 (final)
# speedup vs baseline: 1.0220x; 1.0074x over previous
import numpy as np
import concourse.bass as bass
import concourse.mybir as mybir
from concourse.bass_utils import run_bass_kernel_spmd

F32 = mybir.dt.float32
BF16 = mybir.dt.bfloat16
AF = mybir.ActivationFunctionType
ALU = mybir.AluOpType
AX = mybir.AxisListType

NCORES = 8
T = 1024
TP = T + 3
TH = 512
KD = 16
IT = 32
NS = 16
MH = 11
HSC = 22
NE = 8
EPS = 1e-5
INV_D = 1.0 / 2048.0


class Prog:

    ENGINES = ("pe", "v", "s", "g", "q")
    SKIP_SELF = {"pe"}
    NQ = 8

    def __init__(self):
        self.ops = []
        self.last_w = {}
        self.readers = {}
        self.last_op = {e: None for e in self.ENGINES}
        self.qlane = {}
        self.lane_last = [None] * self.NQ
        self.lane_cnt = [0] * self.NQ
        self.glane = {}
        self.glane_last = [None] * 2
        self.glane_cnt = [0, 0]
        self.slane = {}
        self.slane_last = [None] * 2
        self.slane_cnt = [0, 0]
        self.is_fence = set()

    def op(self, eng, fn, reads=(), writes=(), fence_deps=None, fence=False,
           gdma=False, sdma=False):
        idx = len(self.ops)
        deps = set()

        def need(i2, force=False):
            e2 = self.ops[i2][0]
            if e2 == eng and eng in self.SKIP_SELF and not force:
                return
            deps.add(i2)

        if fence_deps:
            pass
        self.ops.append((eng, None, None))
        if fence_deps:
            for i2 in fence_deps:
                if i2 is not None:
                    need(i2, force=True)
        for r in reads:
            if r in self.last_w:
                need(self.last_w[r])
        for w in writes:
            if w in self.last_w:
                need(self.last_w[w])
            for i2 in self.readers.get(w, ()):
                need(i2)
        self.ops[idx] = (eng, fn, deps)
        for r in reads:
            self.readers.setdefault(r, []).append(idx)
        for w in writes:
            self.last_w[w] = idx
            self.readers[w] = []
        self.last_op[eng] = idx
        if eng == "q" and not fence:
            lane = sum(self.lane_cnt) % self.NQ
            self.lane_cnt[lane] += 1
            self.qlane[idx] = (lane, self.lane_cnt[lane])
            self.lane_last[lane] = idx
        if gdma:
            lane = sum(self.glane_cnt) % 2
            self.glane_cnt[lane] += 1
            self.glane[idx] = (lane, self.glane_cnt[lane])
            self.glane_last[lane] = idx
        if sdma:
            lane = sum(self.slane_cnt) % 2
            self.slane_cnt[lane] += 1
            self.slane[idx] = (lane, self.slane_cnt[lane])
            self.slane_last[lane] = idx
        if fence:
            self.is_fence.add(idx)
        return idx

    def fence(self):
        snap = [self.last_op[e] for e in self.ENGINES if e != "q"]
        snap += [i for i in self.lane_last if i is not None]
        snap += [i for i in self.glane_last if i is not None]
        snap += [i for i in self.slane_last if i is not None]
        for e in self.ENGINES:
            self.op(e, lambda E: None, fence_deps=snap, fence=True)
        self.last_w.clear()
        self.readers.clear()

    def prepare(self):
        needed = set()
        for _, _, deps in self.ops:
            needed.update(deps)
        self.eng_ops = {e: [] for e in self.ENGINES}
        for idx, (eng, _, _) in enumerate(self.ops):
            self.eng_ops[eng].append(idx)
        cnt = {e: 0 for e in self.ENGINES}
        self.inc_val = [0] * len(self.ops)
        self.has_inc = [False] * len(self.ops)
        for idx, (eng, _, _) in enumerate(self.ops):
            if idx in self.glane:
                self.has_inc[idx] = True
                self.inc_val[idx] = 16 * self.glane[idx][1]
            elif idx in self.slane:
                self.has_inc[idx] = True
                self.inc_val[idx] = 16 * self.slane[idx][1]
            elif eng == "q":
                if idx in self.qlane:
                    self.has_inc[idx] = True
                    self.inc_val[idx] = 16 * self.qlane[idx][1]
                    cnt["q"] += 16
            elif idx in needed:
                cnt[eng] += 1
                self.has_inc[idx] = True
                self.inc_val[idx] = cnt[eng]
            else:
                self.inc_val[idx] = cnt[eng]
        run = {e: 0 for e in self.ENGINES}
        for idx, (eng, _, _) in enumerate(self.ops):
            if eng != "q":
                if self.has_inc[idx]:
                    if idx not in self.glane and idx not in self.slane:
                        run[eng] = self.inc_val[idx]
                else:
                    self.inc_val[idx] = run[eng]
        self._know = {}
        return cnt

    def _sem_of(self, idx, sems, qsems):
        eng = self.ops[idx][0]
        if idx in self.glane:
            lane, n = self.glane[idx]
            return self.gqsems[lane], 16 * n, ("gq", lane)
        if idx in self.slane:
            lane, n = self.slane[idx]
            return self.sqsems[lane], 16 * n, ("sq", lane)
        if eng == "q":
            lane, n = self.qlane[idx]
            return qsems[lane], 16 * n, ("q", lane)
        return sems[eng], self.inc_val[idx], (eng,)

    def emit_engine(self, e, eobj, sems, qsems):
        know = self._know.setdefault(e, {})
        for idx in self.eng_ops[e]:
            _, fn, deps = self.ops[idx]
            for i2 in sorted(deps):
                if (self.ops[i2][0] == "q" and i2 not in self.qlane):
                    continue
                sem, v, key = self._sem_of(i2, sems, qsems)
                if v > know.get(key, 0):
                    eobj.wait_ge(sem, v)
                    know[key] = v
            if e == "q" and idx in self.qlane:
                lane, n = self.qlane[idx]
                if n > 1:
                    v = 16 * (n - 1)
                    key = ("q", lane)
                    if v > know.get(key, 0):
                        eobj.wait_ge(qsems[lane], v)
                        know[key] = v
            if idx in self.glane:
                lane, n = self.glane[idx]
                if n > 1:
                    v = 16 * (n - 1)
                    key = ("gq", lane)
                    if v > know.get(key, 0):
                        eobj.wait_ge(self.gqsems[lane], v)
                        know[key] = v
            if idx in self.slane:
                lane, n = self.slane[idx]
                if n > 1:
                    v = 16 * (n - 1)
                    key = ("sq", lane)
                    if v > know.get(key, 0):
                        eobj.wait_ge(self.sqsems[lane], v)
                        know[key] = v
            inst = fn(eobj)
            if idx in self.glane:
                lane, n = self.glane[idx]
                inst.then_inc(self.gqsems[lane], 16)
            elif idx in self.slane:
                lane, n = self.slane[idx]
                inst.then_inc(self.sqsems[lane], 16)
            elif self.has_inc[idx] and e != "q":
                if inst is None:
                    eobj.sem_inc(sems[e], 1)
                else:
                    inst.then_inc(sems[e], 1)
            elif e == "q" and idx in self.qlane:
                lane, n = self.qlane[idx]
                inst.then_inc(qsems[lane], 16)


def _tile_w(w):
    K, M = w.shape
    return np.ascontiguousarray(
        w.reshape(K // 128, 128, M // 128, 128).transpose(2, 1, 0, 3))


def _ct1(v):
    return np.ascontiguousarray(np.asarray(v, np.float32).reshape(-1, 128).T)


def _ct2(v):
    v = np.asarray(v, np.float32)
    C, W = v.shape
    return np.ascontiguousarray(v.reshape(C // 128, 128, W).transpose(1, 0, 2))


def build_program(taps=()):
    nc = bass.Bass("TRN2", target_bir_lowering=False)
    inv_d = 1.0 / (KD * 128)
    tap_set = set(taps)

    dI = lambda nm, shp: nc.dram_tensor(nm, shp, F32, kind="ExternalInput").ap()
    dO = lambda nm, shp: nc.dram_tensor(nm, shp, F32, kind="ExternalOutput").ap()
    dS = lambda nm, shp: nc.dram_tensor(nm, shp, F32).ap()

    x_in = dI("x_in", [128, KD, TP])
    w_inproj = dI("w_inproj", [2 * IT, 128, KD, 128])
    convw_i = dI("convw", [128, IT, 4])
    convb_i = dI("convb", [128, IT])
    xproj_i = dI("w_xproj", [128, IT, 160])
    dtproj_i = dI("w_dtproj", [128, IT, 128])
    dtb_i = dI("dtb", [128, IT])
    alog_i = dI("alog", [128, IT, NS])
    dvec_i = dI("dvec", [128, IT])
    w_outproj = dI("w_outproj", [KD, 128, IT, 128])
    gatew_i = dI("gatew", [128, KD, NE])
    w_mg = dI("w_mg", [NE, MH, 128, KD, 128])
    w_mu = dI("w_mu", [NE, MH, 128, KD, 128])
    w_md = dI("w_md", [NE, KD, 128, MH, 128])
    w_sg = dI("w_sg", [HSC, 128, KD, 128])
    w_su = dI("w_su", [HSC, 128, KD, 128])
    w_sd = dI("w_sd", [KD, 128, HSC, 128])
    csel_i = dI("csel", [128, NCORES])
    eye_i = dI("eye", [128, 128])

    out = dO("out", [128, KD, T])
    tp = {}
    for name, shp in (("t_ln1", [128, KD, TP]), ("t_u", [IT, 128, T]),
                      ("t_ssm", [160, T]), ("t_yloc", [IT, 128, T]),
                      ("t_hin", [128, IT, NS]), ("t_y", [IT, 128, T]),
                      ("t_mo", [128, KD, T]), ("t_x2", [128, KD, T]),
                      ("t_comb", [NE, T]), ("t_ffn", [128, KD, T]),
                      ("t_dt", [IT, 128, T]), ("t_S", [IT, 128, T]),
                      ("t_ln3", [128, KD, T])):
        if name in tap_set:
            tp[name] = dO(name, shp)

    hb_u = dS("hb_u", [IT, 128, T])
    hb_gate = dS("hb_gate", [IT, 128, T])
    hb_S = dS("hb_S", [IT, 128, T])
    hb_yloc = dS("hb_yloc", [IT, 128, T])
    hb_x2 = dS("hb_x2", [KD, 128, T])
    hb_yb = nc.dram_tensor("hb_yb", [IT, 128, 2 * TH], BF16).ap()
    cc_in = dS("cc_in", [IT * 128, 2 * NS])
    cc_out = dS("cc_out", [NCORES * IT * 128, 2 * NS])

    from contextlib import ExitStack
    ctx = ExitStack()
    sb = lambda nm, shp: ctx.enter_context(nc.sbuf_tensor(nm, shp, F32))
    pst = lambda nm, shp: ctx.enter_context(nc.psum_tensor(nm, shp, F32))

    AR1 = sb("AR1", [128, max(KD * TP, NS * TH + IT * 128, IT * TH,
                              KD * TH + 8 * TH)])
    WSL = max(KD, IT // 2, MH) * 128
    AR2 = sb("AR2", [128, 3, WSL])
    AR3 = sb("AR3", [128, max(IT * 160, NS * TH, (MH + 2) * TH)])
    AR4 = sb("AR4", [128, max(2 * TP + 2 * TH, NCORES * IT * NS, KD * TH, 16 * TH)])
    AR5 = sb("AR5", [128, max(12 * TH, 2 * IT * 2 * NS)])
    c_convw = sb("c_convw", [128, IT, 4])
    c_convb = sb("c_convb", [128, IT])
    c_dtb = sb("c_dtb", [128, IT])
    c_D = sb("c_D", [128, IT])
    c_A = sb("c_A", [128, IT, NS])
    c_gatew = sb("c_gatew", [128, KD, NE])
    c_csel = sb("c_csel", [128, NCORES])
    c_eye = sb("c_eye", [128, 128])
    c_ones1 = sb("c_ones1", [1, 128])
    c_onesk = sb("c_onesk", [128, 1])
    c_zeros = sb("c_zeros", [128, TH])
    c_eps = sb("c_eps", [1, 1])
    c_neghalf = sb("c_neghalf", [1, 1])
    c_negone = sb("c_negone", [128, 1])
    s_ts = sb("s_ts", [128, T])
    s_bc = sb("s_bc", [32, T])
    s_hcar = sb("s_hcar", [128, IT, NS])
    s_scar = sb("s_scar", [128, IT])
    s_dtsum = sb("s_dtsum", [128, IT, 2])
    s_hin = sb("s_hin", [128, IT, NS])
    r_m = sb("r_m", [1, TH])
    r_v = sb("r_v", [1, TH])
    s_rs = sb("s_rs", [1, TH])
    s_rb = sb("s_rb", [1, TH])
    s_stage = sb("s_stage", [32, TH])
    s_lgf = s_stage
    s_tok = sb("s_tok", [128, TH // 128, NE])
    s_tok2 = sb("s_tok2", [128, max(TH // 128, 2), NE])
    s_trow = sb("s_trow", [128, TH // 128])
    s_comb = sb("s_comb", [32, TH])

    _a1 = nc.lookup_mloc(AR1).addr
    _a2 = nc.lookup_mloc(AR2).addr
    _a3 = nc.lookup_mloc(AR3).addr
    bln3 = nc.alloc_sbuf_tensor_at(
        "bln3", [128, 2, KD, TH], BF16, offset=_a1).ap()
    s_stage2 = nc.alloc_sbuf_tensor_at(
        "s_stage2", [32, 8, TH], F32, offset=_a1 + 4 * KD * TH).ap()
    mo2 = nc.alloc_sbuf_tensor_at(
        "mo2", [128, KD, TH], F32, offset=_a3).ap()
    bw = nc.alloc_sbuf_tensor_at(
        "bw", [128, 2, 2, WSL], BF16, offset=_a2 + WSL * 4).ap()
    bg = nc.alloc_sbuf_tensor_at(
        "bg", [128, 2, MH, TH], BF16, offset=_a3).ap()
    _a4 = nc.lookup_mloc(AR4).addr
    _a5 = nc.lookup_mloc(AR5).addr
    sxlo = nc.alloc_sbuf_tensor_at(
        "sxlo", [128, KD, TP], BF16, offset=_a1).ap()
    SXHI = []
    _scr_stride = ((TP * 2 + 31) // 32) * 32
    _n4 = (AR4.shape[1] * 4 - ((2 * TP * 4 + 31) // 32) * 32) // _scr_stride
    for k in range(KD):
        if k < _n4:
            off = _a4 + ((2 * TP * 4 + 31) // 32) * 32 + k * _scr_stride
        else:
            off = _a3 + IT * 160 * 4 + (k - _n4) * _scr_stride
        SXHI.append(nc.alloc_sbuf_tensor_at(
            f"sxhi{k}", [128, TP], BF16, offset=off).ap())
    ylo = nc.alloc_sbuf_tensor_at(
        "ylo", [128, IT, TH], BF16, offset=_a1).ap()
    ybar = nc.alloc_sbuf_tensor_at(
        "ybar", [128, IT, 2 * TH], BF16, offset=_a1).ap()
    ybsp = nc.alloc_sbuf_tensor_at(
        "ybsp", [128, 2, TH], BF16, offset=_a5 + 5 * TH * 4).ap()
    YHI = []
    for i in range(IT):
        if i < IT - 1:
            off = _a3 + i * TH * 2
        else:
            off = _a5 + 4 * TH * 4
        YHI.append(nc.alloc_sbuf_tensor_at(
            f"yhi{i}", [128, TH], BF16, offset=off).ap())
    p0 = pst("p0", [128, TH])
    p1 = pst("p1", [128, TH])
    p2 = pst("p2", [128, TH])
    p3 = pst("p3", [128, TH])
    p4 = pst("p4", [128, TH])
    p5 = pst("p5", [128, TH])
    p6 = pst("p6", [128, TH])
    PS = [p0, p1, p2, p3]
    PK = ["p0", "p1", "p2", "p3"]
    DB = [p2, p3, p5, p6]
    DK = ["p2", "p3", "p5", "p6"]
    OB = [p0, p1, p5, p6]
    OK = ["p0", "p1", "p5", "p6"]

    sems = {}
    for e in Prog.ENGINES:
        sems[e] = ctx.enter_context(nc.semaphore("sem_" + e))
    qsems = [ctx.enter_context(nc.semaphore(f"sem_q{i}")) for i in range(Prog.NQ)]
    gqsems = [ctx.enter_context(nc.semaphore(f"sem_gq{i}")) for i in range(2)]
    sqsems = [ctx.enter_context(nc.semaphore(f"sem_sq{i}")) for i in range(2)]
    sem_boot = ctx.enter_context(nc.semaphore("sem_boot"))

    P = Prog()
    P.gqsems = gqsems
    P.sqsems = sqsems

    sx = AR1[:, 0:KD * TP].rearrange("p (k t) -> p k t", k=KD)
    wsl = lambda i: AR2[:, i % 2, 0:KD * 128 if True else 0].rearrange("p (k c) -> p k c", c=128)
    wk = lambda i: f"w{i % 2}"
    xprojT = AR3[:, 0:IT * 160].rearrange("p (i j) -> p i j", j=160)
    B_rep = AR3[:, 0:NS * TH].rearrange("p (n t) -> p n t", t=TH)
    C_rep5 = AR3[:, 0:NS * TH].rearrange("p (n t) -> p n t", t=TH)
    g_ar = AR3[:, 0:MH * TH].rearrange("p (m t) -> p m t", t=TH)
    ga_tmp = AR3[:, MH * TH:(MH + 1) * TH]
    s_crep = AR3[:, (MH + 1) * TH:(MH + 2) * TH]
    C_rep4 = AR1[:, 0:NS * TH].rearrange("p (n t) -> p n t", t=TH)
    dtprojT = AR1[:, NS * TH:NS * TH + IT * 128].rearrange("p (i c) -> p i c", c=128)
    y_half = AR1[:, 0:IT * TH].rearrange("p (i t) -> p i t", t=TH)
    ln3 = AR1[:, 0:KD * TH].rearrange("p (k t) -> p k t", t=TH)
    hin = lambda i: AR4[:, (i % 2) * TP:(i % 2) * TP + TP]
    hends = AR4[:, 0:NCORES * IT * NS].rearrange("p (c i n) -> p c i n", i=IT, n=NS)
    mo = AR4[:, 0:KD * TH].rearrange("p (k t) -> p k t", t=TH)
    racc = AR4[:, 0:KD * TH].rearrange("p (k t) -> p k t", t=TH)
    A5 = lambda i: AR5[:, i * TH:(i + 1) * TH]
    A5k = lambda i: f"A5s{i}"
    s_all = AR5[:, 0:2 * IT * 2 * NS].rearrange("p (b i w) -> p b i w", i=IT, w=2 * NS)
    s_pq = AR5[:, 5 * TH:5 * TH + IT * 2 * NS].rearrange("p (i w) -> p i w", w=2 * NS)
    s_sb = AR5[:, 8 * TH:9 * TH]
    s_bb = AR5[:, 9 * TH:10 * TH]

    def ld(dst_ap, src_ap, wkeys, rkeys=()):
        return P.op("q", lambda q: q.dma_start(out=dst_ap, in_=src_ap),
                    reads=list(rkeys), writes=list(wkeys))

    st = ld

    ld(c_convw[:, :, :], convw_i, ["c_convw"])
    ld(c_convb[:, :], convb_i, ["c_convb"])
    ld(c_dtb[:, :], dtb_i, ["c_dtb"])
    ld(c_D[:, :], dvec_i, ["c_D"])
    ld(c_A[:, :, :], alog_i, ["c_A"])
    ld(c_gatew[:, :, :], gatew_i, ["c_gatew"])
    ld(c_csel[:, :], csel_i, ["c_csel"])
    ld(c_eye[:, :], eye_i, ["c_eye"])
    ld(xprojT[:, :, :], xproj_i, ["xprojT"])
    ld(sx[:, :, :], x_in[:, :, :], ["sx"])
    P.op("g", lambda g: g.memset(c_ones1[:, :], 1.0), writes=["c_ones1"])
    P.op("g", lambda g: g.memset(c_onesk[:, :], 1.0), writes=["c_onesk"])
    P.op("g", lambda g: g.memset(c_zeros[:, :], 0.0), writes=["c_zeros"])
    P.op("g", lambda g: g.memset(c_eps[:, :], EPS), writes=["c_eps"])
    P.op("g", lambda g: g.memset(c_neghalf[:, :], -0.5), writes=["c_neghalf"])
    P.op("g", lambda g: g.memset(c_negone[:, :], -1.0), writes=["c_negone"])
    P.op("g", lambda g: g.memset(s_comb[:, :], 0.0), writes=["s_comb"])
    P.op("s", lambda s: s.activation(c_A[:, :, :], c_A[:, :, :], AF.Exp,
                                     bias=0.0, scale=1.0),
         reads=["c_A"], writes=["c_A"])
    P.op("v", lambda v: v.tensor_scalar_mul(c_A[:, :, :], c_A[:, :, :], -1.0),
         reads=["c_A"], writes=["c_A"])

    def wsplit(src_ap, nk, slot, cast_eng=None):
        stg = AR2[:, 0, 0:nk * 128]
        whi = bw[:, slot, 0, 0:nk * 128]
        wlo = bw[:, slot, 1, 0:nk * 128]
        if cast_eng is None:
            P.op("g", lambda g, whi=whi, src_ap=src_ap: g.dma_start(
                out=whi.rearrange("p (k c) -> p k c", c=128), in_=src_ap),
                reads=[], writes=[f"bw{slot}h"], gdma=True)
            ld(stg, src_ap, ["wstg"])
        else:
            ld(stg, src_ap, ["wstg"])
            if cast_eng == "s":
                P.op("s", lambda s, whi=whi, stg=stg: s.activation(
                    whi, stg, AF.Copy, bias=0.0, scale=1.0),
                    reads=["wstg"], writes=[f"bw{slot}h"])
            else:
                P.op("v", lambda v, whi=whi, stg=stg: v.tensor_copy(whi, stg),
                     reads=["wstg"], writes=[f"bw{slot}h"])
        P.op("v", lambda v, stg=stg, whi=whi, wlo=wlo: v.tensor_sub(
            wlo, stg, whi),
            reads=["wstg", f"bw{slot}h"], writes=[f"bw{slot}l"])
        return (bw[:, slot, 0, 0:nk * 128].rearrange("p (k c) -> p k c", c=128),
                bw[:, slot, 1, 0:nk * 128].rearrange("p (k c) -> p k c", c=128),
                [f"bw{slot}h"], [f"bw{slot}l"])

    def mm3(bank, bkey, whi, wlo, wkh, wkl, rh_hi, rh_lo, rkeys, nk,
            first, last, w=TH):
        for k in range(nk):
            P.op("pe", lambda pe, k=k, bank=bank, whi=whi, rh_hi=rh_hi, w=w:
                 pe.matmul(bank[:, 0:w], whi[:, k, :], rh_hi(k),
                           start=(first and k == 0), stop=False),
                 reads=wkh + rkeys, writes=[bkey])
            P.op("pe", lambda pe, k=k, bank=bank, whi=whi, rh_lo=rh_lo, w=w:
                 pe.matmul(bank[:, 0:w], whi[:, k, :], rh_lo(k),
                           start=False, stop=False),
                 reads=wkh + rkeys, writes=[bkey])
        for k in range(nk):
            P.op("pe", lambda pe, k=k, bank=bank, wlo=wlo, rh_hi=rh_hi, w=w:
                 pe.matmul(bank[:, 0:w], wlo[:, k, :], rh_hi(k),
                           start=False, stop=(last and k == nk - 1)),
                 reads=wkl + rkeys, writes=[bkey])

    def emit_ln(src3, nk, tlen, dst3, src_reg, dst_reg, scr, scrk,
                res_load=None, dst_sink=None):
        slices = [(a, min(a + TH, tlen)) for a in range(0, tlen, TH)]
        for (a, b) in slices:
            w = b - a
            for k in range(nk):
                P.op("pe", lambda pe, k=k, a=a, b=b, w=w: pe.matmul(
                    p2[0:1, 0:w], c_onesk[:, :], src3[:, k, a:b],
                    start=(k == 0), stop=(k == nk - 1)),
                    reads=[src_reg, "c_onesk"], writes=["p2"])
            for k in range(nk):
                sc, sck = scr(k), scrk(k)
                P.op("s", lambda s, k=k, a=a, w=w, sc=sc: s.activation(
                    sc[:, 0:w], src3[:, k, a:a + w], AF.Square, bias=0.0,
                    scale=1.0),
                    reads=[src_reg], writes=[sck])
                P.op("pe", lambda pe, k=k, w=w, sc=sc: pe.matmul(
                    p3[0:1, 0:w], c_onesk[:, :], sc[:, 0:w],
                    start=(k == 0), stop=(k == nk - 1)),
                    reads=[sck, "c_onesk"], writes=["p3"])
            P.op("v", lambda v, w=w: v.tensor_scalar_mul(
                r_m[0:1, 0:w], p2[0:1, 0:w], inv_d),
                reads=["p2"], writes=["r_m"])
            P.op("v", lambda v, w=w: v.tensor_scalar_mul(
                r_v[0:1, 0:w], p3[0:1, 0:w], inv_d),
                reads=["p3"], writes=["r_v"])
            P.op("v", lambda v, w=w: v.tensor_mul(
                s_rs[0:1, 0:w], r_m[0:1, 0:w], r_m[0:1, 0:w]),
                reads=["r_m"], writes=["s_rs"])
            P.op("v", lambda v, w=w: v.tensor_sub(
                r_v[0:1, 0:w], r_v[0:1, 0:w], s_rs[0:1, 0:w]),
                reads=["r_v", "s_rs"], writes=["r_v"])
            P.op("s", lambda s, w=w: s.activation(
                r_v[0:1, 0:w], r_v[0:1, 0:w], AF.Ln,
                bias=c_eps[0:1, 0:1], scale=1.0),
                reads=["r_v", "c_eps"], writes=["r_v"])
            P.op("s", lambda s, w=w: s.activation(
                s_rs[0:1, 0:w], r_v[0:1, 0:w], AF.Exp, bias=0.0,
                scale=c_neghalf[0:1, 0:1]),
                reads=["r_v", "c_neghalf"], writes=["s_rs"])
            P.op("v", lambda v, w=w: v.tensor_mul(
                s_rb[0:1, 0:w], r_m[0:1, 0:w], s_rs[0:1, 0:w]),
                reads=["r_m", "s_rs"], writes=["s_rb"])
            P.op("v", lambda v, a=a, w=w: v.tensor_scalar_mul(
                s_rb[0:1, 0:w], s_rb[0:1, 0:w], -1.0),
                reads=["s_rb"], writes=["s_rb"])
            P.op("pe", lambda pe, a=a, w=w: pe.matmul(
                p2[:, 0:w], c_ones1[:, :], s_rs[0:1, 0:w],
                start=True, stop=True),
                reads=["s_rs", "c_ones1"], writes=["p2"])
            P.op("v", lambda v, a=a, w=w: v.tensor_copy(
                s_sb[:, 0:w], p2[:, 0:w]),
                reads=["p2"], writes=["s_sb"])
            P.op("pe", lambda pe, a=a, w=w: pe.matmul(
                p3[:, 0:w], c_ones1[:, :], s_rb[0:1, 0:w],
                start=True, stop=True),
                reads=["s_rb", "c_ones1"], writes=["p3"])
            P.op("v", lambda v, a=a, w=w: v.tensor_copy(
                s_bb[:, 0:w], p3[:, 0:w]),
                reads=["p3"], writes=["s_bb"])
            for k in range(nk):
                sc, sck = scr(k), scrk(k)
                P.op("v", lambda v, k=k, a=a, w=w, sc=sc: v.tensor_mul(
                    sc[:, 0:w], src3[:, k, a:a + w], s_sb[:, 0:w]),
                    reads=[src_reg, "s_sb"], writes=[sck])
                if res_load is None and dst_sink is None:
                    P.op("v", lambda v, k=k, a=a, w=w, sc=sc: v.tensor_add(
                        dst3[:, k, a:a + w], sc[:, 0:w], s_bb[:, 0:w]),
                        reads=[sck, "s_bb"], writes=[dst_reg])
                else:
                    P.op("v", lambda v, k=k, w=w, sc=sc: v.tensor_add(
                        sc[:, 0:w], sc[:, 0:w], s_bb[:, 0:w]),
                        reads=[sck, "s_bb"], writes=[sck])
                    if res_load is not None:
                        rap, rkey = res_load(k)
                        if dst_sink is None:
                            P.op("v", lambda v, k=k, a=a, w=w, sc=sc, rap=rap:
                                 v.tensor_add(dst3[:, k, a:a + w], sc[:, 0:w],
                                              rap),
                                 reads=[sck, rkey], writes=[dst_reg])
                        else:
                            dst_sink(k, sc, sck, rap, rkey)
                    else:
                        dst_sink(k, sc, sck, None, None)

    emit_ln(sx, KD, TP, sx, "sx", "sx",
            scr=lambda k: hin(k), scrk=lambda k: f"hin{k % 2}")
    if "t_ln1" in tp:
        st(tp["t_ln1"][:, :, :], sx[:, :, :], [], rkeys=["sx"])

    for k in range(KD):
        P.op("s", lambda s, k=k: s.activation(
            SXHI[k][:, :], sx[:, k, :], AF.Copy, bias=0.0, scale=1.0),
            reads=["sx"], writes=[f"sxh{k}"])
    for k in range(KD):
        P.op("v", lambda v, k=k: v.tensor_sub(
            sxlo[:, k, :], sx[:, k, :], SXHI[k][:, :]),
            reads=["sx", f"sxh{k}"], writes=["sxlo", "sx"])

    NT1 = [(0, 3)] + [(3 + i * TH, 3 + (i + 1) * TH) for i in range(T // TH)]
    NT2 = NT1[1:]
    pre1 = wsplit(w_inproj[0], KD, 0)
    xp_defer = [None]
    for m in range(2 * IT):
        whi, wlo, wkh, wkl = pre1
        is_h = m < IT
        ub = AR5[:, (m % 2) * 2 * TH:((m % 2) * 2 + 2) * TH]
        uk = f"ub{m % 2}"
        gb = AR5[:, (4 + (m % 2) * 2) * TH:(4 + (m % 2) * 2 + 2) * TH]
        gk = f"gb{m % 2}"
        for ti, (a, b) in enumerate(NT1 if is_h else NT2):
            w = b - a
            bank, bkey = PS[ti % 2], PK[ti % 2]
            mm3(bank, bkey, whi, wlo, wkh, wkl,
                lambda k, a=a, b=b: SXHI[k][:, a:b],
                lambda k, a=a, b=b: sxlo[:, k, a:b],
                [f"sxh{kk}" for kk in range(KD)] + ["sxlo"], KD,
                True, True, w=w)
            if is_h:
                P.op("v", lambda v, a=a, w=w, bank=bank, m=m: v.tensor_copy(
                    hin(m)[:, a:a + w], bank[:, 0:w]),
                    reads=[bkey], writes=[f"hin{m % 2}"])
            else:
                P.op("s", lambda s, a=a, w=w, bank=bank, gb=gb: s.activation(
                    gb[:, a - 3:a - 3 + w], bank[:, 0:w], AF.Silu,
                    bias=0.0, scale=1.0),
                    reads=[bkey], writes=[gk])
        if m + 1 < 2 * IT:
            pre1 = wsplit(w_inproj[m + 1], KD, (m + 1) % 2)
        if xp_defer[0] is not None:
            xp_defer[0]()
            xp_defer[0] = None
        if is_h:
            hv, hk = hin(m), f"hin{m % 2}"
            P.op("v", lambda v, m=m, hv=hv, ub=ub: v.tensor_scalar(
                ub[:, 0:T], hv[:, 0:T], c_convw[:, m, 0:1], None, ALU.mult),
                reads=[hk, "c_convw"], writes=[uk])
            for j in range(1, 4):
                P.op("v", lambda v, m=m, j=j, hv=hv, ub=ub: v.scalar_tensor_tensor(
                    ub[:, 0:T], hv[:, j:j + T], c_convw[:, m, j:j + 1], ub[:, 0:T],
                    ALU.mult, ALU.add),
                    reads=[hk, "c_convw", uk], writes=[uk])
            P.op("s", lambda s, m=m, ub=ub: s.activation(
                ub[:, 0:T], ub[:, 0:T], AF.Silu, bias=c_convb[:, m:m + 1],
                scale=1.0),
                reads=[uk, "c_convb"], writes=[uk])

            def xproj(m=m, ub=ub, uk=uk):
                for a2 in (0, TH):
                    for Mg, (ja, jb) in enumerate(((0, 128), (128, 160))):
                        bank, bkey = PS[2 + Mg], PK[2 + Mg]
                        P.op("pe", lambda pe, m=m, a2=a2, ja=ja, jb=jb,
                             bank=bank, ub=ub: pe.matmul(
                             bank[0:jb - ja, 0:TH], xprojT[:, m, ja:jb],
                             ub[:, a2:a2 + TH], start=True, stop=True),
                             reads=["xprojT", uk], writes=[bkey])
                        dst, dk = (s_ts, "s_ts") if Mg == 0 else (s_bc, "s_bc")
                        np_ = 128 if Mg == 0 else 32
                        if m == 0:
                            P.op("v", lambda v, a2=a2, bank=bank, dst=dst,
                                 np_=np_: v.tensor_copy(dst[:, a2:a2 + TH],
                                                        bank[0:np_, 0:TH]),
                                 reads=[bkey], writes=[dk])
                        else:
                            P.op("v", lambda v, a2=a2, bank=bank, dst=dst,
                                 np_=np_: v.tensor_add(dst[:, a2:a2 + TH],
                                                       dst[:, a2:a2 + TH],
                                                       bank[0:np_, 0:TH]),
                                 reads=[bkey, dk], writes=[dk])
                st(hb_u[m], ub[:, 0:T], [f"hbu{m}"], rkeys=[uk])

            if m == IT - 1:
                xproj()
            else:
                xp_defer[0] = xproj
        else:
            st(hb_gate[m - IT], gb[:, 0:T], [f"hbg{m - IT}"], rkeys=[gk])

    if "t_u" in tp:
        st(tp["t_u"][:, :, :], hb_u[:, :, :], [], rkeys=[f"hbu{i}" for i in range(IT)])
    if "t_ssm" in tp:
        st(tp["t_ssm"][0:128, :], s_ts[:, :], [], rkeys=["s_ts"])
        st(tp["t_ssm"][128:160, :], s_bc[:, :], [], rkeys=["s_bc"])

    P.fence()

    ld(dtprojT[:, :, :], dtproj_i, ["dtprojT"])
    for th in range(2):
        ta, tb = th * TH, (th + 1) * TH
        for n in range(NS):
            P.op("v", lambda v, n=n, ta=ta, tb=tb: v.stream_shuffle(
                s_stage[:, :], s_bc[:, ta:tb], [n] + [0] * 31),
                reads=["s_bc"], writes=["s_stage"])
            P.op("pe", lambda pe, n=n: pe.matmul(
                p2[:, 0:TH], c_ones1[:, :], s_stage[0:1, 0:TH],
                start=True, stop=True),
                reads=["s_stage", "c_ones1"], writes=["p2"])
            P.op("v", lambda v, n=n: v.tensor_copy(B_rep[:, n, :], p2[:, 0:TH]),
                reads=["p2"], writes=["B_rep"])
            P.op("v", lambda v, n=n, ta=ta, tb=tb: v.stream_shuffle(
                s_stage[:, :], s_bc[:, ta:tb], [NS + n] + [0] * 31),
                reads=["s_bc"], writes=["s_stage"])
            P.op("pe", lambda pe, n=n: pe.matmul(
                p3[:, 0:TH], c_ones1[:, :], s_stage[0:1, 0:TH],
                start=True, stop=True),
                reads=["s_stage", "c_ones1"], writes=["p3"])
            P.op("s", lambda s, n=n: s.activation(
                C_rep4[:, n, :], p3[:, 0:TH], AF.Copy, bias=0.0, scale=1.0),
                reads=["p3"], writes=["C_rep4"])
        t_dtu = A5(2)
        for it in range(IT):
            t_t1, k1 = A5(3), "A5s3"
            t_t2, k2 = A5(4), "A5s4"
            t_dt = t_t1
            t_S, sSk = A5(it % 2), A5k(it % 2)
            t_ya, yak = A5(8), A5k(8)
            t_yb, ybk = A5(9), A5k(9)
            t_u, uk2 = A5(10 + it % 2), A5k(10 + it % 2)
            ld(t_u[:, :], hb_u[it][:, ta:tb], [uk2], rkeys=[f"hbu{it}"])
            bank, bkey = PS[it % 2], PK[it % 2]
            P.op("pe", lambda pe, it=it, bank=bank, ta=ta, tb=tb: pe.matmul(
                bank[:, 0:TH], dtprojT[:, it, :], s_ts[:, ta:tb],
                start=True, stop=True),
                reads=["dtprojT", "s_ts"], writes=[bkey])
            P.op("s", lambda s, it=it, bank=bank, t_t1=t_t1: s.activation(
                t_t1[:, :], bank[:, 0:TH], AF.Abs, bias=c_dtb[:, it:it + 1],
                scale=1.0),
                reads=[bkey, "c_dtb"], writes=[k1])
            P.op("s", lambda s, t_t1=t_t1, t_t2=t_t2: s.activation(
                t_t2[:, :], t_t1[:, :], AF.Exp, bias=0.0,
                scale=c_negone[:, 0:1]),
                reads=[k1, "c_negone"], writes=[k2])
            P.op("s", lambda s, t_t1=t_t1, t_t2=t_t2: s.activation(
                t_t1[:, :], t_t2[:, :], AF.Ln, bias=1.0, scale=1.0),
                reads=[k2], writes=[k1])
            P.op("s", lambda s, it=it, bank=bank, t_t2=t_t2: s.activation(
                t_t2[:, :], bank[:, 0:TH], AF.Relu, bias=c_dtb[:, it:it + 1],
                scale=1.0),
                reads=[bkey, "c_dtb"], writes=[k2])
            P.op("v", lambda v, it=it, th=th, t_dt=t_dt, t_t1=t_t1, t_t2=t_t2:
                 v.scalar_tensor_tensor(
                 t_dt[:, :], t_t1[:, :], 1.0, t_t2[:, :], ALU.mult, ALU.add,
                 accum_out=s_dtsum[:, it, th:th + 1]),
                reads=[k1, k2], writes=[k1, "s_dtsum"])
            if "t_dt" in tp:
                st(tp["t_dt"][it][:, ta:tb], t_dt[:, :], [], rkeys=[k1])
            init_s = 0.0 if th == 0 else s_scar[:, it:it + 1]
            P.op("v", lambda v, init_s=init_s, t_S=t_S, t_dt=t_dt:
                 v.tensor_tensor_scan(
                 out=t_S[:, :], data0=t_dt[:, :], data1=c_zeros[:, :],
                 initial=init_s, op0=ALU.add, op1=ALU.add),
                reads=[k1, "c_zeros"] + ([f"sc{it}"] if th else []),
                writes=[sSk])
            if th == 0:
                P.op("v", lambda v, it=it, t_S=t_S: v.tensor_copy(
                    s_scar[:, it:it + 1], t_S[:, TH - 1:TH]),
                    reads=[sSk], writes=[f"sc{it}"])
            st(hb_S[it][:, ta:tb], t_S[:, :], [f"hbS{it}"], rkeys=[sSk])
            P.op("v", lambda v, t_u=t_u, uk2=uk2, t_dt=t_dt: v.tensor_mul(
                t_dtu[:, :], t_dt[:, :], t_u[:, :]),
                reads=[k1, uk2], writes=["A5s2"])
            for ng in range(2):
                nsl = slice(8 * ng, 8 * ng + 8)
                dA8f = AR4[:, 0:8 * TH]
                dA8 = dA8f.rearrange("p (n t) -> p n t", t=TH)
                h8f = AR4[:, 8 * TH:16 * TH]
                h8 = h8f.rearrange("p (n t) -> p n t", t=TH)
                for j in range(8):
                    P.op("s", lambda s, it=it, ng=ng, j=j, dA8=dA8, t_dt=t_dt:
                         s.activation(
                         dA8[:, j, :], t_dt[:, :], AF.Exp, bias=0.0,
                         scale=c_A[:, it, 8 * ng + j:8 * ng + j + 1]),
                        reads=[k1, "c_A"], writes=["dA8"])
                P.op("v", lambda v, nsl=nsl, h8=h8: v.tensor_mul(
                    h8[:, :, :],
                    t_dtu[:, :].unsqueeze(1).broadcast_to([128, 8, TH]),
                    B_rep[:, nsl, :]),
                    reads=["A5s2", "B_rep"], writes=["h8"])
                P.op("v", lambda v, dA8=dA8: v.tensor_copy(
                    s_tok2[:, 0, :], dA8[:, :, 0]),
                    reads=["dA8"], writes=["s_tok2"])
                P.op("v", lambda v, dA8=dA8: v.memset(dA8[:, :, 0], 0.0),
                    reads=[], writes=["dA8"])
                if th == 1:
                    P.op("v", lambda v, it=it, nsl=nsl: v.tensor_mul(
                        s_tok2[:, 1, :], s_tok2[:, 0, :], s_hcar[:, it, nsl]),
                        reads=["s_tok2", f"hc{it}"], writes=["s_tok2"])
                    P.op("v", lambda v, h8=h8: v.tensor_add(
                        h8[:, :, 0], h8[:, :, 0], s_tok2[:, 1, :]),
                        reads=["s_tok2", "h8"], writes=["h8"])
                P.op("v", lambda v, dA8f=dA8f, h8f=h8f: v.tensor_tensor_scan(
                    out=h8f, data0=dA8f, data1=h8f,
                    initial=0.0, op0=ALU.mult, op1=ALU.add),
                    reads=["dA8", "h8"], writes=["h8"])
                P.op("v", lambda v, it=it, nsl=nsl, h8=h8: v.tensor_copy(
                    s_hcar[:, it, nsl], h8[:, :, TH - 1]),
                    reads=["h8"], writes=[f"hc{it}"])
                P.op("v", lambda v, nsl=nsl, h8=h8: v.tensor_mul(
                    h8[:, :, :], h8[:, :, :], C_rep4[:, nsl, :]),
                    reads=["h8", "C_rep4"], writes=["h8"])
                P.op("v", lambda v, h8=h8: v.tensor_add(
                    h8[:, 0:4, :], h8[:, 0:4, :], h8[:, 4:8, :]),
                    reads=["h8"], writes=["h8"])
                P.op("v", lambda v, h8=h8: v.tensor_add(
                    h8[:, 0:2, :], h8[:, 0:2, :], h8[:, 2:4, :]),
                    reads=["h8"], writes=["h8"])
                if ng == 0:
                    P.op("v", lambda v, t_ya=t_ya, h8=h8: v.tensor_add(
                        t_ya[:, :], h8[:, 0, :], h8[:, 1, :]),
                        reads=["h8"], writes=[yak])
                else:
                    P.op("v", lambda v, t_yb=t_yb, h8=h8: v.tensor_add(
                        t_yb[:, :], h8[:, 0, :], h8[:, 1, :]),
                        reads=["h8"], writes=[ybk])
            P.op("v", lambda v, t_ya=t_ya, t_yb=t_yb: v.tensor_add(
                t_ya[:, :], t_ya[:, :], t_yb[:, :]),
                reads=[yak, ybk], writes=[yak])
            st(hb_yloc[it][:, ta:tb], t_ya[:, :], [f"hbY{it}"], rkeys=[yak])

    if "t_yloc" in tp:
        st(tp["t_yloc"][:, :, :], hb_yloc[:, :, :], [], rkeys=[f"hbY{i}" for i in range(IT)])
    if "t_S" in tp:
        st(tp["t_S"][:, :, :], hb_S[:, :, :], [], rkeys=[f"hbS{i}" for i in range(IT)])

    P.op("v", lambda v: v.tensor_add(
        s_dtsum[:, :, 0:1], s_dtsum[:, :, 0:1], s_dtsum[:, :, 1:2]),
        reads=["s_dtsum"], writes=["s_dtsum"])
    for it in range(IT):
        P.op("s", lambda s, it=it: s.activation(
            s_pq[:, it, 0:NS], c_A[:, it, :], AF.Exp, bias=0.0,
            scale=s_dtsum[:, it, 0:1]),
            reads=["c_A", "s_dtsum"], writes=["s_pq"])
    P.op("v", lambda v: v.tensor_copy(s_pq[:, :, NS:2 * NS], s_hcar[:, :, :]),
         reads=[f"hc{i}" for i in range(IT)], writes=["s_pq"])
    st(cc_in.rearrange("(i p) w -> p i w", p=128), s_pq[:, :, :], ["cc_in"],
       rkeys=["s_pq"])
    P.op("g", lambda g: g.collective_compute(
        "AllGather", ALU.bypass, replica_groups=[list(range(NCORES))],
        ins=[cc_in.opt()], outs=[cc_out.opt()]),
        reads=["cc_in"], writes=["cc_out"])
    P.fence()
    for c in range(NCORES):
        ld(s_all[:, c % 2, :, :],
           cc_out.rearrange("(b i p) w -> b p i w", i=IT, p=128)[c],
           [f"all{c % 2}"], rkeys=["cc_out"])
        if c == 0:
            P.op("v", lambda v: v.tensor_copy(
                hends[:, 0, :, :], s_all[:, 0, :, NS:2 * NS]),
                reads=["all0"], writes=["hends"])
        else:
            P.op("v", lambda v, c=c: v.tensor_mul(
                hends[:, c, :, :], s_all[:, c % 2, :, 0:NS],
                hends[:, c - 1, :, :]),
                reads=[f"all{c % 2}", "hends"], writes=["hends"])
            P.op("v", lambda v, c=c: v.tensor_add(
                hends[:, c, :, :], hends[:, c, :, :],
                s_all[:, c % 2, :, NS:2 * NS]),
                reads=[f"all{c % 2}", "hends"], writes=["hends"])
    P.op("g", lambda g: g.memset(s_hin[:, :, :], 0.0), writes=["s_hin"])
    for c in range(NCORES):
        P.op("v", lambda v, c=c: v.scalar_tensor_tensor(
            s_hin[:, :, :], hends[:, c, :, :], c_csel[:, c:c + 1],
            s_hin[:, :, :], ALU.mult, ALU.add),
            reads=["hends", "c_csel", "s_hin"], writes=["s_hin"])
    if "t_hin" in tp:
        st(tp["t_hin"][:, :, :], s_hin[:, :, :], [], rkeys=["s_hin"])
    P.fence()
    def emit_crep5_th0():
        for n in range(NS):
            P.op("v", lambda v, n=n: v.stream_shuffle(
                s_stage[:, :], s_bc[:, 0:TH], [NS + n] + [0] * 31),
                reads=["s_bc"], writes=["s_stage"])
            P.op("pe", lambda pe, n=n: pe.matmul(
                p2[:, 0:TH], c_ones1[:, :], s_stage[0:1, 0:TH],
                start=True, stop=True),
                reads=["s_stage", "c_ones1"], writes=["p2"])
            P.op("v", lambda v, n=n: v.tensor_copy(C_rep5[:, n, :], p2[:, 0:TH]),
                reads=["p2"], writes=["C_rep5"])

    def emit_fix_th0():
        ta, tb = 0, TH
        f_e, f_t = A5(6), A5(7)
        for it in range(IT):
            f_S, kS = A5(it % 2), A5k(it % 2)
            f_u, ku = A5(2 + it % 2), A5k(2 + it % 2)
            f_g, kg = A5(4 + it % 2), A5k(4 + it % 2)
            ld(y_half[:, it, :], hb_yloc[it][:, ta:tb], [f"y{it}"],
               rkeys=[f"hbY{it}"])
            ld(f_S[:, :], hb_S[it][:, ta:tb], [kS], rkeys=[f"hbS{it}"])
            ld(f_u[:, :], hb_u[it][:, ta:tb], [ku], rkeys=[f"hbu{it}"])
            ld(f_g[:, :], hb_gate[it][:, ta:tb], [kg], rkeys=[f"hbg{it}"])
            f_a2, k_a2 = A5(10 + it % 2), A5k(10 + it % 2)
            for n in range(NS):
                e_sl, e_k = (A5(6), "A5s6") if n % 2 == 0 else (A5(8), "A5s8")
                t_sl, t_k = (A5(7), "A5s7") if n % 2 == 0 else (A5(9), "A5s9")
                P.op("s", lambda s, it=it, n=n, f_S=f_S, e_sl=e_sl: s.activation(
                    e_sl[:, :], f_S[:, :], AF.Exp, bias=0.0,
                    scale=c_A[:, it, n:n + 1]),
                    reads=[kS, "c_A"], writes=[e_k])
                P.op("v", lambda v, n=n, e_sl=e_sl, t_sl=t_sl: v.tensor_mul(
                    t_sl[:, :], e_sl[:, :], C_rep5[:, n, :]),
                    reads=[e_k, "C_rep5"], writes=[t_k])
                if n % 2 == 0:
                    P.op("v", lambda v, it=it, n=n, t_sl=t_sl:
                         v.scalar_tensor_tensor(
                         y_half[:, it, :], t_sl[:, :], s_hin[:, it, n:n + 1],
                         y_half[:, it, :], ALU.mult, ALU.add),
                         reads=[t_k, "s_hin", f"y{it}"], writes=[f"y{it}"])
                else:
                    src2 = c_zeros if n == 1 else f_a2
                    src2k = "c_zeros" if n == 1 else k_a2
                    P.op("v", lambda v, it=it, n=n, t_sl=t_sl, f_a2=f_a2,
                         src2=src2: v.scalar_tensor_tensor(
                         f_a2[:, :], t_sl[:, :], s_hin[:, it, n:n + 1],
                         src2[:, 0:TH], ALU.mult, ALU.add),
                         reads=[t_k, "s_hin", src2k], writes=[k_a2])
            P.op("v", lambda v, it=it, f_a2=f_a2: v.tensor_add(
                y_half[:, it, :], y_half[:, it, :], f_a2[:, :]),
                reads=[f"y{it}", k_a2], writes=[f"y{it}"])
            P.op("v", lambda v, it=it, f_u=f_u: v.scalar_tensor_tensor(
                y_half[:, it, :], f_u[:, :], c_D[:, it:it + 1],
                y_half[:, it, :], ALU.mult, ALU.add),
                reads=[ku, "c_D", f"y{it}"], writes=[f"y{it}"])
            P.op("v", lambda v, it=it, f_g=f_g: v.tensor_mul(
                y_half[:, it, :], y_half[:, it, :], f_g[:, :]),
                reads=[kg, f"y{it}"], writes=[f"y{it}"])
            if "t_y" in tp:
                st(tp["t_y"][it][:, ta:tb], y_half[:, it, :], [],
                   rkeys=[f"y{it}"])

    def emit_ysplit_th0():
        for it in range(IT):
            P.op("s", lambda s, it=it: s.activation(
                YHI[it][:, :], y_half[:, it, :], AF.Copy, bias=0.0, scale=1.0),
                reads=[f"y{it}"], writes=[f"yh{it}", "C_rep5"])
        for it in range(IT):
            P.op("v", lambda v, it=it: v.tensor_sub(
                ylo[:, it, :], y_half[:, it, :], YHI[it][:, :]),
                reads=[f"y{it}", f"yh{it}"],
                writes=["yloK", f"y{it // 2}", f"y{(it + 1) // 2}"])

    def emit_outproj(rh_hi, rh_lo, rkeys, mo_ap, mo_key):
        for k in range(KD):
            bank, bkey = OB[k % 4], OK[k % 4]
            for half in range(2):
                whi, wlo, wkh, wkl = wsplit(
                    w_outproj[k][:, half * (IT // 2):(half + 1) * (IT // 2), :],
                    IT // 2, half)
                mm3(bank, bkey, whi, wlo, wkh, wkl,
                    lambda kk, half=half: rh_hi(half * (IT // 2) + kk),
                    lambda kk, half=half: rh_lo(half * (IT // 2) + kk),
                    rkeys, IT // 2, half == 0, half == 1)
            P.op("v", lambda v, k=k, bank=bank, mo_ap=mo_ap: v.tensor_copy(
                mo_ap[:, k, :], bank[:, 0:TH]),
                reads=[bkey], writes=[mo_key])

    def emit_ln23(th, mo_ap, mo_key):
        ta, tb = th * TH, (th + 1) * TH

        def xres_load(k, ta=ta, tb=tb):
            sl, key = A5(2 + k % 2), A5k(2 + k % 2)
            ld(sl[:, :], x_in[:, k, 3 + ta:3 + tb], [key])
            return sl[:, :], key

        emit_ln(mo_ap, KD, TH, mo_ap, mo_key, mo_key,
                scr=lambda k: A5(k % 2), scrk=lambda k: A5k(k % 2),
                res_load=xres_load)
        for k in range(KD):
            st(hb_x2[k][:, ta:tb], mo_ap[:, k, :], [f"hbx2{k}"],
               rkeys=[mo_key])
            if "t_x2" in tp:
                st(tp["t_x2"][:, k, ta:tb], mo_ap[:, k, :], [],
                   rkeys=[mo_key])
        P.fence()

        def ln3_sink(k, sc, sck, rap, rkey):
            P.op("pe", lambda pe, k=k, sc=sc: pe.matmul(
                p2[0:NE, 0:TH], c_gatew[:, k, :], sc[:, 0:TH],
                start=(k == 0), stop=(k == KD - 1)),
                reads=["c_gatew", sck], writes=["p2"])
            P.op("s", lambda s, k=k, sc=sc: s.activation(
                bln3[:, 0, k, :], sc[:, 0:TH], AF.Copy, bias=0.0, scale=1.0),
                reads=[sck], writes=["bln3"])
            P.op("v", lambda v, k=k, sc=sc: v.tensor_sub(
                bln3[:, 1, k, :], sc[:, 0:TH], bln3[:, 0, k, :]),
                reads=[sck, "bln3"], writes=["bln3"])

        emit_ln(mo_ap, KD, TH, None, mo_key, "ln3",
                scr=lambda k: A5(k % 2), scrk=lambda k: A5k(k % 2),
                dst_sink=ln3_sink)
        P.fence()

    def emit_gating(th):
        ta, tb = th * TH, (th + 1) * TH
        P.op("v", lambda v: v.tensor_copy(s_lgf[0:NE, :], p2[0:NE, 0:TH]),
             reads=["p2"], writes=["s_stage"])
        for tc in range(TH // 128):
            P.op("pe", lambda pe, tc=tc: pe.transpose(
                p3[:, 0:NE], s_lgf[0:NE, tc * 128:(tc + 1) * 128],
                c_eye[0:NE, 0:NE]),
                reads=["s_stage", "c_eye"], writes=["p3"])
            P.op("v", lambda v, tc=tc: v.tensor_copy(
                s_tok[:, tc, :], p3[:, 0:NE]),
                reads=["p3"], writes=["s_tok"])
        for tc in range(TH // 128):
            P.op("v", lambda v, tc=tc: v.reduce_max(
                s_trow[:, tc:tc + 1], s_tok[:, tc, :], axis=AX.X, negate=True),
                reads=["s_tok"], writes=["s_trow"])
            P.op("s", lambda s, tc=tc: s.activation(
                s_tok2[:, tc, :], s_tok[:, tc, :], AF.Exp,
                bias=s_trow[:, tc:tc + 1], scale=1.0),
                reads=["s_tok", "s_trow"], writes=["s_tok2"])
            P.op("v", lambda v, tc=tc: v.reduce_sum(
                s_trow[:, tc:tc + 1], s_tok2[:, tc, :], axis=AX.X),
                reads=["s_tok2"], writes=["s_trow"])
            P.op("v", lambda v, tc=tc: v.reciprocal(
                s_trow[:, tc:tc + 1], s_trow[:, tc:tc + 1]),
                reads=["s_trow"], writes=["s_trow"])
            P.op("v", lambda v, tc=tc: v.tensor_scalar(
                s_tok[:, tc, :], s_tok2[:, tc, :], s_trow[:, tc:tc + 1], None,
                ALU.mult),
                reads=["s_tok2", "s_trow"], writes=["s_tok"])
            P.op("v", lambda v, tc=tc: v.reduce_max(
                s_trow[:, tc:tc + 1], s_tok[:, tc, :], axis=AX.X, negate=True),
                reads=["s_tok"], writes=["s_trow"])
            P.op("v", lambda v, tc=tc: v.tensor_scalar(
                s_tok2[:, tc, :], s_tok[:, tc, :], s_trow[:, tc:tc + 1], None,
                ALU.add),
                reads=["s_tok", "s_trow"], writes=["s_tok2"])
            P.op("v", lambda v, tc=tc: v.tensor_single_scalar(
                s_tok2[:, tc, :], s_tok2[:, tc, :], 0.0, ALU.is_ge),
                reads=["s_tok2"], writes=["s_tok2"])
            P.op("v", lambda v, tc=tc: v.scalar_tensor_tensor(
                s_tok2[:, tc, :], s_tok2[:, tc, :], -1e30, s_tok[:, tc, :],
                ALU.mult, ALU.add),
                reads=["s_tok2", "s_tok"], writes=["s_tok2"])
            P.op("v", lambda v, tc=tc: v.reduce_max(
                s_trow[:, tc:tc + 1], s_tok2[:, tc, :], axis=AX.X, negate=True),
                reads=["s_tok2"], writes=["s_trow"])
            P.op("v", lambda v, tc=tc: v.tensor_scalar(
                s_tok2[:, tc, :], s_tok[:, tc, :], s_trow[:, tc:tc + 1], None,
                ALU.add),
                reads=["s_tok", "s_trow"], writes=["s_tok2"])
            P.op("v", lambda v, tc=tc: v.tensor_single_scalar(
                s_tok2[:, tc, :], s_tok2[:, tc, :], 0.0, ALU.is_ge),
                reads=["s_tok2"], writes=["s_tok2"])
            P.op("v", lambda v, tc=tc: v.tensor_mul(
                s_tok[:, tc, :], s_tok[:, tc, :], s_tok2[:, tc, :]),
                reads=["s_tok", "s_tok2"], writes=["s_tok"])
            P.op("pe", lambda pe, tc=tc: pe.transpose(
                p3[0:NE, 0:128], s_tok[:, tc, :], c_eye[:, :]),
                reads=["s_tok", "c_eye"], writes=["p3"])
            P.op("v", lambda v, tc=tc: v.tensor_copy(
                s_comb[0:NE, tc * 128:(tc + 1) * 128], p3[0:NE, 0:128]),
                reads=["p3"], writes=["s_comb"])
        if "t_comb" in tp:
            st(tp["t_comb"][:, ta:tb], s_comb[0:NE, :], [], rkeys=["s_comb"])

    def gate_up(wg_pre, wu_ap, wg_next, mm, use_comb, ce=None):
        gf = A5(4)
        whi, wlo, wkh, wkl = wg_pre
        mm3(p0, "p0", whi, wlo, wkh, wkl,
            lambda k: bln3[:, 0, k, :], lambda k: bln3[:, 1, k, :],
            ["bln3"], KD, True, True)
        whi2, wlo2, wkh2, wkl2 = wsplit(wu_ap, KD, 1, cast_eng=ce)
        mm3(p1, "p1", whi2, wlo2, wkh2, wkl2,
            lambda k: bln3[:, 0, k, :], lambda k: bln3[:, 1, k, :],
            ["bln3"], KD, True, True)
        nxt = (wsplit(wg_next, KD, 0, cast_eng=ce)
               if wg_next is not None else None)
        P.op("s", lambda s: s.activation(
            ga_tmp[:, :], p0[:, 0:TH], AF.Silu, bias=0.0, scale=1.0),
            reads=["p0"], writes=["ga"])
        P.op("v", lambda v, gf=gf: v.tensor_mul(
            gf[:, :], ga_tmp[:, :], p1[:, 0:TH]),
            reads=["ga", "p1"], writes=["A5s4"])
        if use_comb:
            P.op("v", lambda v, gf=gf: v.tensor_mul(
                gf[:, :], gf[:, :], s_crep[:, :]),
                reads=["A5s4", "s_crep"], writes=["A5s4"])
        P.op("s", lambda s, mm=mm, gf=gf: s.activation(
            bg[:, 0, mm, :], gf[:, :], AF.Copy, bias=0.0, scale=1.0),
            reads=["A5s4"], writes=[f"bg{mm}h"])
        P.op("v", lambda v, mm=mm, gf=gf: v.tensor_sub(
            bg[:, 1, mm, :], gf[:, :], bg[:, 0, mm, :]),
            reads=["A5s4", f"bg{mm}h"], writes=[f"bg{mm}l"])
        return nxt

    def down(wd_fn, first, ce=None, nxt_gw=None):
        pre = wsplit(wd_fn(0), MH, 0, cast_eng=ce)
        nxt = None
        for k in range(KD):
            bank, bkey = DB[k % 4], DK[k % 4]
            whi, wlo, wkh, wkl = pre
            gkeys = [f"bg{kk}{hl}" for kk in range(MH) for hl in "hl"]
            mm3(bank, bkey, whi, wlo, wkh, wkl,
                lambda kk: bg[:, 0, kk, :], lambda kk: bg[:, 1, kk, :],
                gkeys, MH, True, True)
            if k + 1 < KD:
                pre = wsplit(wd_fn(k + 1), MH, (k + 1) % 2, cast_eng=ce)
            if k == KD - 1 and nxt_gw is not None:
                nxt = wsplit(nxt_gw, KD, 0, cast_eng=ce)
            if first:
                P.op("v", lambda v, k=k, bank=bank: v.tensor_copy(
                    racc[:, k, :], bank[:, 0:TH]),
                    reads=[bkey], writes=[f"racc{k}"])
            else:
                P.op("v", lambda v, k=k, bank=bank: v.tensor_add(
                    racc[:, k, :], racc[:, k, :], bank[:, 0:TH]),
                    reads=[bkey, f"racc{k}"], writes=[f"racc{k}"])
        return nxt

    def emit_moe(th, tick, ce_gu=None, ce_dn=None):
        ta, tb = th * TH, (th + 1) * TH
        pre = wsplit(w_sg[0], KD, 0, cast_eng=ce_gu)
        for hh in range(2):
            for mm in range(MH):
                m = hh * MH + mm
                nxt = w_sg[m + 1] if mm + 1 < MH else None
                pre = gate_up(pre, w_su[m], nxt, mm, use_comb=False, ce=ce_gu)
                tick(1)
            nseg = w_sg[MH] if hh == 0 else w_mg[0][0]
            pre = down(lambda k, hh=hh: w_sd[k][:, hh * MH:(hh + 1) * MH, :],
                       first=(hh == 0), ce=ce_dn, nxt_gw=nseg)
            tick(4)
        def eshuf(e):
            P.op("v", lambda v, e=e: v.stream_shuffle(
                s_stage2[:, 4 + e % 4, :], s_comb[:, :], [e] + [0] * 31),
                reads=["s_comb"], writes=[f"st2e{e % 4}"])

        for e in range(4):
            eshuf(e)
        for e in range(NE):
            P.op("pe", lambda pe, e=e: pe.matmul(
                p4[:, 0:TH], c_ones1[:, :], s_stage2[0:1, 4 + e % 4, :],
                start=True, stop=True),
                reads=[f"st2e{e % 4}", "c_ones1"], writes=["p4"])
            if e + 4 < NE:
                eshuf(e + 4)
            P.op("v", lambda v: v.tensor_copy(s_crep[:, :], p4[:, 0:TH]),
                reads=["p4"], writes=["s_crep"])
            for mm in range(MH):
                nxt = w_mg[e][mm + 1] if mm + 1 < MH else None
                pre = gate_up(pre, w_mu[e][mm], nxt, mm, use_comb=True,
                              ce=ce_gu)
                tick(1)
            pre = down(lambda k, e=e: w_md[e][k], first=False, ce=ce_dn,
                       nxt_gw=w_mg[e + 1][0] if e + 1 < NE else None)
            tick(4)
        if "t_ffn" in tp:
            for k in range(KD):
                st(tp["t_ffn"][:, k, ta:tb], racc[:, k, :], [],
                   rkeys=[f"racc{k}"])

    def emit_ln4(th):
        ta, tb = th * TH, (th + 1) * TH

        def x2_load(k, ta=ta, tb=tb):
            sl, key = A5(2 + k % 2), A5k(2 + k % 2)
            ld(sl[:, :], hb_x2[k][:, ta:tb], [key], rkeys=[f"hbx2{k}"])
            return sl[:, :], key

        def out_sink(k, sc, sck, rap, rkey, ta=ta, tb=tb):
            ostg, okey = A5(6 + k % 2), A5k(6 + k % 2)
            P.op("v", lambda v, sc=sc, rap=rap, ostg=ostg: v.tensor_add(
                ostg[:, :], sc[:, 0:TH], rap),
                reads=[sck, rkey], writes=[okey])
            st(out[:, k, ta:tb], ostg[:, :], ["out_hbm"], rkeys=[okey])

        emit_ln(racc, KD, TH, None, "racc", "unused",
                scr=lambda k: A5(k % 2), scrk=lambda k: A5k(k % 2),
                res_load=x2_load, dst_sink=out_sink)

    FXC = [A5(8), A5(9), A5(10), A5(11)]
    PRT = [A5(6), A5(7)]

    def s_pref(ng, it):
        sl, key = A5(it % 2), f"fxs{it % 2}"
        P.op("s", lambda s, it=it, sl=sl: s.dma_start(
            out=sl[:, :], in_=hb_S[it][:, TH:2 * TH]),
            reads=[f"hbS{it}"], writes=[key], sdma=True)

    def emit_shuf(ng):
        for j in range(4):
            n = 4 * ng + j
            P.op("v", lambda v, n=n, j=j: v.stream_shuffle(
                s_stage2[:, j, :], s_bc[:, TH:2 * TH], [NS + n] + [0] * 31),
                reads=["s_bc"], writes=[f"st2{j}"])

    def mk_crep1(ng):
        def blk():
            if ng > 0:
                cce_add(IT - 1)
            for j in range(4):
                P.op("pe", lambda pe, j=j: pe.matmul(
                    p4[:, 0:TH], c_ones1[:, :], s_stage2[0:1, j, :],
                    start=True, stop=True),
                    reads=[f"st2{j}", "c_ones1"], writes=["p4"])
                P.op("v", lambda v, j=j: v.tensor_copy(
                    FXC[j][:, :], p4[:, 0:TH]),
                    reads=["p4"], writes=[f"fxc{j}"])
            s_pref(ng, 0)
        return blk

    def cce_add(it):
        P.op("g", lambda g, it=it: g.dma_start(
            out=hb_yloc[it][:, TH:2 * TH], in_=PRT[it % 2][:, :],
            accum_op=ALU.add),
            reads=[f"fxp{it % 2}"], writes=[f"hbY{it}"], gdma=True)

    def mk_fix1(ng, it):
        def blk():
            if it > 0:
                cce_add(it - 1)
            if it + 1 < IT:
                s_pref(ng, it + 1)
            if it == IT // 2 and ng + 1 < 4:
                emit_shuf(ng + 1)
            kss = f"fxs{it % 2}"
            sS = A5(it % 2)
            kp = f"fxp{it % 2}"
            for j in range(4):
                n = 4 * ng + j
                et, ket = (A5(5), "fxe0") if j % 2 == 0 else (A5(2), "fxe1")
                P.op("s", lambda s, n=n, sS=sS, et=et: s.activation(
                    et[:, :], sS[:, :], AF.Exp, bias=0.0,
                    scale=c_A[:, it, n:n + 1]),
                    reads=[kss, "c_A"], writes=[ket])
                P.op("v", lambda v, j=j, et=et: v.tensor_mul(
                    et[:, :], et[:, :], FXC[j][:, :]),
                    reads=[ket, f"fxc{j}"], writes=[ket])
                src2 = c_zeros if j == 0 else PRT[it % 2]
                src2k = "c_zeros" if j == 0 else kp
                P.op("v", lambda v, n=n, et=et, src2=src2, it=it:
                     v.scalar_tensor_tensor(
                     PRT[it % 2][:, :], et[:, :], s_hin[:, it, n:n + 1],
                     src2[:, 0:TH], ALU.mult, ALU.add),
                     reads=[ket, "s_hin", src2k], writes=[kp])
        return blk

    def mk_flush():
        def blk():
            cce_add(IT - 1)
        return blk

    def mk_yfin1(it):
        def blk():
            yv, kyv = A5(it % 2), f"fxy{it % 2}"
            uv, kuv = A5(2 + it % 2), f"fxu{it % 2}"
            gv, kgv = A5(6 + it % 2), f"fxg{it % 2}"
            P.op("s", lambda s, it=it, yv=yv: s.dma_start(
                out=yv[:, :], in_=hb_yloc[it][:, TH:2 * TH]),
                reads=[f"hbY{it}"], writes=[kyv, f"fxs{it % 2}"], sdma=True)
            P.op("s", lambda s, it=it, uv=uv: s.dma_start(
                out=uv[:, :], in_=hb_u[it][:, TH:2 * TH]),
                reads=[f"hbu{it}"],
                writes=[kuv, "fxe1" if it % 2 == 0 else "A5s3f"], sdma=True)
            P.op("s", lambda s, it=it, gv=gv: s.dma_start(
                out=gv[:, :], in_=hb_gate[it][:, TH:2 * TH]),
                reads=[f"hbg{it}"], writes=[kgv, f"fxp{it % 2}"], sdma=True)
            P.op("v", lambda v, it=it, uv=uv, yv=yv: v.scalar_tensor_tensor(
                yv[:, :], uv[:, :], c_D[:, it:it + 1], yv[:, :],
                ALU.mult, ALU.add),
                reads=[kuv, "c_D", kyv], writes=[kyv])
            P.op("v", lambda v, yv=yv, gv=gv: v.tensor_mul(
                yv[:, :], yv[:, :], gv[:, :]),
                reads=[kgv, kyv], writes=[kyv])
            if "t_y" in tp:
                P.op("s", lambda s, it=it, yv=yv: s.dma_start(
                    out=tp["t_y"][it][:, TH:2 * TH], in_=yv[:, :]),
                    reads=[kyv], writes=[], sdma=True)
            P.op("s", lambda s, yv=yv: s.activation(
                ybsp[:, 0, :], yv[:, :], AF.Copy, bias=0.0, scale=1.0),
                reads=[kyv], writes=["ybsph", "fxe0"])
            P.op("v", lambda v, yv=yv: v.tensor_sub(
                ybsp[:, 1, :], yv[:, :], ybsp[:, 0, :]),
                reads=[kyv, "ybsph"], writes=["ybspl"])
            P.op("s", lambda s, it=it: s.dma_start(
                out=hb_yb[it], in_=ybsp[:, :, :].rearrange("p a w -> p (a w)")),
                reads=["ybsph", "ybspl"], writes=[f"hbyb{it}"], sdma=True)
        return blk

    fx_blocks = []
    for ng in range(4):
        if ng > 0:
            fx_blocks.append(mk_crep1(ng))
        for it in range(IT):
            fx_blocks.append(mk_fix1(ng, it))
    fx_blocks.append(mk_flush())
    for it in range(IT):
        fx_blocks.append(mk_yfin1(it))

    def tick(k):
        for _ in range(k):
            if fx_blocks:
                fx_blocks.pop(0)()

    emit_crep5_th0()
    emit_fix_th0()
    emit_ysplit_th0()
    ykeys0 = [f"yh{i}" for i in range(IT)] + ["yloK"]
    emit_outproj(lambda it: YHI[it][:, :], lambda it: ylo[:, it, :], ykeys0,
                 mo, "mo")
    if "t_mo" in tp:
        for k in range(KD):
            st(tp["t_mo"][:, k, 0:TH], mo[:, k, :], [], rkeys=["mo"])
    P.fence()
    emit_ln23(0, mo, "mo")
    emit_gating(0)
    emit_shuf(0)
    mk_crep1(0)()
    emit_moe(0, tick, ce_gu="s", ce_dn="v")
    while fx_blocks:
        fx_blocks.pop(0)()
    P.fence()
    ystep = max(1, IT // 4)
    for qq in range(IT // ystep):
        ld(ybar[:, ystep * qq:ystep * (qq + 1), :],
           hb_yb[ystep * qq:ystep * (qq + 1)].rearrange("i p w -> p i w"),
           ["ybar"], rkeys=[f"hbyb{i}" for i in range(IT)])
    emit_outproj(lambda it: ybar[:, it, 0:TH], lambda it: ybar[:, it, TH:2 * TH],
                 ["ybar"], mo2, "mo2")
    if "t_mo" in tp:
        for k in range(KD):
            st(tp["t_mo"][:, k, TH:2 * TH], mo2[:, k, :], [], rkeys=["mo2"])
    emit_ln4(0)
    P.fence()

    emit_ln23(1, mo2, "mo2")
    emit_gating(1)
    emit_moe(1, lambda k: None)
    P.fence()
    emit_ln4(1)
    P.fence()

    cnt = P.prepare()
    with nc.Block() as block:
        @block.gpsimd
        def _(g):
            for e in Prog.ENGINES:
                g.sem_clear(sems[e])
            for qs in qsems:
                g.sem_clear(qs)
            for qs in gqsems:
                g.sem_clear(qs)
            for qs in sqsems:
                g.sem_clear(qs)
            g.sem_clear(sem_boot)
            g.sem_inc(sem_boot, 1)
            P.emit_engine("g", g, sems, qsems)

        @block.tensor
        def _(pe):
            pe.wait_ge(sem_boot, 1)
            P.emit_engine("pe", pe, sems, qsems)

        @block.vector
        def _(v):
            v.wait_ge(sem_boot, 1)
            P.emit_engine("v", v, sems, qsems)

        @block.scalar
        def _(s):
            s.wait_ge(sem_boot, 1)
            P.emit_engine("s", s, sems, qsems)

        @block.sync
        def _(q):
            q.wait_ge(sem_boot, 1)
            P.emit_engine("q", q, sems, qsems)

    ctx.close()
    return nc


def _prep_inputs(inputs):
    f = lambda k: np.ascontiguousarray(np.asarray(inputs[k], np.float32))
    x = f("L0_attn_x")[0]
    common = {
        "w_inproj": _tile_w(np.ascontiguousarray(f("L0_attn_in_proj_w").T)),
        "convw": _ct2(f("L0_attn_conv_w")[:, 0, :]),
        "convb": _ct1(f("L0_attn_conv_b")),
        "w_xproj": _ct2(np.ascontiguousarray(f("L0_attn_x_proj_w").T)),
        "w_dtproj": np.ascontiguousarray(
            f("L0_attn_dt_proj_w").reshape(IT, 128, 128).transpose(2, 0, 1)),
        "dtb": _ct1(f("L0_attn_dt_proj_b")),
        "alog": _ct2(f("L0_attn_A_log")),
        "dvec": _ct1(f("L0_attn_D")),
        "w_outproj": _tile_w(np.ascontiguousarray(f("L0_attn_out_proj_w").T)),
        "gatew": _ct2(f("L0_ffn_gate_w")),
        "w_mg": np.ascontiguousarray(np.stack(
            [_tile_w(f("L0_ffn_w_gate")[e]) for e in range(NE)])),
        "w_mu": np.ascontiguousarray(np.stack(
            [_tile_w(f("L0_ffn_w_up")[e]) for e in range(NE)])),
        "w_md": np.ascontiguousarray(np.stack(
            [_tile_w(f("L0_ffn_w_down")[e]) for e in range(NE)])),
        "w_sg": _tile_w(f("L0_ffn_sw_gate")),
        "w_su": _tile_w(f("L0_ffn_sw_up")),
        "w_sd": _tile_w(f("L0_ffn_sw_down")),
        "eye": np.eye(128, dtype=np.float32),
    }
    dm = KD * 128
    xpad = np.concatenate([np.zeros((3, dm), np.float32), x], axis=0)
    in_maps = []
    for c in range(NCORES):
        xs = xpad[T * c:T * c + TP]
        xs = np.ascontiguousarray(xs.reshape(TP, KD, 128).transpose(2, 1, 0))
        csel = np.zeros((128, NCORES), np.float32)
        if c > 0:
            csel[:, c - 1] = 1.0
        m = dict(common)
        m["x_in"] = xs
        m["csel"] = csel
        in_maps.append(m)
    return in_maps


def kernel(**inputs):
    in_maps = _prep_inputs(inputs)
    nc = build_program()
    res = run_bass_kernel_spmd(nc, in_maps, core_ids=list(range(NCORES)))
    full = np.empty((1, NCORES * T, KD * 128), np.float32)
    for c in range(NCORES):
        o = np.asarray(res.results[c]["out"])
        full[0, T * c:T * (c + 1), :] = o.transpose(2, 1, 0).reshape(T, KD * 128)
    return full
